# version 10
# baseline (speedup 1.0000x reference)
from contextlib import ExitStack
import numpy as np
import ml_dtypes

import concourse.bass as bass
import concourse.mybir as mybir
from concourse.bass_utils import run_bass_kernel_spmd

F32 = mybir.dt.float32
BF16 = mybir.dt.bfloat16
I32 = mybir.dt.int32
AF = mybir.ActivationFunctionType
ALU = mybir.AluOpType

D = 1024
H = 16
S = 4096
T = 2048
HID = 3072
DC = 8
HC = 24
NSLOT = 8
QW = 256
NKB = 32
EPS_L = 1e-5
EPS_R = float(np.finfo(np.float32).eps)
DEQ = 1.0 / 127.0
C2EPS = float((127.0 * 127.0) ** 2 * EPS_L)

P1_CH = 512
P4_CH = 256
P5_CH = 256


class Ctr:
    def __init__(self, nc, names):
        self.sems = {}
        self.c = {}
        self.guards = []
        for n in names:
            g = nc.semaphore(n)
            self.guards.append(g)
            self.sems[n] = g.__enter__()
            self.c[n] = 0

    def inc(self, instr, name, n=1):
        instr.then_inc(self.sems[name], n)
        self.c[name] += n
        return self.c[name]

    def wait(self, eng, name, val=None):
        v = self.c[name] if val is None else val
        if v > 0:
            eng.wait_ge(self.sems[name], v)

    def close(self):
        for g in reversed(self.guards):
            g.__exit__(None, None, None)


R_M, R_MSQ, R_MM, R_V, R_SD, R_R = (k * 512 for k in range(6))


def _ln_rows(block, ctr, rows, st_s, st_q, n, sqrt_scale, sqrt_bias):
    @block.vector
    def _(v):
        ctr.wait(v, "pe")
        v.tensor_scalar_mul(rows[0:1, R_M:R_M + n], st_s[0:1, :n], 1.0 / D)
        v.tensor_scalar_mul(rows[0:1, R_MSQ:R_MSQ + n], st_q[0:1, :n], 1.0 / D)
        v.tensor_mul(rows[0:1, R_MM:R_MM + n], rows[0:1, R_M:R_M + n],
                     rows[0:1, R_M:R_M + n])
        i = v.tensor_sub(rows[0:1, R_V:R_V + n], rows[0:1, R_MSQ:R_MSQ + n],
                         rows[0:1, R_MM:R_MM + n])
        ctr.inc(i, "dve")

    @block.scalar
    def _(a):
        ctr.wait(a, "dve")
        i = a.activation(out=rows[0:1, R_SD:R_SD + n], in_=rows[0:1, R_V:R_V + n],
                         func=AF.Sqrt, scale=sqrt_scale, bias=sqrt_bias)
        ctr.inc(i, "act")

    @block.vector
    def _(v):
        ctr.wait(v, "act")
        i = v.reciprocal(rows[0:1, R_R:R_R + n], rows[0:1, R_SD:R_SD + n])
        ctr.inc(i, "dve")
    return ctr.c["dve"]


def _bcast_mr(block, ctr, bc, ones_f, rows, n, rows_done):
    @block.tensor
    def _(t):
        ctr.wait(t, "dve", rows_done)
        t.matmul(bc[:, 0:n], ones_f[:], rows[0:1, R_M:R_M + n],
                 start=True, stop=True)
        i = t.matmul(bc[:, 256:256 + n], ones_f[:], rows[0:1, R_R:R_R + n],
                     start=True, stop=True)
        ctr.inc(i, "pe")
    return ctr.c["pe"]


def build_program(debug=False, ngroups=8):
    nc = bass.Bass("TRN2", target_bir_lowering=False)

    xT_own = nc.dram_tensor("xT_own", [D, T], F32, kind="ExternalInput").ap()
    xT_kv = nc.dram_tensor("xT_kv", [D, S], F32, kind="ExternalInput").ap()
    wq = nc.dram_tensor("wq", [D, D], F32, kind="ExternalInput").ap()
    wkv = nc.dram_tensor("wkv", [D, 2 * D], F32, kind="ExternalInput").ap()
    wgh = nc.dram_tensor("wgh", [D, H], F32, kind="ExternalInput").ap()
    wout = nc.dram_tensor("wout", [D, D], F32, kind="ExternalInput").ap()
    normw = nc.dram_tensor("normw", [128, DC], F32, kind="ExternalInput").ap()
    wgT = nc.dram_tensor("wgT", [D, HID], I32, kind="ExternalInput").ap()
    wuT = nc.dram_tensor("wuT", [D, HID], I32, kind="ExternalInput").ap()
    wdT = nc.dram_tensor("wdT", [HID, D], I32, kind="ExternalInput").ap()
    masks = nc.dram_tensor("masks", [4, 128, QW], BF16, kind="ExternalInput").ap()
    ident16 = nc.dram_tensor("ident16", [H, H], BF16, kind="ExternalInput").ap()
    outT = nc.dram_tensor("outT", [D, T], F32, kind="ExternalOutput").ap()
    x1T = nc.dram_tensor("x1T", [D, T], F32, kind="Internal").ap()
    wdTb = nc.dram_tensor("wdTb", [128, DC, HC, 128], BF16, kind="Internal").ap()
    wgb = nc.dram_tensor("wgb", [128, HC // 2, DC, 256], BF16, kind="Internal").ap()
    wub = nc.dram_tensor("wub", [128, HC // 2, DC, 256], BF16, kind="Internal").ap()
    if debug:
        dbg_xatt_own = nc.dram_tensor("dbg_xatt_own", [128, DC, T], BF16,
                                      kind="ExternalOutput").ap()
        dbg_xatt_kv = nc.dram_tensor("dbg_xatt_kv", [128, DC, S], BF16,
                                     kind="ExternalOutput").ap()
        dbg_kt = nc.dram_tensor("dbg_kt", [128, S], BF16, kind="ExternalOutput").ap()
        dbg_qt = nc.dram_tensor("dbg_qt", [128, T], BF16, kind="ExternalOutput").ap()
        dbg_vg = nc.dram_tensor("dbg_vg", [128, NKB * 130], BF16,
                                kind="ExternalOutput").ap()
        dbg_gh = nc.dram_tensor("dbg_gh", [H, T], BF16, kind="ExternalOutput").ap()
        dbg_wsl = nc.dram_tensor("dbg_wsl", [128, 2, 3, DC, 128], BF16,
                                 kind="ExternalOutput").ap()
        dbg_wsl2 = nc.dram_tensor("dbg_wsl2", [128, 2, 3, DC, 128], BF16,
                                  kind="ExternalOutput").ap()
        dbg_oall = nc.dram_tensor("dbg_oall", [128, DC, T], BF16,
                                  kind="ExternalOutput").ap()
        dbg_x1 = nc.dram_tensor("dbg_x1", [D, T], F32, kind="ExternalOutput").ap()

    ctr = Ctr(nc, ["lsp", "lgp", "pe", "act", "dve", "gp", "out",
                   "lw0", "lw1", "lx0", "lx1", "l40", "l41",
                   "l50", "l51", "ld0", "ld1", "sd0", "sd1"])

    with ExitStack() as _st0:
        ones_f = _st0.enter_context(nc.sbuf_tensor("ones_f", [1, 128], F32))
        onec_b = _st0.enter_context(nc.sbuf_tensor("onec_b", [128, 1], BF16))
        normw_sb = _st0.enter_context(nc.sbuf_tensor("normw_sb", [128, DC], F32))
        rows = _st0.enter_context(nc.sbuf_tensor("rows", [1, 6 * 512], F32))
        oall = _st0.enter_context(nc.sbuf_tensor("oall", [128, DC, T], BF16))
        masks_sb = _st0.enter_context(nc.sbuf_tensor("masks_sb", [128, 4 * QW], BF16))
        cbias = _st0.enter_context(nc.sbuf_tensor("cbias", [1, 4], F32))
        block = _st0.enter_context(nc.Block())
        @block.sync
        def _(sp):
            i = sp.dma_start(out=normw_sb[:], in_=normw[:])
            ctr.inc(i, "lsp", 16)
            i = sp.dma_start(
                out=masks_sb[:, :].rearrange("p (m q) -> p m q", m=4),
                in_=masks.rearrange("m p q -> p m q"))
            ctr.inc(i, "lsp", 16)

        @block.vector
        def _(v):
            v.memset(ones_f[:], 1.0)
            v.memset(onec_b[:], 1.0)
            v.memset(cbias[0:1, 0:1], EPS_R * EPS_L)
            v.memset(cbias[0:1, 1:2], EPS_L)
            i = v.memset(cbias[0:1, 2:3], C2EPS)
            ctr.inc(i, "dve")

        with ExitStack() as _st1:
            xatt_kv = _st1.enter_context(nc.sbuf_tensor("xatt_kv", [128, DC, S], BF16))
            xatt_own = _st1.enter_context(nc.sbuf_tensor("xatt_own", [128, DC, T], BF16))
            with ExitStack() as _st2:
                xraw = _st2.enter_context(nc.sbuf_tensor("xraw", [128, 2, DC, P1_CH], F32))
                xbf = _st2.enter_context(nc.sbuf_tensor("xbf", [128, 2, P1_CH], BF16))
                sqbf = _st2.enter_context(nc.sbuf_tensor("sqbf", [128, 2, P1_CH], BF16))
                tbuf = _st2.enter_context(nc.sbuf_tensor("tbuf", [128, P1_CH], F32))
                st_s = _st2.enter_context(nc.psum_tensor("st_s", [128, 512], F32))
                st_q = _st2.enter_context(nc.psum_tensor("st_q", [128, 512], F32))
                bc1 = _st2.enter_context(nc.psum_tensor("bc1", [128, 512], F32))
                chunks = [("kv", i) for i in range(S // P1_CH)] + [
                    ("own", i) for i in range(T // P1_CH)]
                apply_done = {}
                buf_guard = {0: 0, 1: 0}
                for ci, (kind, i0) in enumerate(chunks):
                    buf = ci % 2
                    src = xT_kv if kind == "kv" else xT_own
                    dst = xatt_kv if kind == "kv" else xatt_own
                    t0 = i0 * P1_CH

                    lxs = f"lx{buf}"

                    @block.sync
                    def _(sp):
                        if ci >= 2:
                            ctr.wait(sp, "dve", apply_done[ci - 2])
                        i = sp.dma_start(
                            out=xraw[:, buf],
                            in_=src[:, t0:t0 + P1_CH].rearrange(
                                "(dc p) t -> p dc t", p=128))
                        ctr.inc(i, lxs, 16)
                    ld_here = ctr.c[lxs]

                    for dc in range(DC):
                        b2 = dc % 2

                        @block.vector
                        def _(v):
                            ctr.wait(v, lxs, ld_here)
                            ctr.wait(v, "pe", buf_guard[b2])
                            v.tensor_copy(xbf[:, b2], xraw[:, buf, dc, :])
                            i = v.tensor_mul(sqbf[:, b2], xraw[:, buf, dc, :],
                                             xraw[:, buf, dc, :])
                            ctr.inc(i, "dve")
                        xb_done = ctr.c["dve"]

                        @block.tensor
                        def _(t):
                            ctr.wait(t, "dve", xb_done)
                            i = t.matmul(st_s[0:1, :], onec_b[:], xbf[:, b2],
                                         start=(dc == 0), stop=(dc == DC - 1))
                            i = t.matmul(st_q[0:1, :], onec_b[:], sqbf[:, b2],
                                         start=(dc == 0), stop=(dc == DC - 1))
                            ctr.inc(i, "pe")
                        buf_guard[b2] = ctr.c["pe"]

                    rows_done = _ln_rows(block, ctr, rows, st_s, st_q, P1_CH,
                                         1.0 + EPS_R, cbias[0:1, 0:1])

                    for half in range(P1_CH // 256):
                        hs = slice(half * 256, half * 256 + 256)

                        @block.tensor
                        def _(t):
                            if half == 0:
                                ctr.wait(t, "dve", rows_done)
                            else:
                                ctr.wait(t, "dve")
                            t.matmul(bc1[:, 0:256], ones_f[:],
                                     rows[0:1, R_M + half * 256:R_M + half * 256 + 256],
                                     start=True, stop=True)
                            i = t.matmul(bc1[:, 256:512], ones_f[:],
                                         rows[0:1, R_R + half * 256:R_R + half * 256 + 256],
                                         start=True, stop=True)
                            ctr.inc(i, "pe")
                        bc_now = ctr.c["pe"]

                        @block.vector
                        def _(v):
                            ctr.wait(v, "pe", bc_now)
                            for dc in range(DC):
                                v.tensor_sub(tbuf[:, 0:256],
                                             xraw[:, buf, dc, hs], bc1[:, 0:256])
                                i = v.scalar_tensor_tensor(
                                    dst[:, dc, t0 + half * 256:t0 + half * 256 + 256],
                                    tbuf[:, 0:256], normw_sb[:, dc:dc + 1],
                                    bc1[:, 256:512], ALU.mult, ALU.mult)
                            ctr.inc(i, "dve")
                        apply_done[ci] = ctr.c["dve"]

            if debug:
                @block.sync
                def _(sp):
                    ctr.wait(sp, "dve")
                    i = sp.dma_start(out=dbg_xatt_own[:], in_=xatt_own[:])
                    ctr.inc(i, "out", 16)
                    i = sp.dma_start(out=dbg_xatt_kv[:], in_=xatt_kv[:])
                    ctr.inc(i, "out", 16)

            with ExitStack() as _st3:
                g_heads = _st3.enter_context(nc.sbuf_tensor("g_heads", [H, T], BF16))
                kt = _st3.enter_context(nc.sbuf_tensor("kt", [128, S], BF16))
                vg = _st3.enter_context(nc.sbuf_tensor("vg", [128, NKB * 130], BF16))
                qt = _st3.enter_context(nc.sbuf_tensor("qt", [128, T], BF16))
                pt = _st3.enter_context(nc.sbuf_tensor("pt", [128, 4, 2, 512], BF16))
                wsl = _st3.enter_context(nc.sbuf_tensor("wsl", [128, 2, 3, DC, 128], BF16))
                wstg = _st3.enter_context(nc.sbuf_tensor("wstg", [128, DC, 128], F32))
                wgh_sb = _st3.enter_context(nc.sbuf_tensor("wgh_sb", [128, DC, H], BF16))
                wghstg = _st3.enter_context(nc.sbuf_tensor("wghstg", [128, DC, H], F32))
                obst = _st3.enter_context(nc.sbuf_tensor("obst", [64, T], BF16))
                rnsb = _st3.enter_context(nc.sbuf_tensor("rnsb", [65, 512], F32))
                ssb = _st3.enter_context(nc.sbuf_tensor("ssb", [65, 512], F32))
                otsb = _st3.enter_context(nc.sbuf_tensor("otsb", [65, 2, 512], F32))
                ones65 = _st3.enter_context(nc.sbuf_tensor("ones65", [65, 128], F32))
                oneh = _st3.enter_context(nc.sbuf_tensor("oneh", [16, 16], BF16))
                bcs_sb = _st3.enter_context(nc.sbuf_tensor("bcs_sb", [128, 512], F32))
                vgv = vg[:, :].rearrange("p (kb two c) -> p kb two c", two=2, c=65)

                @block.vector
                def _(v):
                    i = v.memset(ones65[:], 1.0)
                    ctr.inc(i, "dve")

                @block.sync
                def _(sp):
                    i = sp.dma_start(out=oneh[:], in_=ident16[:])
                    ctr.inc(i, "lsp", 16)
                oneh_ld = ctr.c["lsp"]

                proj_done = {}
                w_ready = {}

                wcvt_done = {}

                def load_group_weights(g):
                    lws = f"lw{g % 2}"
                    srcs = [wq[:, g * 128:(g + 1) * 128],
                            wkv[:, g * 128:(g + 1) * 128],
                            wkv[:, D + g * 128:D + (g + 1) * 128]]
                    for s3 in range(3):
                        @block.sync
                        def _(sp):
                            prev = wcvt_done.get((g, s3 - 1),
                                                 wcvt_done.get((g - 1, 2), 0))
                            ctr.wait(sp, "dve", prev)
                            i = sp.dma_start(
                                out=wstg[:],
                                in_=srcs[s3].rearrange("(dc p) c -> p dc c", p=128))
                            ctr.inc(i, lws, 16)
                        ld = ctr.c[lws]

                        @block.vector
                        def _(v):
                            ctr.wait(v, lws, ld)
                            if g >= 2 and s3 == 0:
                                ctr.wait(v, "pe", proj_done[g - 2])
                            i = v.tensor_copy(wsl[:, g % 2, s3], wstg[:])
                            wcvt_done[(g, s3)] = ctr.inc(i, "dve")
                    w_ready[g] = wcvt_done[(g, 2)]

                load_group_weights(0)
                if debug:
                    @block.sync
                    def _(sp):
                        ctr.wait(sp, "dve", w_ready[0])
                        i = sp.dma_start(out=dbg_wsl2[:], in_=wsl[:])
                        ctr.inc(i, "out", 16)

                @block.sync
                def _(sp):
                    i = sp.dma_start(
                        out=wghstg[:],
                        in_=wgh.rearrange("(dc p) h -> p dc h", p=128))
                    ctr.inc(i, "lgp", 16)
                wgh_raw = ctr.c["lgp"]

                @block.vector
                def _(v):
                    ctr.wait(v, "lgp", wgh_raw)
                    i = v.tensor_copy(wgh_sb[:], wghstg[:])
                    ctr.inc(i, "dve")
                wgh_ld = ctr.c["dve"]

                obst_dma = 0
                epi_done = 0
                otsb_guard = {0: 0, 1: 0}
                rn_done_by = {}
                prev_exp = 0
                exp_by_stbuf = {0: 0, 1: 0}
                pv_by_pbuf = {0: 0, 1: 0, 2: 0, 3: 0}
                pbuf_next = 0
                for g in range(ngroups):
                    wb = g % 2

                    with ExitStack() as _st4:
                        prj0 = _st4.enter_context(nc.psum_tensor(f"prj0_{g}", [128, 512], F32))
                        prj1 = _st4.enter_context(nc.psum_tensor(f"prj1_{g}", [128, 512], F32))
                        gat = _st4.enter_context(nc.psum_tensor(f"gat_{g}", [128, 512], F32))
                        prj = [prj0, prj1]
                        prj_guard = {0: 0, 1: 0}

                        def proj_block(ops_fn, n_pieces, evac_fn, first_fn=None):
                            for piece in range(n_pieces):
                                pb = piece % 2

                                @block.tensor
                                def _(t):
                                    if piece == 0:
                                        ctr.wait(t, "dve", w_ready[g])
                                        if first_fn is not None:
                                            first_fn(t)
                                    ctr.wait(t, "dve", prj_guard[pb])
                                    for dc in range(DC):
                                        out_ap, lhsT, rhs = ops_fn(piece, pb, dc)
                                        i = t.matmul(out_ap, lhsT, rhs,
                                                     start=(dc == 0),
                                                     stop=(dc == DC - 1))
                                    ctr.inc(i, "pe")
                                mm_done = ctr.c["pe"]

                                @block.vector
                                def _(v):
                                    ctr.wait(v, "pe", mm_done)
                                    i = evac_fn(v, piece, pb)
                                    prj_guard[pb] = ctr.inc(i, "dve")

                        def k_first(t):
                            ctr.wait(t, "dve")

                        def k_ops(piece, pb, dc):
                            return (prj[pb][:, 0:512],
                                    wsl[:, wb, 1, dc, :],
                                    xatt_kv[:, dc, piece * 512:(piece + 1) * 512])

                        def k_evac(v, piece, pb):
                            return v.tensor_copy(
                                kt[:, piece * 512:(piece + 1) * 512], prj[pb][:, 0:512])

                        proj_block(k_ops, S // 512, k_evac, first_fn=k_first)

                        @block.vector
                        def _(v):
                            i = v.memset(vgv[:, :, :, 64:65], 1.0)
                            ctr.inc(i, "dve")

                        def v_ops(kb, pb, dc):
                            return (prj[pb][:, 0:128],
                                    xatt_kv[:, dc, kb * 128:(kb + 1) * 128],
                                    wsl[:, wb, 2, dc, :])

                        def v_evac(v, kb, pb):
                            return v.tensor_copy(
                                vgv[:, kb, :, 0:64],
                                prj[pb][:, 0:128].rearrange("p (two c) -> p two c", two=2))

                        proj_block(v_ops, NKB, v_evac)

                        def q_ops(piece, pb, dc):
                            return (prj[pb][:, 0:512],
                                    wsl[:, wb, 0, dc, :],
                                    xatt_own[:, dc, piece * 512:(piece + 1) * 512])

                        def q_evac(v, piece, pb):
                            return v.tensor_copy(
                                qt[:, piece * 512:(piece + 1) * 512], prj[pb][:, 0:512])

                        proj_block(q_ops, T // 512, q_evac)

                        if g == 0:
                            sig_done = 0
                            for piece in range(T // 512):
                                @block.tensor
                                def _(t):
                                    ctr.wait(t, "dve", wgh_ld)
                                    ctr.wait(t, "act", sig_done)
                                    for dc in range(DC):
                                        i = t.matmul(gat[0:16, 0:512],
                                                     wgh_sb[:, dc, :],
                                                     xatt_own[:, dc, piece * 512:(piece + 1) * 512],
                                                     start=(dc == 0), stop=(dc == DC - 1))
                                    ctr.inc(i, "pe")
                                mm_done = ctr.c["pe"]

                                @block.scalar
                                def _(a):
                                    ctr.wait(a, "pe", mm_done)
                                    i = a.activation(
                                        out=g_heads[:, piece * 512:(piece + 1) * 512],
                                        in_=gat[0:16, 0:512], func=AF.Sigmoid)
                                    ctr.inc(i, "act")
                                sig_done = ctr.c["act"]

                        proj_done[g] = ctr.c["pe"]
                        if g < 7 and not (debug and g + 1 >= ngroups):
                            load_group_weights(g + 1)
                        all_proj = ctr.c["dve"]

                    with ExitStack() as _st5:
                        st2a0 = _st5.enter_context(nc.psum_tensor(f"st2a0_{g}", [128, 512], F32))
                        st2a1 = _st5.enter_context(nc.psum_tensor(f"st2a1_{g}", [128, 512], F32))
                        st2b0 = _st5.enter_context(nc.psum_tensor(f"st2b0_{g}", [128, 512], F32))
                        st2b1 = _st5.enter_context(nc.psum_tensor(f"st2b1_{g}", [128, 512], F32))
                        stAl = [st2a0, st2a1]
                        stBl = [st2b0, st2b1]
                        ot0 = _st5.enter_context(nc.psum_tensor(f"ot0_{g}", [128, 512], F32))
                        ot1 = _st5.enter_context(nc.psum_tensor(f"ot1_{g}", [128, 512], F32))
                        bcs = _st5.enter_context(nc.psum_tensor(f"bcs_{g}", [128, 512], F32))
                        for j in range(NSLOT):
                            C = 4 * (j + 1)
                            ng2 = C // 2
                            if j > 0:
                                @block.vector
                                def _(v):
                                    i = v.reciprocal(rnsb[64:65, 0:512],
                                                     otsb[64:65, (j - 1) % 2, 0:512])
                                    ctr.inc(i, "dve")
                                rn_done_by[j - 1] = ctr.c["dve"]
                            P0 = ctr.c["pe"]
                            A0 = ctr.c["act"]
                            M0 = ctr.c["dve"]
                            pb0 = pbuf_next
                            pbuf_next = (pb0 + ng2) % 4
                            entry_pv = dict(pv_by_pbuf)
                            entry_exp = dict(exp_by_stbuf)

                            def p_st(gi):
                                return P0 + 2 * gi + 1

                            def p_pv(gi):
                                return P0 + 2 * gi + 2

                            def a_exp2(gi):
                                return A0 + 2 * gi + 2

                            @block.tensor
                            def _(t):
                                for gi in range(ng2):
                                    sb2 = gi % 2
                                    if j == 0 and gi == 0:
                                        ctr.wait(t, "dve", all_proj)
                                        ctr.wait(t, "act", A0)
                                    if gi >= 2:
                                        ctr.wait(t, "act", a_exp2(gi - 2))
                                    elif entry_exp[sb2]:
                                        ctr.wait(t, "act", entry_exp[sb2])
                                    stA = stAl[sb2]
                                    stB = stBl[sb2]
                                    for kb2 in range(2):
                                        kb = gi * 2 + kb2
                                        sl = slice(kb2 * 256, kb2 * 256 + 256)
                                        t.matmul(stA[:, sl],
                                                 kt[0:64, kb * 128:(kb + 1) * 128],
                                                 qt[0:64, j * QW:(j + 1) * QW],
                                                 start=True, stop=True,
                                                 tile_position=(0, 0))
                                        i = t.matmul(stB[:, sl],
                                                     kt[64:128, kb * 128:(kb + 1) * 128],
                                                     qt[64:128, j * QW:(j + 1) * QW],
                                                     start=True, stop=True,
                                                     tile_position=(64, 0))
                                    ctr.inc(i, "pe")
                                    pbuf = (pb0 + gi) % 4
                                    if gi >= ng2 - 2:
                                        ctr.wait(t, "dve", M0 + (gi - (ng2 - 2)) + 1)
                                    else:
                                        ctr.wait(t, "act", a_exp2(gi))
                                    if gi == 0:
                                        ctr.wait(t, "dve", epi_done)
                                    for kb2 in range(2):
                                        kb = gi * 2 + kb2
                                        sl = slice(kb2 * 256, kb2 * 256 + 256)
                                        t.matmul(ot0[0:65, 0:256],
                                                 vg[:, kb * 130:kb * 130 + 65],
                                                 pt[:, pbuf, 0][:, sl],
                                                 start=(kb == 0), stop=(kb == C - 1))
                                        i = t.matmul(ot1[0:65, 0:256],
                                                     vg[:, kb * 130 + 65:kb * 130 + 130],
                                                     pt[:, pbuf, 1][:, sl],
                                                     start=(kb == 0), stop=(kb == C - 1))
                                    ctr.inc(i, "pe")

                            @block.scalar
                            def _(a):
                                for gi in range(ng2):
                                    sb2 = gi % 2
                                    pbuf = (pb0 + gi) % 4
                                    a.wait_ge(ctr.sems["pe"], p_st(gi))
                                    ptfree = p_pv(gi - 4) if gi >= 4 else entry_pv[pbuf]
                                    if ptfree:
                                        a.wait_ge(ctr.sems["pe"], ptfree)
                                    stA = stAl[sb2]
                                    stB = stBl[sb2]
                                    i = a.activation(out=pt[:, pbuf, 0], in_=stA[:],
                                                     func=AF.Exp, scale=0.125)
                                    ctr.inc(i, "act")
                                    i = a.activation(out=pt[:, pbuf, 1], in_=stB[:],
                                                     func=AF.Exp, scale=0.125)
                                    ctr.inc(i, "act")

                            @block.vector
                            def _(v):
                                for mi in range(2):
                                    gi = ng2 - 2 + mi
                                    pbuf = (pb0 + gi) % 4
                                    v.wait_ge(ctr.sems["act"], a_exp2(gi))
                                    if j == 0:
                                        ctr.wait(v, "lsp", 32)
                                    msl = slice(mi * 512, mi * 512 + 512)
                                    v.tensor_mul(pt[:, pbuf, 0], pt[:, pbuf, 0],
                                                 masks_sb[:, msl])
                                    i = v.tensor_mul(pt[:, pbuf, 1], pt[:, pbuf, 1],
                                                     masks_sb[:, msl])
                                    ctr.inc(i, "dve")

                            for gi in range(ng2):
                                pv_by_pbuf[(pb0 + gi) % 4] = p_pv(gi)
                            exp_by_stbuf[0] = a_exp2(ng2 - 2)
                            exp_by_stbuf[1] = a_exp2(ng2 - 1)
                            assert ctr.c["pe"] == P0 + 2 * ng2
                            assert ctr.c["act"] == A0 + 2 * ng2
                            assert ctr.c["dve"] == M0 + 2
                            pv_last = ctr.c["pe"]

                            ob = j % 2

                            @block.vector
                            def _(v):
                                ctr.wait(v, "pe", pv_last)
                                ctr.wait(v, "dve", otsb_guard[ob])
                                v.tensor_copy(otsb[0:65, ob, 0:256], ot0[0:65, 0:256])
                                i = v.tensor_copy(otsb[0:65, ob, 256:512],
                                                  ot1[0:65, 0:256])
                                ctr.inc(i, "dve")
                            epi_done = ctr.c["dve"]

                            def emit_tail_rest(jj):
                                obb = jj % 2

                                @block.tensor
                                def _(t):
                                    ctr.wait(t, "dve", rn_done_by[jj])
                                    ctr.wait(t, "lsp", oneh_ld)
                                    t.matmul(bcs[64:65, 0:256],
                                             oneh[:, 2 * g:2 * g + 1],
                                             g_heads[:, jj * QW:(jj + 1) * QW],
                                             start=True, stop=True,
                                             tile_position=(0, 64))
                                    i = t.matmul(bcs[64:65, 256:512],
                                                 oneh[:, 2 * g + 1:2 * g + 2],
                                                 g_heads[:, jj * QW:(jj + 1) * QW],
                                                 start=True, stop=True,
                                                 tile_position=(0, 64))
                                    ctr.inc(i, "pe")
                                grow_done = ctr.c["pe"]

                                @block.vector
                                def _(v):
                                    ctr.wait(v, "pe", grow_done)
                                    i = v.tensor_mul(ssb[64:65, :], bcs[64:65, :],
                                                     rnsb[64:65, :])
                                    ctr.inc(i, "dve")
                                s_done = ctr.c["dve"]

                                @block.tensor
                                def _(t):
                                    ctr.wait(t, "dve", s_done)
                                    t.matmul(bcs[:, 0:256], ones65[64:65, :],
                                             ssb[64:65, 0:256], start=True, stop=True)
                                    i = t.matmul(bcs[:, 256:512], ones65[64:65, :],
                                                 ssb[64:65, 256:512],
                                                 start=True, stop=True)
                                    ctr.inc(i, "pe")
                                bcs_done = ctr.c["pe"]

                                @block.vector
                                def _(v):
                                    ctr.wait(v, "pe", bcs_done)
                                    if jj == 0:
                                        ctr.wait(v, "lsp", obst_dma)
                                    v.tensor_mul(oall[0:64, g, jj * QW:(jj + 1) * QW],
                                                 otsb[0:64, obb, 0:256],
                                                 bcs[0:64, 0:256])
                                    i = v.tensor_mul(obst[0:64, jj * QW:(jj + 1) * QW],
                                                     otsb[0:64, obb, 256:512],
                                                     bcs[0:64, 256:512])
                                    ctr.inc(i, "dve")
                                otsb_guard[obb] = ctr.c["dve"]

                            if j > 0:
                                emit_tail_rest(j - 1)
                            if j == NSLOT - 1:
                                @block.vector
                                def _(v):
                                    i = v.reciprocal(rnsb[64:65, 0:512],
                                                     otsb[64:65, ob, 0:512])
                                    ctr.inc(i, "dve")
                                rn_done_by[j] = ctr.c["dve"]
                                emit_tail_rest(j)
                            epi_apply = ctr.c["dve"]

                        @block.sync
                        def _(sp):
                            ctr.wait(sp, "dve", epi_apply)
                            i = sp.dma_start(out=oall[64:128, g, :], in_=obst[:])
                            ctr.inc(i, "lsp", 16)
                        obst_dma = ctr.c["lsp"]

                        if debug and g == 0:
                            @block.sync
                            def _(sp):
                                i = sp.dma_start(out=dbg_wsl[:], in_=wsl[:])
                                ctr.inc(i, "out", 16)
                                i = sp.dma_start(out=dbg_kt[:], in_=kt[:])
                                ctr.inc(i, "out", 16)
                                i = sp.dma_start(out=dbg_qt[:], in_=qt[:])
                                ctr.inc(i, "out", 16)
                                i = sp.dma_start(out=dbg_vg[:], in_=vg[:])
                                ctr.inc(i, "out", 16)
                                i = sp.dma_start(out=dbg_gh[:], in_=g_heads[:])
                                ctr.inc(i, "out", 16)

        with ExitStack() as _st6:
            pass
            if debug:
                @block.sync
                def _(sp):
                    ctr.wait(sp, "dve")
                    ctr.wait(sp, "lsp")
                    i = sp.dma_start(out=dbg_oall[:], in_=oall[:])
                    ctr.inc(i, "out", 16)

            with ExitStack() as _st7:
                wout_bf = _st7.enter_context(nc.sbuf_tensor("wout_bf", [128, DC, D], BF16))
                wistg = _st7.enter_context(nc.sbuf_tensor("wistg", [128, 2, HC, 128], I32))
                woutstg = _st7.enter_context(nc.sbuf_tensor("woutstg", [128, DC, D], F32))
                wdstg_b = _st7.enter_context(nc.sbuf_tensor("wdstg_b", [128, 2, HC, 128], BF16))
                x0b = _st7.enter_context(nc.sbuf_tensor("x0b", [128, 2, DC, P4_CH], F32))
                yf = _st7.enter_context(nc.sbuf_tensor("yf", [128, 2, DC, P4_CH], F32))
                ybf = _st7.enter_context(nc.sbuf_tensor("ybf", [128, 2, P4_CH], BF16))
                sqb = _st7.enter_context(nc.sbuf_tensor("sqb", [128, 2, P4_CH], BF16))
                x1b = _st7.enter_context(nc.sbuf_tensor("x1b", [128, DC, P4_CH], F32))
                wo0 = _st7.enter_context(nc.psum_tensor("wo0", [128, 256], F32))
                wo1 = _st7.enter_context(nc.psum_tensor("wo1", [128, 256], F32))
                st_s4 = _st7.enter_context(nc.psum_tensor("st_s4", [128, 512], F32))
                st_q4 = _st7.enter_context(nc.psum_tensor("st_q4", [128, 512], F32))
                bc4 = _st7.enter_context(nc.psum_tensor("bc4", [128, 512], F32))
                wo = [wo0, wo1]

                @block.sync
                def _(sp):
                    ctr.wait(sp, "dve")
                    ctr.wait(sp, "pe")
                    i = sp.dma_start(
                        out=woutstg[:],
                        in_=wout.rearrange("(dc p) c -> p dc c", p=128))
                    ctr.inc(i, "lgp", 16)
                wout_raw = ctr.c["lgp"]

                @block.vector
                def _(v):
                    ctr.wait(v, "lgp", wout_raw)
                    i = v.tensor_copy(wout_bf[:], woutstg[:])
                    ctr.inc(i, "dve")
                wout_ld = ctr.c["dve"]

                prep_ld = {}
                prep_cv = {}
                prep_st = {}

                def emit_w_prep(k):
                    par = k % 2
                    lds = f"ld{par}"
                    sds = f"sd{par}"
                    if k < 16:
                        w_src = wgT if k < 8 else wuT
                        w_dst = wgb if k < 8 else wub
                        dcx = k % 8
                        src_ap = w_src[dcx * 128:(dcx + 1) * 128, :]
                        dst_ap = w_dst[:, :, dcx, :]
                        stg_i = wistg[:, par, :, :].rearrange("p hc c -> p (hc c)")
                        stg_b = wdstg_b[:, par, :, :].rearrange(
                            "p hc c -> p (hcp c)", hcp=HC // 2) if False else                             wdstg_b[:, par, :, :].rearrange("p hc c -> p (hc c)").rearrange(
                                "p (hcp c) -> p hcp c", c=256)
                    else:
                        dcx = k - 16
                        src_ap = wdT[:, dcx * 128:(dcx + 1) * 128].rearrange(
                            "(hc p) c -> p hc c", p=128)
                        dst_ap = wdTb[:, dcx]
                        stg_i = wistg[:, par]
                        stg_b = wdstg_b[:, par]

                    @block.sync
                    def _(sp):
                        if k >= 2:
                            ctr.wait(sp, "act", prep_cv[k - 2])
                        i = sp.dma_start(out=stg_i, in_=src_ap)
                        prep_ld[k] = ctr.inc(i, lds, 16)

                    @block.scalar
                    def _(a):
                        ctr.wait(a, lds, prep_ld[k])
                        if k >= 2:
                            ctr.wait(a, sds, prep_st[k - 2])
                        i = a.activation(out=stg_b, in_=stg_i, func=AF.Copy)
                        prep_cv[k] = ctr.inc(i, "act")

                    @block.sync
                    def _(sp):
                        ctr.wait(sp, "act", prep_cv[k])
                        i = sp.dma_start(out=dst_ap, in_=stg_b)
                        prep_st[k] = ctr.inc(i, sds, 16)


                st4l = [st_s4, st_q4]
                evac_entry = {0: 0, 1: 0}
                stat_entry = {0: 0, 1: 0}
                yf_guard = {0: 0, 1: 0}
                x0_guard = {0: 0, 1: 0}
                x1_dma = [0]
                nch4 = T // P4_CH

                def emit_p4_tail(cc, p_apply_deps):
                    st4 = st4l[cc % 2]
                    rows_done = _ln_rows(block, ctr, rows, st4, st4[:, 256:512],
                                         P4_CH, 1.0, cbias[0:1, 1:2])
                    bc_done = _bcast_mr(block, ctr, bc4, ones_f, rows, P4_CH,
                                        rows_done)
                    x0_ld_cc, l4s_cc = p_apply_deps
                    t0c = cc * P4_CH

                    @block.vector
                    def _(v):
                        ctr.wait(v, "pe", bc_done)
                        ctr.wait(v, l4s_cc, x0_ld_cc)
                        ctr.wait(v, "lsp", x1_dma[0])
                        for dc in range(DC):
                            v.tensor_sub(yf[:, cc % 2, dc, :], yf[:, cc % 2, dc, :],
                                         bc4[:, 0:P4_CH])
                            v.tensor_mul(yf[:, cc % 2, dc, :], yf[:, cc % 2, dc, :],
                                         bc4[:, 256:256 + P4_CH])
                            i = v.tensor_add(x1b[:, dc, :], yf[:, cc % 2, dc, :],
                                             x0b[:, cc % 2, dc, :])
                        ctr.inc(i, "dve")
                    x1_done = ctr.c["dve"]
                    yf_guard[cc % 2] = x1_done
                    x0_guard[cc % 2] = x1_done

                    @block.sync
                    def _(sp):
                        ctr.wait(sp, "dve", x1_done)
                        i = sp.dma_start(
                            out=x1T[:, t0c:t0c + P4_CH].rearrange(
                                "(dc p) t -> p dc t", p=128),
                            in_=x1b[:])
                        ctr.inc(i, "lsp", 16)
                        if debug:
                            i = sp.dma_start(
                                out=dbg_x1[:, t0c:t0c + P4_CH].rearrange(
                                    "(dc p) t -> p dc t", p=128),
                                in_=x1b[:])
                            ctr.inc(i, "out", 16)
                    x1_dma[0] = ctr.c["lsp"]

                prev_deps = None
                for ch in range(nch4):
                    for k3 in range(3):
                        emit_w_prep(3 * ch + k3)
                    t0 = ch * P4_CH
                    xb = ch % 2
                    l4s = f"l4{xb}"

                    @block.sync
                    def _(sp):
                        if ch >= 2:
                            ctr.wait(sp, "dve", x0_guard[xb])
                        i = sp.dma_start(
                            out=x0b[:, xb],
                            in_=xT_own[:, t0:t0 + P4_CH].rearrange(
                                "(dc p) t -> p dc t", p=128))
                        ctr.inc(i, l4s, 16)
                    x0_ld = ctr.c[l4s]

                    Pw0 = ctr.c["pe"]
                    Aw0 = ctr.c["act"]
                    Dw0 = ctr.c["dve"]
                    st4 = st4l[xb]

                    @block.tensor
                    def _(t):
                        for dc in range(DC):
                            pb = dc % 2
                            if ch == 0 and dc == 0:
                                ctr.wait(t, "dve", wout_ld)
                                ctr.wait(t, "lsp")
                                ctr.wait(t, "dve")
                            if dc >= 2:
                                t.wait_ge(ctr.sems["act"], Aw0 + dc - 1)
                            elif evac_entry[pb]:
                                t.wait_ge(ctr.sems["act"], evac_entry[pb])
                            for ic in range(DC):
                                i = t.matmul(wo[pb][:, 0:P4_CH],
                                             wout_bf[:, ic, dc * 128:(dc + 1) * 128],
                                             oall[:, ic, t0:t0 + P4_CH],
                                             start=(ic == 0), stop=(ic == DC - 1))
                            ctr.inc(i, "pe")
                        for dc in range(DC):
                            pb = dc % 2
                            t.wait_ge(ctr.sems["dve"], Dw0 + dc + 1)
                            i = t.matmul(st4[0:1, 0:P4_CH], onec_b[:], ybf[:, pb],
                                         start=(dc == 0), stop=False)
                            i = t.matmul(st4[0:1, 256:256 + P4_CH], onec_b[:],
                                         sqb[:, pb],
                                         start=False, stop=(dc == DC - 1))
                            ctr.inc(i, "pe")

                    @block.scalar
                    def _(a):
                        for dc in range(DC):
                            pb = dc % 2
                            a.wait_ge(ctr.sems["pe"], Pw0 + dc + 1)
                            if dc == 0 and yf_guard[xb]:
                                a.wait_ge(ctr.sems["dve"], yf_guard[xb])
                            i = a.copy(yf[:, xb, dc, :], wo[pb][:, 0:P4_CH])
                            ctr.inc(i, "act")

                    @block.vector
                    def _(v):
                        for dc in range(DC):
                            pb = dc % 2
                            v.wait_ge(ctr.sems["act"], Aw0 + dc + 1)
                            if dc >= 2:
                                v.wait_ge(ctr.sems["pe"], Pw0 + 8 + dc - 1)
                            elif stat_entry[pb]:
                                v.wait_ge(ctr.sems["pe"], stat_entry[pb])
                            v.tensor_copy(ybf[:, pb], yf[:, xb, dc, :])
                            i = v.tensor_mul(sqb[:, pb], yf[:, xb, dc, :],
                                             yf[:, xb, dc, :])
                            ctr.inc(i, "dve")

                    evac_entry[0] = Aw0 + 7
                    evac_entry[1] = Aw0 + 8
                    stat_entry[0] = Pw0 + 8 + 7
                    stat_entry[1] = Pw0 + 8 + 8
                    assert ctr.c["pe"] == Pw0 + 16
                    assert ctr.c["act"] == Aw0 + 8
                    assert ctr.c["dve"] == Dw0 + 8

                    if prev_deps is not None:
                        emit_p4_tail(ch - 1, prev_deps)
                    prev_deps = (x0_ld, l4s)
                emit_p4_tail(nch4 - 1, prev_deps)
                x1_all = ctr.c["lsp"]

            with ExitStack() as _st8:
                wd_bf = _st8.enter_context(nc.sbuf_tensor("wd_bf", [128, 2, HC, 128], BF16))
                wgu = _st8.enter_context(nc.sbuf_tensor("wgu", [128, 2, 2, DC, 256], BF16))
                x1f = _st8.enter_context(nc.sbuf_tensor("x1f", [128, 2, DC, P5_CH], F32))
                x2n = _st8.enter_context(nc.sbuf_tensor("x2n", [128, DC, P5_CH], BF16))
                hbuf = _st8.enter_context(nc.sbuf_tensor("hbuf", [128, HC, P5_CH], BF16))
                silu_b = _st8.enter_context(nc.sbuf_tensor("silu_b", [128, 2, P5_CH], F32))
                yf5 = _st8.enter_context(nc.sbuf_tensor("yf5", [128, DC, P5_CH], F32))
                ybf5 = _st8.enter_context(nc.sbuf_tensor("ybf5", [128, 2, P5_CH], BF16))
                sqb5 = _st8.enter_context(nc.sbuf_tensor("sqb5", [128, 2, P5_CH], BF16))
                outb = _st8.enter_context(nc.sbuf_tensor("outb", [128, DC, P5_CH], F32))
                gu0a = _st8.enter_context(nc.psum_tensor("gu0a", [128, 512], F32))
                gu1a = _st8.enter_context(nc.psum_tensor("gu1a", [128, 512], F32))
                gu0b = _st8.enter_context(nc.psum_tensor("gu0b", [128, 512], F32))
                gu1b = _st8.enter_context(nc.psum_tensor("gu1b", [128, 512], F32))
                gu0l = [gu0a, gu0b]
                gu1l = [gu1a, gu1b]
                dn0 = _st8.enter_context(nc.psum_tensor("dn0", [128, 256], F32))
                dn1 = _st8.enter_context(nc.psum_tensor("dn1", [128, 256], F32))
                st5 = _st8.enter_context(nc.psum_tensor("st5", [128, 512], F32))
                bc5 = _st8.enter_context(nc.psum_tensor("bc5", [128, 512], F32))
                dn = [dn0, dn1]
                nch = T // P5_CH
                wprep_s0 = 12 * 16
                wprep_s1 = 12 * 16
                wgu_guard = {0: 0, 1: 0}
                gu_entry = {0: 0, 1: 0}
                wgu_ld = {}
                stat_guard = {0: 0, 1: 0}
                ev5_guard = {0: 0, 1: 0}
                wd_guard = {0: 0, 1: 0}
                x1f_guard = {0: 0, 1: 0}
                out_dma = 0
                for ch in range(nch):
                    t0 = ch * P5_CH
                    xb = ch % 2

                    l5s = f"l5{xb}"

                    @block.sync
                    def _(sp):
                        if ch == 0:
                            ctr.wait(sp, "lsp", x1_all)
                        ctr.wait(sp, "dve", x1f_guard[xb])
                        i = sp.dma_start(
                            out=x1f[:, xb],
                            in_=x1T[:, t0:t0 + P5_CH].rearrange(
                                "(dc p) t -> p dc t", p=128))
                        ctr.inc(i, l5s, 16)
                    x1_ld = ctr.c[l5s]

                    for dc in range(DC):
                        b2 = dc % 2

                        @block.vector
                        def _(v):
                            ctr.wait(v, l5s, x1_ld)
                            ctr.wait(v, "pe", stat_guard[b2])
                            v.tensor_copy(ybf5[:, b2], x1f[:, xb, dc, :])
                            i = v.tensor_mul(sqb5[:, b2], x1f[:, xb, dc, :],
                                             x1f[:, xb, dc, :])
                            ctr.inc(i, "dve")
                        xb_done = ctr.c["dve"]

                        @block.tensor
                        def _(t):
                            ctr.wait(t, "dve", xb_done)
                            t.matmul(st5[0:1, 0:P5_CH], onec_b[:], ybf5[:, b2],
                                     start=(dc == 0), stop=False)
                            i = t.matmul(st5[0:1, 256:256 + P5_CH], onec_b[:],
                                         sqb5[:, b2],
                                         start=False, stop=(dc == DC - 1))
                            ctr.inc(i, "pe")
                        stat_guard[b2] = ctr.c["pe"]

                    rows_done = _ln_rows(block, ctr, rows, st5, st5[:, 256:512], P5_CH,
                                         1.0, cbias[0:1, 1:2])
                    bc_done = _bcast_mr(block, ctr, bc5, ones_f, rows, P5_CH, rows_done)

                    @block.vector
                    def _(v):
                        ctr.wait(v, "pe", bc_done)
                        for dc in range(DC):
                            v.tensor_sub(yf5[:, dc, :], x1f[:, xb, dc, :],
                                         bc5[:, 0:P5_CH])
                            i = v.tensor_mul(x2n[:, dc, :], yf5[:, dc, :],
                                             bc5[:, 256:256 + P5_CH])
                        ctr.inc(i, "dve")
                    x2n_done = ctr.c["dve"]

                    Pg0 = ctr.c["pe"]
                    Ag0 = ctr.c["act"]
                    Dg0 = ctr.c["dve"]

                    def g_gu(hc):
                        return Pg0 + hc + 1

                    def g_silu(hc):
                        return Ag0 + hc + 1

                    def g_h(hc):
                        return Dg0 + hc + 1

                    for hcp in range(HC // 2):
                        wpar = hcp % 2
                        lws5 = f"lw{wpar}"

                        @block.sync
                        def _(sp):
                            if ch == 0 and hcp < 2:
                                ctr.wait(sp, "sd0", wprep_s0)
                                ctr.wait(sp, "sd1", wprep_s1)
                            ctr.wait(sp, "pe", wgu_guard[wpar])
                            i = sp.dma_start(out=wgu[:, wpar, 0],
                                             in_=wgb[:, hcp])
                            ctr.inc(i, lws5, 16)
                            i = sp.dma_start(out=wgu[:, wpar, 1],
                                             in_=wub[:, hcp])
                            ctr.inc(i, lws5, 16)
                        wgu_ld[(ch, hcp)] = ctr.c[lws5]

                        @block.tensor
                        def _(t):
                            for hc in (2 * hcp, 2 * hcp + 1):
                                gb = hc % 2
                                cc = (hc % 2) * 128
                                if hc == 2 * hcp:
                                    ctr.wait(t, lws5, wgu_ld[(ch, hcp)])
                                if hc == 0:
                                    ctr.wait(t, "dve", x2n_done)
                                if hc >= 2:
                                    ctr.wait(t, "dve", g_h(hc - 2))
                                elif gu_entry[gb]:
                                    ctr.wait(t, "dve", gu_entry[gb])
                                for ic in range(DC):
                                    t.matmul(gu0l[gb][:, 0:P5_CH],
                                             wgu[:, wpar, 0, ic, cc:cc + 128],
                                             x2n[:, ic, :],
                                             start=(ic == 0), stop=(ic == DC - 1))
                                for ic in range(DC):
                                    i = t.matmul(gu1l[gb][:, 0:P5_CH],
                                                 wgu[:, wpar, 1, ic, cc:cc + 128],
                                                 x2n[:, ic, :],
                                                 start=(ic == 0), stop=(ic == DC - 1))
                                ctr.inc(i, "pe")
                        wgu_guard[wpar] = ctr.c["pe"]

                    @block.scalar
                    def _(a):
                        for hc in range(HC):
                            gb = hc % 2
                            a.wait_ge(ctr.sems["pe"], g_gu(hc))
                            if hc >= 2:
                                a.wait_ge(ctr.sems["dve"], g_h(hc - 2))
                            i = a.activation(out=silu_b[:, gb],
                                             in_=gu0l[gb][:, 0:P5_CH],
                                             func=AF.Silu, scale=DEQ)
                            ctr.inc(i, "act")

                    @block.vector
                    def _(v):
                        for hc in range(HC):
                            gb = hc % 2
                            v.wait_ge(ctr.sems["act"], g_silu(hc))
                            i = v.tensor_mul(hbuf[:, hc, :], silu_b[:, gb],
                                             gu1l[gb][:, 0:P5_CH])
                            ctr.inc(i, "dve")

                    gu_entry[0] = g_h(HC - 2)
                    gu_entry[1] = g_h(HC - 1)
                    assert ctr.c["pe"] == Pg0 + HC
                    assert ctr.c["act"] == Ag0 + HC
                    assert ctr.c["dve"] == Dg0 + HC
                    h_all = ctr.c["dve"]

                    for dc in range(DC):
                        pb = dc % 2

                        lds = f"ld{pb}"

                        @block.sync
                        def _(sp):
                            if ch == 0 and dc < 2:
                                ctr.wait(sp, "sd0", wprep_s0)
                                ctr.wait(sp, "sd1", wprep_s1)
                            ctr.wait(sp, "pe", wd_guard[pb])
                            i = sp.dma_start(out=wd_bf[:, pb], in_=wdTb[:, dc])
                            ctr.inc(i, lds, 16)
                        wd_ld = ctr.c[lds]

                        @block.tensor
                        def _(t):
                            ctr.wait(t, lds, wd_ld)
                            ctr.wait(t, "dve", h_all)
                            ctr.wait(t, "act", ev5_guard[pb])
                            for hc in range(HC):
                                i = t.matmul(dn[pb][:, 0:P5_CH],
                                             wd_bf[:, pb, hc, :],
                                             hbuf[:, hc, :],
                                             start=(hc == 0), stop=(hc == HC - 1))
                            ctr.inc(i, "pe")
                        wd_guard[pb] = ctr.c["pe"]
                        mm_done = ctr.c["pe"]

                        @block.scalar
                        def _(a):
                            ctr.wait(a, "pe", mm_done)
                            i = a.copy(yf5[:, dc, :], dn[pb][:, 0:P5_CH])
                            ctr.inc(i, "act")
                        ev5_guard[pb] = ctr.c["act"]

                        @block.vector
                        def _(v):
                            ctr.wait(v, "act", ev5_guard[pb])
                            v.tensor_copy(ybf5[:, pb], yf5[:, dc, :])
                            i = v.tensor_mul(sqb5[:, pb], yf5[:, dc, :],
                                             yf5[:, dc, :])
                            ctr.inc(i, "dve")
                        yb_done = ctr.c["dve"]

                        @block.tensor
                        def _(t):
                            ctr.wait(t, "dve", yb_done)
                            t.matmul(st5[0:1, 0:P5_CH], onec_b[:], ybf5[:, pb],
                                     start=(dc == 0), stop=False)
                            i = t.matmul(st5[0:1, 256:256 + P5_CH], onec_b[:],
                                         sqb5[:, pb],
                                         start=False, stop=(dc == DC - 1))
                            ctr.inc(i, "pe")

                    rows2 = _ln_rows(block, ctr, rows, st5, st5[:, 256:512], P5_CH,
                                     1.0, cbias[0:1, 2:3])
                    bc2 = _bcast_mr(block, ctr, bc5, ones_f, rows, P5_CH, rows2)

                    @block.vector
                    def _(v):
                        ctr.wait(v, "pe", bc2)
                        ctr.wait(v, "out", out_dma)
                        for dc in range(DC):
                            v.tensor_sub(yf5[:, dc, :], yf5[:, dc, :], bc5[:, 0:P5_CH])
                            v.tensor_mul(yf5[:, dc, :], yf5[:, dc, :],
                                         bc5[:, 256:256 + P5_CH])
                            i = v.tensor_add(outb[:, dc, :], yf5[:, dc, :],
                                             x1f[:, xb, dc, :])
                        ctr.inc(i, "dve")
                    out_done = ctr.c["dve"]
                    x1f_guard[xb] = out_done

                    @block.sync
                    def _(sp):
                        ctr.wait(sp, "dve", out_done)
                        i = sp.dma_start(
                            out=outT[:, t0:t0 + P5_CH].rearrange(
                                "(dc p) t -> p dc t", p=128),
                            in_=outb[:])
                        ctr.inc(i, "out", 16)
                    out_dma = ctr.c["out"]

                @block.sync
                def _(sp):
                    ctr.wait(sp, "out")

    ctr.close()
    return nc


def _host_prep(L0_attn_tokens, L0_attn_norm_w, L0_attn_Wq, L0_attn_Wkv,
               L0_attn_Wout, L0_attn_Wg, L0_ffn_w_gate, L0_ffn_w_up,
               L0_ffn_w_down):
    bf = ml_dtypes.bfloat16
    shared = {
        "wq": np.ascontiguousarray(L0_attn_Wq, dtype=np.float32),
        "wkv": np.ascontiguousarray(L0_attn_Wkv, dtype=np.float32),
        "wgh": np.ascontiguousarray(L0_attn_Wg, dtype=np.float32),
        "wout": np.ascontiguousarray(L0_attn_Wout, dtype=np.float32),
        "normw": np.ascontiguousarray(
            np.asarray(L0_attn_norm_w, dtype=np.float32).reshape(DC, 128).T),
        "wgT": np.ascontiguousarray(np.asarray(L0_ffn_w_gate, dtype=np.int32).T),
        "wuT": np.ascontiguousarray(np.asarray(L0_ffn_w_up, dtype=np.int32).T),
        "wdT": np.ascontiguousarray(np.asarray(L0_ffn_w_down, dtype=np.int32).T),
    }
    k_i = np.arange(128)[:, None]
    q_i = np.arange(QW)[None, :]
    t1 = (q_i >= k_i).astype(bf)
    t2 = (q_i >= 128 + k_i).astype(bf)
    ones = np.ones((128, QW), dtype=bf)
    zeros = np.zeros((128, QW), dtype=bf)
    shared["ident16"] = np.eye(H, dtype=bf)
    masks_by_par = {
        0: np.ascontiguousarray(np.stack([t1, t2, zeros, zeros])),
        1: np.ascontiguousarray(np.stack([ones, ones, t1, t2])),
    }
    toks = np.asarray(L0_attn_tokens, dtype=np.float32)
    own_cols_by_par = {
        par: np.concatenate([np.arange((2 * j + par) * QW, (2 * j + par + 1) * QW)
                             for j in range(NSLOT)])
        for par in range(2)
    }
    in_maps = []
    for c in range(8):
        seq, par = c // 2, c % 2
        xt = np.ascontiguousarray(toks[seq].T)
        m = dict(shared)
        m["xT_kv"] = xt
        m["xT_own"] = np.ascontiguousarray(xt[:, own_cols_by_par[par]])
        m["masks"] = masks_by_par[par]
        in_maps.append(m)
    return in_maps, own_cols_by_par


def kernel(**inputs):
    in_maps, own_cols_by_par = _host_prep(**inputs)
    nc = build_program()
    res = run_bass_kernel_spmd(nc, in_maps, core_ids=list(range(8)))
    out = np.empty((4, S, D), dtype=np.float32)
    for c in range(8):
        seq, par = c // 2, c % 2
        out[seq, own_cols_by_par[par], :] = res.results[c]["outT"].T
    return out


# version 11
# speedup vs baseline: 1.2027x; 1.2027x over previous
from contextlib import ExitStack
import numpy as np
import ml_dtypes

import concourse.bass as bass
import concourse.mybir as mybir
from concourse.bass_utils import run_bass_kernel_spmd

F32 = mybir.dt.float32
BF16 = mybir.dt.bfloat16
I32 = mybir.dt.int32
AF = mybir.ActivationFunctionType
ALU = mybir.AluOpType

D = 1024
H = 16
S = 4096
T = 2048
HID = 3072
DC = 8
HC = 24
NSLOT = 8
QW = 256
NKB = 32
EPS_L = 1e-5
EPS_R = float(np.finfo(np.float32).eps)
DEQ = 1.0 / 127.0
C2EPS = float((127.0 * 127.0) ** 2 * EPS_L)

P1_CH = 512
P4_CH = 256
P5_CH = 256


class Ctr:
    def __init__(self, nc, names):
        self.sems = {}
        self.c = {}
        self.guards = []
        for n in names:
            g = nc.semaphore(n)
            self.guards.append(g)
            self.sems[n] = g.__enter__()
            self.c[n] = 0

    def inc(self, instr, name, n=1):
        instr.then_inc(self.sems[name], n)
        self.c[name] += n
        return self.c[name]

    def wait(self, eng, name, val=None):
        v = self.c[name] if val is None else val
        if v > 0:
            eng.wait_ge(self.sems[name], v)

    def close(self):
        for g in reversed(self.guards):
            g.__exit__(None, None, None)


R_M, R_MSQ, R_MM, R_V, R_SD, R_R = (k * 512 for k in range(6))


def _ln_rows(block, ctr, rows, st_s, st_q, n, sqrt_scale, sqrt_bias):
    @block.vector
    def _(v):
        ctr.wait(v, "pe")
        v.tensor_scalar_mul(rows[0:1, R_M:R_M + n], st_s[0:1, :n], 1.0 / D)
        v.tensor_scalar_mul(rows[0:1, R_MSQ:R_MSQ + n], st_q[0:1, :n], 1.0 / D)
        v.tensor_mul(rows[0:1, R_MM:R_MM + n], rows[0:1, R_M:R_M + n],
                     rows[0:1, R_M:R_M + n])
        i = v.tensor_sub(rows[0:1, R_V:R_V + n], rows[0:1, R_MSQ:R_MSQ + n],
                         rows[0:1, R_MM:R_MM + n])
        ctr.inc(i, "dve")

    @block.scalar
    def _(a):
        ctr.wait(a, "dve")
        i = a.activation(out=rows[0:1, R_SD:R_SD + n], in_=rows[0:1, R_V:R_V + n],
                         func=AF.Sqrt, scale=sqrt_scale, bias=sqrt_bias)
        ctr.inc(i, "act")

    @block.vector
    def _(v):
        ctr.wait(v, "act")
        i = v.reciprocal(rows[0:1, R_R:R_R + n], rows[0:1, R_SD:R_SD + n])
        ctr.inc(i, "dve")
    return ctr.c["dve"]


def _bcast_mr(block, ctr, bc, ones_f, rows, n, rows_done):
    @block.tensor
    def _(t):
        ctr.wait(t, "dve", rows_done)
        t.matmul(bc[:, 0:n], ones_f[:], rows[0:1, R_M:R_M + n],
                 start=True, stop=True)
        i = t.matmul(bc[:, 256:256 + n], ones_f[:], rows[0:1, R_R:R_R + n],
                     start=True, stop=True)
        ctr.inc(i, "pe")
    return ctr.c["pe"]


def build_program(debug=False, ngroups=8):
    nc = bass.Bass("TRN2", target_bir_lowering=False)

    xT_own = nc.dram_tensor("xT_own", [D, T], F32, kind="ExternalInput").ap()
    xT_kv = nc.dram_tensor("xT_kv", [D, S], F32, kind="ExternalInput").ap()
    wq = nc.dram_tensor("wq", [D, D], F32, kind="ExternalInput").ap()
    wkv = nc.dram_tensor("wkv", [D, 2 * D], F32, kind="ExternalInput").ap()
    wgh = nc.dram_tensor("wgh", [D, H], F32, kind="ExternalInput").ap()
    wout = nc.dram_tensor("wout", [D, D], F32, kind="ExternalInput").ap()
    normw = nc.dram_tensor("normw", [128, DC], F32, kind="ExternalInput").ap()
    wgT = nc.dram_tensor("wgT", [D, HID], I32, kind="ExternalInput").ap()
    wuT = nc.dram_tensor("wuT", [D, HID], I32, kind="ExternalInput").ap()
    wdT = nc.dram_tensor("wdT", [HID, D], I32, kind="ExternalInput").ap()
    masks = nc.dram_tensor("masks", [4, 128, QW], BF16, kind="ExternalInput").ap()
    ident16 = nc.dram_tensor("ident16", [H, H], BF16, kind="ExternalInput").ap()
    outT = nc.dram_tensor("outT", [D, T], F32, kind="ExternalOutput").ap()
    x1T = nc.dram_tensor("x1T", [D, T], F32, kind="Internal").ap()
    wdTb = nc.dram_tensor("wdTb", [128, DC, HC, 128], BF16, kind="Internal").ap()
    wgb = nc.dram_tensor("wgb", [128, HC // 2, DC, 256], BF16, kind="Internal").ap()
    wub = nc.dram_tensor("wub", [128, HC // 2, DC, 256], BF16, kind="Internal").ap()
    if debug:
        dbg_xatt_own = nc.dram_tensor("dbg_xatt_own", [128, DC, T], BF16,
                                      kind="ExternalOutput").ap()
        dbg_xatt_kv = nc.dram_tensor("dbg_xatt_kv", [128, DC, S], BF16,
                                     kind="ExternalOutput").ap()
        dbg_kt = nc.dram_tensor("dbg_kt", [128, S], BF16, kind="ExternalOutput").ap()
        dbg_qt = nc.dram_tensor("dbg_qt", [128, T], BF16, kind="ExternalOutput").ap()
        dbg_vg = nc.dram_tensor("dbg_vg", [128, NKB * 130], BF16,
                                kind="ExternalOutput").ap()
        dbg_gh = nc.dram_tensor("dbg_gh", [H, T], BF16, kind="ExternalOutput").ap()
        dbg_wsl = nc.dram_tensor("dbg_wsl", [128, 2, 3, DC, 128], BF16,
                                 kind="ExternalOutput").ap()
        dbg_wsl2 = nc.dram_tensor("dbg_wsl2", [128, 2, 3, DC, 128], BF16,
                                  kind="ExternalOutput").ap()
        dbg_oall = nc.dram_tensor("dbg_oall", [128, DC, T], BF16,
                                  kind="ExternalOutput").ap()
        dbg_x1 = nc.dram_tensor("dbg_x1", [D, T], F32, kind="ExternalOutput").ap()

    ctr = Ctr(nc, ["lsp", "lgp", "pe", "act", "dve", "gp", "out",
                   "lw0", "lw1", "lx0", "lx1", "l40", "l41",
                   "l50", "l51", "ld0", "ld1", "sd0", "sd1"])

    with ExitStack() as _st0:
        ones_f = _st0.enter_context(nc.sbuf_tensor("ones_f", [1, 128], F32))
        onec_b = _st0.enter_context(nc.sbuf_tensor("onec_b", [128, 1], BF16))
        normw_sb = _st0.enter_context(nc.sbuf_tensor("normw_sb", [128, DC], F32))
        rows = _st0.enter_context(nc.sbuf_tensor("rows", [1, 6 * 512], F32))
        oall = _st0.enter_context(nc.sbuf_tensor("oall", [128, DC, T], BF16))
        masks_sb = _st0.enter_context(nc.sbuf_tensor("masks_sb", [128, 4 * QW], BF16))
        cbias = _st0.enter_context(nc.sbuf_tensor("cbias", [1, 4], F32))
        block = _st0.enter_context(nc.Block())
        @block.sync
        def _(sp):
            i = sp.dma_start(out=normw_sb[:], in_=normw[:])
            ctr.inc(i, "lsp", 16)
            i = sp.dma_start(
                out=masks_sb[:, :].rearrange("p (m q) -> p m q", m=4),
                in_=masks.rearrange("m p q -> p m q"))
            ctr.inc(i, "lsp", 16)

        @block.vector
        def _(v):
            v.memset(ones_f[:], 1.0)
            v.memset(onec_b[:], 1.0)
            v.memset(cbias[0:1, 0:1], EPS_R * EPS_L)
            v.memset(cbias[0:1, 1:2], EPS_L)
            i = v.memset(cbias[0:1, 2:3], C2EPS)
            ctr.inc(i, "dve")

        with ExitStack() as _st1:
            xatt_kv = _st1.enter_context(nc.sbuf_tensor("xatt_kv", [128, DC, S], BF16))
            xatt_own = _st1.enter_context(nc.sbuf_tensor("xatt_own", [128, DC, T], BF16))
            with ExitStack() as _st2:
                xraw = _st2.enter_context(nc.sbuf_tensor("xraw", [128, 2, DC, P1_CH], F32))
                xbf = _st2.enter_context(nc.sbuf_tensor("xbf", [128, 2, P1_CH], BF16))
                sqbf = _st2.enter_context(nc.sbuf_tensor("sqbf", [128, 2, P1_CH], BF16))
                tbuf = _st2.enter_context(nc.sbuf_tensor("tbuf", [128, P1_CH], F32))
                st_s = _st2.enter_context(nc.psum_tensor("st_s", [128, 512], F32))
                st_q = _st2.enter_context(nc.psum_tensor("st_q", [128, 512], F32))
                bc1 = _st2.enter_context(nc.psum_tensor("bc1", [128, 512], F32))
                chunks = [("kv", i) for i in range(S // P1_CH)] + [
                    ("own", i) for i in range(T // P1_CH)]
                apply_done = {}
                buf_guard = {0: 0, 1: 0}
                for ci, (kind, i0) in enumerate(chunks):
                    buf = ci % 2
                    src = xT_kv if kind == "kv" else xT_own
                    dst = xatt_kv if kind == "kv" else xatt_own
                    t0 = i0 * P1_CH

                    lxs = f"lx{buf}"

                    @block.sync
                    def _(sp):
                        if ci >= 2:
                            ctr.wait(sp, "dve", apply_done[ci - 2])
                        i = sp.dma_start(
                            out=xraw[:, buf],
                            in_=src[:, t0:t0 + P1_CH].rearrange(
                                "(dc p) t -> p dc t", p=128))
                        ctr.inc(i, lxs, 16)
                    ld_here = ctr.c[lxs]

                    for dc in range(DC):
                        b2 = dc % 2

                        @block.vector
                        def _(v):
                            ctr.wait(v, lxs, ld_here)
                            ctr.wait(v, "pe", buf_guard[b2])
                            v.tensor_copy(xbf[:, b2], xraw[:, buf, dc, :])
                            i = v.tensor_mul(sqbf[:, b2], xraw[:, buf, dc, :],
                                             xraw[:, buf, dc, :])
                            ctr.inc(i, "dve")
                        xb_done = ctr.c["dve"]

                        @block.tensor
                        def _(t):
                            ctr.wait(t, "dve", xb_done)
                            i = t.matmul(st_s[0:1, :], onec_b[:], xbf[:, b2],
                                         start=(dc == 0), stop=(dc == DC - 1))
                            i = t.matmul(st_q[0:1, :], onec_b[:], sqbf[:, b2],
                                         start=(dc == 0), stop=(dc == DC - 1))
                            ctr.inc(i, "pe")
                        buf_guard[b2] = ctr.c["pe"]

                    rows_done = _ln_rows(block, ctr, rows, st_s, st_q, P1_CH,
                                         1.0 + EPS_R, cbias[0:1, 0:1])

                    for half in range(P1_CH // 256):
                        hs = slice(half * 256, half * 256 + 256)

                        @block.tensor
                        def _(t):
                            if half == 0:
                                ctr.wait(t, "dve", rows_done)
                            else:
                                ctr.wait(t, "dve")
                            t.matmul(bc1[:, 0:256], ones_f[:],
                                     rows[0:1, R_M + half * 256:R_M + half * 256 + 256],
                                     start=True, stop=True)
                            i = t.matmul(bc1[:, 256:512], ones_f[:],
                                         rows[0:1, R_R + half * 256:R_R + half * 256 + 256],
                                         start=True, stop=True)
                            ctr.inc(i, "pe")
                        bc_now = ctr.c["pe"]

                        @block.vector
                        def _(v):
                            ctr.wait(v, "pe", bc_now)
                            for dc in range(DC):
                                v.tensor_sub(tbuf[:, 0:256],
                                             xraw[:, buf, dc, hs], bc1[:, 0:256])
                                i = v.scalar_tensor_tensor(
                                    dst[:, dc, t0 + half * 256:t0 + half * 256 + 256],
                                    tbuf[:, 0:256], normw_sb[:, dc:dc + 1],
                                    bc1[:, 256:512], ALU.mult, ALU.mult)
                            ctr.inc(i, "dve")
                        apply_done[ci] = ctr.c["dve"]

            if debug:
                @block.sync
                def _(sp):
                    ctr.wait(sp, "dve")
                    i = sp.dma_start(out=dbg_xatt_own[:], in_=xatt_own[:])
                    ctr.inc(i, "out", 16)
                    i = sp.dma_start(out=dbg_xatt_kv[:], in_=xatt_kv[:])
                    ctr.inc(i, "out", 16)

            with ExitStack() as _st3:
                g_heads = _st3.enter_context(nc.sbuf_tensor("g_heads", [H, T], BF16))
                kt = _st3.enter_context(nc.sbuf_tensor("kt", [128, S], BF16))
                vg = _st3.enter_context(nc.sbuf_tensor("vg", [128, NKB * 130], BF16))
                qt = _st3.enter_context(nc.sbuf_tensor("qt", [128, T], BF16))
                pt = _st3.enter_context(nc.sbuf_tensor("pt", [128, 4, 2, 512], BF16))
                wsl = _st3.enter_context(nc.sbuf_tensor("wsl", [128, 2, 3, DC, 128], BF16))
                wstg = _st3.enter_context(nc.sbuf_tensor("wstg", [128, DC, 128], F32))
                wgh_sb = _st3.enter_context(nc.sbuf_tensor("wgh_sb", [128, DC, H], BF16))
                wghstg = _st3.enter_context(nc.sbuf_tensor("wghstg", [128, DC, H], F32))
                obst = _st3.enter_context(nc.sbuf_tensor("obst", [64, T], BF16))
                rnsb = _st3.enter_context(nc.sbuf_tensor("rnsb", [65, 512], F32))
                ssb = _st3.enter_context(nc.sbuf_tensor("ssb", [65, 512], F32))
                otsb = _st3.enter_context(nc.sbuf_tensor("otsb", [65, 2, 512], F32))
                ones65 = _st3.enter_context(nc.sbuf_tensor("ones65", [65, 128], F32))
                oneh = _st3.enter_context(nc.sbuf_tensor("oneh", [16, 16], BF16))
                bcs_sb = _st3.enter_context(nc.sbuf_tensor("bcs_sb", [128, 512], F32))
                vgv = vg[:, :].rearrange("p (kb two c) -> p kb two c", two=2, c=65)

                @block.vector
                def _(v):
                    i = v.memset(ones65[:], 1.0)
                    ctr.inc(i, "dve")

                @block.sync
                def _(sp):
                    i = sp.dma_start(out=oneh[:], in_=ident16[:])
                    ctr.inc(i, "lsp", 16)
                oneh_ld = ctr.c["lsp"]

                proj_done = {}
                w_ready = {}

                wcvt_done = {}

                def load_group_weights(g):
                    lws = f"lw{g % 2}"
                    srcs = [wq[:, g * 128:(g + 1) * 128],
                            wkv[:, g * 128:(g + 1) * 128],
                            wkv[:, D + g * 128:D + (g + 1) * 128]]
                    for s3 in range(3):
                        @block.sync
                        def _(sp):
                            prev = wcvt_done.get((g, s3 - 1),
                                                 wcvt_done.get((g - 1, 2), 0))
                            ctr.wait(sp, "dve", prev)
                            i = sp.dma_start(
                                out=wstg[:],
                                in_=srcs[s3].rearrange("(dc p) c -> p dc c", p=128))
                            ctr.inc(i, lws, 16)
                        ld = ctr.c[lws]

                        @block.vector
                        def _(v):
                            ctr.wait(v, lws, ld)
                            if g >= 2 and s3 == 0:
                                ctr.wait(v, "pe", proj_done[g - 2])
                            i = v.tensor_copy(wsl[:, g % 2, s3], wstg[:])
                            wcvt_done[(g, s3)] = ctr.inc(i, "dve")
                    w_ready[g] = wcvt_done[(g, 2)]

                load_group_weights(0)
                if debug:
                    @block.sync
                    def _(sp):
                        ctr.wait(sp, "dve", w_ready[0])
                        i = sp.dma_start(out=dbg_wsl2[:], in_=wsl[:])
                        ctr.inc(i, "out", 16)

                @block.sync
                def _(sp):
                    i = sp.dma_start(
                        out=wghstg[:],
                        in_=wgh.rearrange("(dc p) h -> p dc h", p=128))
                    ctr.inc(i, "lgp", 16)
                wgh_raw = ctr.c["lgp"]

                @block.vector
                def _(v):
                    ctr.wait(v, "lgp", wgh_raw)
                    i = v.tensor_copy(wgh_sb[:], wghstg[:])
                    ctr.inc(i, "dve")
                wgh_ld = ctr.c["dve"]

                obst_dma = 0
                epi_done = 0
                otsb_guard = {0: 0, 1: 0}
                rn_done_by = {}
                prev_exp = 0
                exp_by_stbuf = {0: 0, 1: 0}
                pv_by_pbuf = {0: 0, 1: 0, 2: 0, 3: 0}
                pbuf_next = 0
                for g in range(ngroups):
                    wb = g % 2

                    with ExitStack() as _st4:
                        prj0 = _st4.enter_context(nc.psum_tensor(f"prj0_{g}", [128, 512], F32))
                        prj1 = _st4.enter_context(nc.psum_tensor(f"prj1_{g}", [128, 512], F32))
                        gat = _st4.enter_context(nc.psum_tensor(f"gat_{g}", [128, 512], F32))
                        prj = [prj0, prj1]
                        prj_guard = {0: 0, 1: 0}

                        def proj_block(ops_fn, n_pieces, evac_fn, first_fn=None):
                            for piece in range(n_pieces):
                                pb = piece % 2

                                @block.tensor
                                def _(t):
                                    if piece == 0:
                                        ctr.wait(t, "dve", w_ready[g])
                                        if first_fn is not None:
                                            first_fn(t)
                                    ctr.wait(t, "dve", prj_guard[pb])
                                    for dc in range(DC):
                                        out_ap, lhsT, rhs = ops_fn(piece, pb, dc)
                                        i = t.matmul(out_ap, lhsT, rhs,
                                                     start=(dc == 0),
                                                     stop=(dc == DC - 1))
                                    ctr.inc(i, "pe")
                                mm_done = ctr.c["pe"]

                                @block.vector
                                def _(v):
                                    ctr.wait(v, "pe", mm_done)
                                    i = evac_fn(v, piece, pb)
                                    prj_guard[pb] = ctr.inc(i, "dve")

                        def k_first(t):
                            ctr.wait(t, "dve")

                        def k_ops(piece, pb, dc):
                            return (prj[pb][:, 0:512],
                                    wsl[:, wb, 1, dc, :],
                                    xatt_kv[:, dc, piece * 512:(piece + 1) * 512])

                        def k_evac(v, piece, pb):
                            return v.tensor_copy(
                                kt[:, piece * 512:(piece + 1) * 512], prj[pb][:, 0:512])

                        proj_block(k_ops, S // 512, k_evac, first_fn=k_first)

                        @block.vector
                        def _(v):
                            i = v.memset(vgv[:, :, :, 64:65], 1.0)
                            ctr.inc(i, "dve")

                        def v_ops(kb, pb, dc):
                            return (prj[pb][:, 0:128],
                                    xatt_kv[:, dc, kb * 128:(kb + 1) * 128],
                                    wsl[:, wb, 2, dc, :])

                        def v_evac(v, kb, pb):
                            return v.tensor_copy(
                                vgv[:, kb, :, 0:64],
                                prj[pb][:, 0:128].rearrange("p (two c) -> p two c", two=2))

                        proj_block(v_ops, NKB, v_evac)

                        def q_ops(piece, pb, dc):
                            return (prj[pb][:, 0:512],
                                    wsl[:, wb, 0, dc, :],
                                    xatt_own[:, dc, piece * 512:(piece + 1) * 512])

                        def q_evac(v, piece, pb):
                            return v.tensor_copy(
                                qt[:, piece * 512:(piece + 1) * 512], prj[pb][:, 0:512])

                        proj_block(q_ops, T // 512, q_evac)

                        if g == 0:
                            sig_done = 0
                            for piece in range(T // 512):
                                @block.tensor
                                def _(t):
                                    ctr.wait(t, "dve", wgh_ld)
                                    ctr.wait(t, "act", sig_done)
                                    for dc in range(DC):
                                        i = t.matmul(gat[0:16, 0:512],
                                                     wgh_sb[:, dc, :],
                                                     xatt_own[:, dc, piece * 512:(piece + 1) * 512],
                                                     start=(dc == 0), stop=(dc == DC - 1))
                                    ctr.inc(i, "pe")
                                mm_done = ctr.c["pe"]

                                @block.scalar
                                def _(a):
                                    ctr.wait(a, "pe", mm_done)
                                    i = a.activation(
                                        out=g_heads[:, piece * 512:(piece + 1) * 512],
                                        in_=gat[0:16, 0:512], func=AF.Sigmoid)
                                    ctr.inc(i, "act")
                                sig_done = ctr.c["act"]

                        proj_done[g] = ctr.c["pe"]
                        if g < 7 and not (debug and g + 1 >= ngroups):
                            load_group_weights(g + 1)
                        all_proj = ctr.c["dve"]

                    with ExitStack() as _st5:
                        stf0 = _st5.enter_context(nc.psum_tensor(f"stf0_{g}", [128, 1024], F32))
                        stf1 = _st5.enter_context(nc.psum_tensor(f"stf1_{g}", [128, 1024], F32))
                        stfl = [stf0, stf1]
                        ot0 = _st5.enter_context(nc.psum_tensor(f"ot0_{g}", [128, 512], F32))
                        ot1 = _st5.enter_context(nc.psum_tensor(f"ot1_{g}", [128, 512], F32))
                        bcs = _st5.enter_context(nc.psum_tensor(f"bcs_{g}", [128, 512], F32))
                        for j in range(NSLOT):
                            C = 4 * (j + 1)
                            ng2 = C // 2
                            if j > 0:
                                @block.vector
                                def _(v):
                                    i = v.reciprocal(rnsb[64:65, 0:512],
                                                     otsb[64:65, (j - 1) % 2, 0:512])
                                    ctr.inc(i, "dve")
                                rn_done_by[j - 1] = ctr.c["dve"]
                            P0 = ctr.c["pe"]
                            A0 = ctr.c["act"]
                            M0 = ctr.c["dve"]
                            pb0 = pbuf_next
                            pbuf_next = (pb0 + ng2) % 4
                            entry_pv = dict(pv_by_pbuf)
                            entry_exp = dict(exp_by_stbuf)

                            def p_st(gi):
                                return P0 + 2 * gi + 1

                            def p_pv(gi):
                                return P0 + 2 * gi + 2

                            def a_exp2(gi):
                                return A0 + gi + 1

                            @block.tensor
                            def _(t):
                                for gi in range(ng2):
                                    sb2 = gi % 2
                                    if j == 0 and gi == 0:
                                        ctr.wait(t, "dve", all_proj)
                                        ctr.wait(t, "act", A0)
                                    if gi >= 2:
                                        ctr.wait(t, "act", a_exp2(gi - 2))
                                    elif entry_exp[sb2]:
                                        ctr.wait(t, "act", entry_exp[sb2])
                                    stf = stfl[sb2]
                                    for kb2 in range(2):
                                        kb = gi * 2 + kb2
                                        sl = slice(kb2 * 256, kb2 * 256 + 256)
                                        slb = slice(512 + kb2 * 256,
                                                    512 + kb2 * 256 + 256)
                                        t.matmul(stf[:, sl],
                                                 kt[0:64, kb * 128:(kb + 1) * 128],
                                                 qt[0:64, j * QW:(j + 1) * QW],
                                                 start=True, stop=True,
                                                 tile_position=(0, 0))
                                        i = t.matmul(stf[:, slb],
                                                     kt[64:128, kb * 128:(kb + 1) * 128],
                                                     qt[64:128, j * QW:(j + 1) * QW],
                                                     start=True, stop=True,
                                                     tile_position=(64, 0))
                                    ctr.inc(i, "pe")
                                    pbuf = (pb0 + gi) % 4
                                    if gi >= ng2 - 2:
                                        ctr.wait(t, "dve", M0 + (gi - (ng2 - 2)) + 1)
                                    else:
                                        ctr.wait(t, "act", a_exp2(gi))
                                    if gi == 0:
                                        ctr.wait(t, "dve", epi_done)
                                    for kb2 in range(2):
                                        kb = gi * 2 + kb2
                                        sl = slice(kb2 * 256, kb2 * 256 + 256)
                                        t.matmul(ot0[0:65, 0:256],
                                                 vg[:, kb * 130:kb * 130 + 65],
                                                 pt[:, pbuf, 0][:, sl],
                                                 start=(kb == 0), stop=(kb == C - 1))
                                        i = t.matmul(ot1[0:65, 0:256],
                                                     vg[:, kb * 130 + 65:kb * 130 + 130],
                                                     pt[:, pbuf, 1][:, sl],
                                                     start=(kb == 0), stop=(kb == C - 1))
                                    ctr.inc(i, "pe")

                            @block.scalar
                            def _(a):
                                for gi in range(ng2):
                                    sb2 = gi % 2
                                    pbuf = (pb0 + gi) % 4
                                    a.wait_ge(ctr.sems["pe"], p_st(gi))
                                    i = a.activation(
                                        out=pt[:, pbuf],
                                        in_=stfl[sb2][:, :].rearrange(
                                            "p (two q) -> p two q", two=2),
                                        func=AF.Exp, scale=0.125)
                                    ctr.inc(i, "act")

                            @block.vector
                            def _(v):
                                for mi in range(2):
                                    gi = ng2 - 2 + mi
                                    pbuf = (pb0 + gi) % 4
                                    v.wait_ge(ctr.sems["act"], a_exp2(gi))
                                    if j == 0:
                                        ctr.wait(v, "lsp", 32)
                                    msl = slice(mi * 512, mi * 512 + 512)
                                    v.tensor_mul(pt[:, pbuf, 0], pt[:, pbuf, 0],
                                                 masks_sb[:, msl])
                                    i = v.tensor_mul(pt[:, pbuf, 1], pt[:, pbuf, 1],
                                                     masks_sb[:, msl])
                                    ctr.inc(i, "dve")

                            for gi in range(ng2):
                                pv_by_pbuf[(pb0 + gi) % 4] = p_pv(gi)
                            exp_by_stbuf[0] = a_exp2(ng2 - 2)
                            exp_by_stbuf[1] = a_exp2(ng2 - 1)
                            assert ctr.c["pe"] == P0 + 2 * ng2
                            assert ctr.c["act"] == A0 + ng2
                            assert ctr.c["dve"] == M0 + 2
                            pv_last = ctr.c["pe"]

                            ob = j % 2

                            @block.vector
                            def _(v):
                                ctr.wait(v, "pe", pv_last)
                                ctr.wait(v, "dve", otsb_guard[ob])
                                v.tensor_copy(otsb[0:65, ob, 0:256], ot0[0:65, 0:256])
                                i = v.tensor_copy(otsb[0:65, ob, 256:512],
                                                  ot1[0:65, 0:256])
                                ctr.inc(i, "dve")
                            epi_done = ctr.c["dve"]

                            def emit_tail_rest(jj):
                                obb = jj % 2

                                @block.tensor
                                def _(t):
                                    ctr.wait(t, "dve", rn_done_by[jj])
                                    ctr.wait(t, "lsp", oneh_ld)
                                    t.matmul(bcs[64:65, 0:256],
                                             oneh[:, 2 * g:2 * g + 1],
                                             g_heads[:, jj * QW:(jj + 1) * QW],
                                             start=True, stop=True,
                                             tile_position=(0, 64))
                                    i = t.matmul(bcs[64:65, 256:512],
                                                 oneh[:, 2 * g + 1:2 * g + 2],
                                                 g_heads[:, jj * QW:(jj + 1) * QW],
                                                 start=True, stop=True,
                                                 tile_position=(0, 64))
                                    ctr.inc(i, "pe")
                                grow_done = ctr.c["pe"]

                                @block.vector
                                def _(v):
                                    ctr.wait(v, "pe", grow_done)
                                    i = v.tensor_mul(ssb[64:65, :], bcs[64:65, :],
                                                     rnsb[64:65, :])
                                    ctr.inc(i, "dve")
                                s_done = ctr.c["dve"]

                                @block.tensor
                                def _(t):
                                    ctr.wait(t, "dve", s_done)
                                    t.matmul(bcs[:, 0:256], ones65[64:65, :],
                                             ssb[64:65, 0:256], start=True, stop=True)
                                    i = t.matmul(bcs[:, 256:512], ones65[64:65, :],
                                                 ssb[64:65, 256:512],
                                                 start=True, stop=True)
                                    ctr.inc(i, "pe")
                                bcs_done = ctr.c["pe"]

                                @block.vector
                                def _(v):
                                    ctr.wait(v, "pe", bcs_done)
                                    if jj == 0:
                                        ctr.wait(v, "lsp", obst_dma)
                                    v.tensor_mul(oall[0:64, g, jj * QW:(jj + 1) * QW],
                                                 otsb[0:64, obb, 0:256],
                                                 bcs[0:64, 0:256])
                                    i = v.tensor_mul(obst[0:64, jj * QW:(jj + 1) * QW],
                                                     otsb[0:64, obb, 256:512],
                                                     bcs[0:64, 256:512])
                                    ctr.inc(i, "dve")
                                otsb_guard[obb] = ctr.c["dve"]

                            if j > 0:
                                emit_tail_rest(j - 1)
                            if j == NSLOT - 1:
                                @block.vector
                                def _(v):
                                    i = v.reciprocal(rnsb[64:65, 0:512],
                                                     otsb[64:65, ob, 0:512])
                                    ctr.inc(i, "dve")
                                rn_done_by[j] = ctr.c["dve"]
                                emit_tail_rest(j)
                            epi_apply = ctr.c["dve"]

                        @block.sync
                        def _(sp):
                            ctr.wait(sp, "dve", epi_apply)
                            i = sp.dma_start(out=oall[64:128, g, :], in_=obst[:])
                            ctr.inc(i, "lsp", 16)
                        obst_dma = ctr.c["lsp"]

                        if debug and g == 0:
                            @block.sync
                            def _(sp):
                                i = sp.dma_start(out=dbg_wsl[:], in_=wsl[:])
                                ctr.inc(i, "out", 16)
                                i = sp.dma_start(out=dbg_kt[:], in_=kt[:])
                                ctr.inc(i, "out", 16)
                                i = sp.dma_start(out=dbg_qt[:], in_=qt[:])
                                ctr.inc(i, "out", 16)
                                i = sp.dma_start(out=dbg_vg[:], in_=vg[:])
                                ctr.inc(i, "out", 16)
                                i = sp.dma_start(out=dbg_gh[:], in_=g_heads[:])
                                ctr.inc(i, "out", 16)

        with ExitStack() as _st6:
            pass
            if debug:
                @block.sync
                def _(sp):
                    ctr.wait(sp, "dve")
                    ctr.wait(sp, "lsp")
                    i = sp.dma_start(out=dbg_oall[:], in_=oall[:])
                    ctr.inc(i, "out", 16)

            with ExitStack() as _st7:
                wout_bf = _st7.enter_context(nc.sbuf_tensor("wout_bf", [128, DC, D], BF16))
                wistg = _st7.enter_context(nc.sbuf_tensor("wistg", [128, 2, HC, 128], I32))
                woutstg = _st7.enter_context(nc.sbuf_tensor("woutstg", [128, DC, D], F32))
                wdstg_b = _st7.enter_context(nc.sbuf_tensor("wdstg_b", [128, 2, HC, 128], BF16))
                x0b = _st7.enter_context(nc.sbuf_tensor("x0b", [128, 2, DC, P4_CH], F32))
                yf = _st7.enter_context(nc.sbuf_tensor("yf", [128, 2, DC, P4_CH], F32))
                ybf = _st7.enter_context(nc.sbuf_tensor("ybf", [128, 2, P4_CH], BF16))
                sqb = _st7.enter_context(nc.sbuf_tensor("sqb", [128, 2, P4_CH], BF16))
                x1b = _st7.enter_context(nc.sbuf_tensor("x1b", [128, DC, P4_CH], F32))
                wo0 = _st7.enter_context(nc.psum_tensor("wo0", [128, 256], F32))
                wo1 = _st7.enter_context(nc.psum_tensor("wo1", [128, 256], F32))
                st_s4 = _st7.enter_context(nc.psum_tensor("st_s4", [128, 512], F32))
                st_q4 = _st7.enter_context(nc.psum_tensor("st_q4", [128, 512], F32))
                bc4 = _st7.enter_context(nc.psum_tensor("bc4", [128, 512], F32))
                wo = [wo0, wo1]

                @block.sync
                def _(sp):
                    ctr.wait(sp, "dve")
                    ctr.wait(sp, "pe")
                    i = sp.dma_start(
                        out=woutstg[:],
                        in_=wout.rearrange("(dc p) c -> p dc c", p=128))
                    ctr.inc(i, "lgp", 16)
                wout_raw = ctr.c["lgp"]

                @block.vector
                def _(v):
                    ctr.wait(v, "lgp", wout_raw)
                    i = v.tensor_copy(wout_bf[:], woutstg[:])
                    ctr.inc(i, "dve")
                wout_ld = ctr.c["dve"]

                prep_ld = {}
                prep_cv = {}
                prep_st = {}

                def emit_w_prep(k):
                    par = k % 2
                    lds = f"ld{par}"
                    sds = f"sd{par}"
                    if k < 16:
                        w_src = wgT if k < 8 else wuT
                        w_dst = wgb if k < 8 else wub
                        dcx = k % 8
                        src_ap = w_src[dcx * 128:(dcx + 1) * 128, :]
                        dst_ap = w_dst[:, :, dcx, :]
                        stg_i = wistg[:, par, :, :].rearrange("p hc c -> p (hc c)")
                        stg_b = wdstg_b[:, par, :, :].rearrange(
                            "p hc c -> p (hcp c)", hcp=HC // 2) if False else                             wdstg_b[:, par, :, :].rearrange("p hc c -> p (hc c)").rearrange(
                                "p (hcp c) -> p hcp c", c=256)
                    else:
                        dcx = k - 16
                        src_ap = wdT[:, dcx * 128:(dcx + 1) * 128].rearrange(
                            "(hc p) c -> p hc c", p=128)
                        dst_ap = wdTb[:, dcx]
                        stg_i = wistg[:, par]
                        stg_b = wdstg_b[:, par]

                    @block.sync
                    def _(sp):
                        if k >= 2:
                            ctr.wait(sp, "act", prep_cv[k - 2])
                        i = sp.dma_start(out=stg_i, in_=src_ap)
                        prep_ld[k] = ctr.inc(i, lds, 16)

                    @block.scalar
                    def _(a):
                        ctr.wait(a, lds, prep_ld[k])
                        if k >= 2:
                            ctr.wait(a, sds, prep_st[k - 2])
                        i = a.activation(out=stg_b, in_=stg_i, func=AF.Copy)
                        prep_cv[k] = ctr.inc(i, "act")

                    @block.sync
                    def _(sp):
                        ctr.wait(sp, "act", prep_cv[k])
                        i = sp.dma_start(out=dst_ap, in_=stg_b)
                        prep_st[k] = ctr.inc(i, sds, 16)


                st4l = [st_s4, st_q4]
                evac_entry = {0: 0, 1: 0}
                stat_entry = {0: 0, 1: 0}
                yf_guard = {0: 0, 1: 0}
                x0_guard = {0: 0, 1: 0}
                x1_dma = [0]
                nch4 = T // P4_CH

                def emit_p4_tail(cc, p_apply_deps):
                    st4 = st4l[cc % 2]
                    rows_done = _ln_rows(block, ctr, rows, st4, st4[:, 256:512],
                                         P4_CH, 1.0, cbias[0:1, 1:2])
                    bc_done = _bcast_mr(block, ctr, bc4, ones_f, rows, P4_CH,
                                        rows_done)
                    x0_ld_cc, l4s_cc = p_apply_deps
                    t0c = cc * P4_CH

                    @block.vector
                    def _(v):
                        ctr.wait(v, "pe", bc_done)
                        ctr.wait(v, l4s_cc, x0_ld_cc)
                        ctr.wait(v, "lsp", x1_dma[0])
                        for dc in range(DC):
                            v.tensor_sub(yf[:, cc % 2, dc, :], yf[:, cc % 2, dc, :],
                                         bc4[:, 0:P4_CH])
                            v.tensor_mul(yf[:, cc % 2, dc, :], yf[:, cc % 2, dc, :],
                                         bc4[:, 256:256 + P4_CH])
                            i = v.tensor_add(x1b[:, dc, :], yf[:, cc % 2, dc, :],
                                             x0b[:, cc % 2, dc, :])
                        ctr.inc(i, "dve")
                    x1_done = ctr.c["dve"]
                    yf_guard[cc % 2] = x1_done
                    x0_guard[cc % 2] = x1_done

                    @block.sync
                    def _(sp):
                        ctr.wait(sp, "dve", x1_done)
                        i = sp.dma_start(
                            out=x1T[:, t0c:t0c + P4_CH].rearrange(
                                "(dc p) t -> p dc t", p=128),
                            in_=x1b[:])
                        ctr.inc(i, "lsp", 16)
                        if debug:
                            i = sp.dma_start(
                                out=dbg_x1[:, t0c:t0c + P4_CH].rearrange(
                                    "(dc p) t -> p dc t", p=128),
                                in_=x1b[:])
                            ctr.inc(i, "out", 16)
                    x1_dma[0] = ctr.c["lsp"]

                prev_deps = None
                for ch in range(nch4):
                    for k3 in range(3):
                        emit_w_prep(3 * ch + k3)
                    t0 = ch * P4_CH
                    xb = ch % 2
                    l4s = f"l4{xb}"

                    @block.sync
                    def _(sp):
                        if ch >= 2:
                            ctr.wait(sp, "dve", x0_guard[xb])
                        i = sp.dma_start(
                            out=x0b[:, xb],
                            in_=xT_own[:, t0:t0 + P4_CH].rearrange(
                                "(dc p) t -> p dc t", p=128))
                        ctr.inc(i, l4s, 16)
                    x0_ld = ctr.c[l4s]

                    Pw0 = ctr.c["pe"]
                    Aw0 = ctr.c["act"]
                    Dw0 = ctr.c["dve"]
                    st4 = st4l[xb]

                    @block.tensor
                    def _(t):
                        for dc in range(DC):
                            pb = dc % 2
                            if ch == 0 and dc == 0:
                                ctr.wait(t, "dve", wout_ld)
                                ctr.wait(t, "lsp")
                                ctr.wait(t, "dve")
                            if dc >= 2:
                                t.wait_ge(ctr.sems["act"], Aw0 + dc - 1)
                            elif evac_entry[pb]:
                                t.wait_ge(ctr.sems["act"], evac_entry[pb])
                            for ic in range(DC):
                                i = t.matmul(wo[pb][:, 0:P4_CH],
                                             wout_bf[:, ic, dc * 128:(dc + 1) * 128],
                                             oall[:, ic, t0:t0 + P4_CH],
                                             start=(ic == 0), stop=(ic == DC - 1))
                            ctr.inc(i, "pe")
                        for dc in range(DC):
                            pb = dc % 2
                            t.wait_ge(ctr.sems["dve"], Dw0 + dc + 1)
                            i = t.matmul(st4[0:1, 0:P4_CH], onec_b[:], ybf[:, pb],
                                         start=(dc == 0), stop=False)
                            i = t.matmul(st4[0:1, 256:256 + P4_CH], onec_b[:],
                                         sqb[:, pb],
                                         start=False, stop=(dc == DC - 1))
                            ctr.inc(i, "pe")

                    @block.scalar
                    def _(a):
                        for dc in range(DC):
                            pb = dc % 2
                            a.wait_ge(ctr.sems["pe"], Pw0 + dc + 1)
                            if dc == 0 and yf_guard[xb]:
                                a.wait_ge(ctr.sems["dve"], yf_guard[xb])
                            i = a.copy(yf[:, xb, dc, :], wo[pb][:, 0:P4_CH])
                            ctr.inc(i, "act")

                    @block.vector
                    def _(v):
                        for dc in range(DC):
                            pb = dc % 2
                            v.wait_ge(ctr.sems["act"], Aw0 + dc + 1)
                            if dc >= 2:
                                v.wait_ge(ctr.sems["pe"], Pw0 + 8 + dc - 1)
                            elif stat_entry[pb]:
                                v.wait_ge(ctr.sems["pe"], stat_entry[pb])
                            v.tensor_copy(ybf[:, pb], yf[:, xb, dc, :])
                            i = v.tensor_mul(sqb[:, pb], yf[:, xb, dc, :],
                                             yf[:, xb, dc, :])
                            ctr.inc(i, "dve")

                    evac_entry[0] = Aw0 + 7
                    evac_entry[1] = Aw0 + 8
                    stat_entry[0] = Pw0 + 8 + 7
                    stat_entry[1] = Pw0 + 8 + 8
                    assert ctr.c["pe"] == Pw0 + 16
                    assert ctr.c["act"] == Aw0 + 8
                    assert ctr.c["dve"] == Dw0 + 8

                    if prev_deps is not None:
                        emit_p4_tail(ch - 1, prev_deps)
                    prev_deps = (x0_ld, l4s)
                emit_p4_tail(nch4 - 1, prev_deps)
                x1_all = ctr.c["lsp"]

            with ExitStack() as _st8:
                wd_bf = _st8.enter_context(nc.sbuf_tensor("wd_bf", [128, 2, HC, 128], BF16))
                wgu = _st8.enter_context(nc.sbuf_tensor("wgu", [128, 2, 2, DC, 256], BF16))
                x1f = _st8.enter_context(nc.sbuf_tensor("x1f", [128, 2, DC, P5_CH], F32))
                x2n = _st8.enter_context(nc.sbuf_tensor("x2n", [128, DC, P5_CH], BF16))
                hbuf = _st8.enter_context(nc.sbuf_tensor("hbuf", [128, HC, P5_CH], BF16))
                silu_b = _st8.enter_context(nc.sbuf_tensor("silu_b", [128, 2, P5_CH], F32))
                yf5 = _st8.enter_context(nc.sbuf_tensor("yf5", [128, DC, P5_CH], F32))
                ybf5 = _st8.enter_context(nc.sbuf_tensor("ybf5", [128, 2, P5_CH], BF16))
                sqb5 = _st8.enter_context(nc.sbuf_tensor("sqb5", [128, 2, P5_CH], BF16))
                outb = _st8.enter_context(nc.sbuf_tensor("outb", [128, DC, P5_CH], F32))
                gu0a = _st8.enter_context(nc.psum_tensor("gu0a", [128, 512], F32))
                gu1a = _st8.enter_context(nc.psum_tensor("gu1a", [128, 512], F32))
                gu0b = _st8.enter_context(nc.psum_tensor("gu0b", [128, 512], F32))
                gu1b = _st8.enter_context(nc.psum_tensor("gu1b", [128, 512], F32))
                gu0l = [gu0a, gu0b]
                gu1l = [gu1a, gu1b]
                dn0 = _st8.enter_context(nc.psum_tensor("dn0", [128, 256], F32))
                dn1 = _st8.enter_context(nc.psum_tensor("dn1", [128, 256], F32))
                st5 = _st8.enter_context(nc.psum_tensor("st5", [128, 512], F32))
                bc5 = _st8.enter_context(nc.psum_tensor("bc5", [128, 512], F32))
                dn = [dn0, dn1]
                nch = T // P5_CH
                wprep_s0 = 12 * 16
                wprep_s1 = 12 * 16
                wgu_guard = {0: 0, 1: 0}
                gu_entry = {0: 0, 1: 0}
                wgu_ld = {}
                stat_guard = {0: 0, 1: 0}
                ev5_guard = {0: 0, 1: 0}
                wd_guard = {0: 0, 1: 0}
                x1f_guard = {0: 0, 1: 0}
                out_dma = 0
                for ch in range(nch):
                    t0 = ch * P5_CH
                    xb = ch % 2

                    l5s = f"l5{xb}"

                    @block.sync
                    def _(sp):
                        if ch == 0:
                            ctr.wait(sp, "lsp", x1_all)
                        ctr.wait(sp, "dve", x1f_guard[xb])
                        i = sp.dma_start(
                            out=x1f[:, xb],
                            in_=x1T[:, t0:t0 + P5_CH].rearrange(
                                "(dc p) t -> p dc t", p=128))
                        ctr.inc(i, l5s, 16)
                    x1_ld = ctr.c[l5s]

                    for dc in range(DC):
                        b2 = dc % 2

                        @block.vector
                        def _(v):
                            ctr.wait(v, l5s, x1_ld)
                            ctr.wait(v, "pe", stat_guard[b2])
                            v.tensor_copy(ybf5[:, b2], x1f[:, xb, dc, :])
                            i = v.tensor_mul(sqb5[:, b2], x1f[:, xb, dc, :],
                                             x1f[:, xb, dc, :])
                            ctr.inc(i, "dve")
                        xb_done = ctr.c["dve"]

                        @block.tensor
                        def _(t):
                            ctr.wait(t, "dve", xb_done)
                            t.matmul(st5[0:1, 0:P5_CH], onec_b[:], ybf5[:, b2],
                                     start=(dc == 0), stop=False)
                            i = t.matmul(st5[0:1, 256:256 + P5_CH], onec_b[:],
                                         sqb5[:, b2],
                                         start=False, stop=(dc == DC - 1))
                            ctr.inc(i, "pe")
                        stat_guard[b2] = ctr.c["pe"]

                    rows_done = _ln_rows(block, ctr, rows, st5, st5[:, 256:512], P5_CH,
                                         1.0, cbias[0:1, 1:2])
                    bc_done = _bcast_mr(block, ctr, bc5, ones_f, rows, P5_CH, rows_done)

                    @block.vector
                    def _(v):
                        ctr.wait(v, "pe", bc_done)
                        for dc in range(DC):
                            v.tensor_sub(yf5[:, dc, :], x1f[:, xb, dc, :],
                                         bc5[:, 0:P5_CH])
                            i = v.tensor_mul(x2n[:, dc, :], yf5[:, dc, :],
                                             bc5[:, 256:256 + P5_CH])
                        ctr.inc(i, "dve")
                    x2n_done = ctr.c["dve"]

                    Pg0 = ctr.c["pe"]
                    Ag0 = ctr.c["act"]
                    Dg0 = ctr.c["dve"]

                    def g_gu(hc):
                        return Pg0 + hc + 1

                    def g_silu(hc):
                        return Ag0 + hc + 1

                    def g_h(hc):
                        return Dg0 + hc + 1

                    for hcp in range(HC // 2):
                        wpar = hcp % 2
                        lws5 = f"lw{wpar}"

                        @block.sync
                        def _(sp):
                            if ch == 0 and hcp < 2:
                                ctr.wait(sp, "sd0", wprep_s0)
                                ctr.wait(sp, "sd1", wprep_s1)
                            ctr.wait(sp, "pe", wgu_guard[wpar])
                            i = sp.dma_start(out=wgu[:, wpar, 0],
                                             in_=wgb[:, hcp])
                            ctr.inc(i, lws5, 16)
                            i = sp.dma_start(out=wgu[:, wpar, 1],
                                             in_=wub[:, hcp])
                            ctr.inc(i, lws5, 16)
                        wgu_ld[(ch, hcp)] = ctr.c[lws5]

                        @block.tensor
                        def _(t):
                            for hc in (2 * hcp, 2 * hcp + 1):
                                gb = hc % 2
                                cc = (hc % 2) * 128
                                if hc == 2 * hcp:
                                    ctr.wait(t, lws5, wgu_ld[(ch, hcp)])
                                if hc == 0:
                                    ctr.wait(t, "dve", x2n_done)
                                if hc >= 2:
                                    ctr.wait(t, "dve", g_h(hc - 2))
                                elif gu_entry[gb]:
                                    ctr.wait(t, "dve", gu_entry[gb])
                                for ic in range(DC):
                                    t.matmul(gu0l[gb][:, 0:P5_CH],
                                             wgu[:, wpar, 0, ic, cc:cc + 128],
                                             x2n[:, ic, :],
                                             start=(ic == 0), stop=(ic == DC - 1))
                                for ic in range(DC):
                                    i = t.matmul(gu1l[gb][:, 0:P5_CH],
                                                 wgu[:, wpar, 1, ic, cc:cc + 128],
                                                 x2n[:, ic, :],
                                                 start=(ic == 0), stop=(ic == DC - 1))
                                ctr.inc(i, "pe")
                        wgu_guard[wpar] = ctr.c["pe"]

                    @block.scalar
                    def _(a):
                        for hc in range(HC):
                            gb = hc % 2
                            a.wait_ge(ctr.sems["pe"], g_gu(hc))
                            if hc >= 2:
                                a.wait_ge(ctr.sems["dve"], g_h(hc - 2))
                            i = a.activation(out=silu_b[:, gb],
                                             in_=gu0l[gb][:, 0:P5_CH],
                                             func=AF.Silu, scale=DEQ)
                            ctr.inc(i, "act")

                    @block.vector
                    def _(v):
                        for hc in range(HC):
                            gb = hc % 2
                            v.wait_ge(ctr.sems["act"], g_silu(hc))
                            i = v.tensor_mul(hbuf[:, hc, :], silu_b[:, gb],
                                             gu1l[gb][:, 0:P5_CH])
                            ctr.inc(i, "dve")

                    gu_entry[0] = g_h(HC - 2)
                    gu_entry[1] = g_h(HC - 1)
                    assert ctr.c["pe"] == Pg0 + HC
                    assert ctr.c["act"] == Ag0 + HC
                    assert ctr.c["dve"] == Dg0 + HC
                    h_all = ctr.c["dve"]

                    for dc in range(DC):
                        pb = dc % 2

                        lds = f"ld{pb}"

                        @block.sync
                        def _(sp):
                            if ch == 0 and dc < 2:
                                ctr.wait(sp, "sd0", wprep_s0)
                                ctr.wait(sp, "sd1", wprep_s1)
                            ctr.wait(sp, "pe", wd_guard[pb])
                            i = sp.dma_start(out=wd_bf[:, pb], in_=wdTb[:, dc])
                            ctr.inc(i, lds, 16)
                        wd_ld = ctr.c[lds]

                        @block.tensor
                        def _(t):
                            ctr.wait(t, lds, wd_ld)
                            ctr.wait(t, "dve", h_all)
                            ctr.wait(t, "act", ev5_guard[pb])
                            for hc in range(HC):
                                i = t.matmul(dn[pb][:, 0:P5_CH],
                                             wd_bf[:, pb, hc, :],
                                             hbuf[:, hc, :],
                                             start=(hc == 0), stop=(hc == HC - 1))
                            ctr.inc(i, "pe")
                        wd_guard[pb] = ctr.c["pe"]
                        mm_done = ctr.c["pe"]

                        @block.scalar
                        def _(a):
                            ctr.wait(a, "pe", mm_done)
                            i = a.copy(yf5[:, dc, :], dn[pb][:, 0:P5_CH])
                            ctr.inc(i, "act")
                        ev5_guard[pb] = ctr.c["act"]

                        @block.vector
                        def _(v):
                            ctr.wait(v, "act", ev5_guard[pb])
                            v.tensor_copy(ybf5[:, pb], yf5[:, dc, :])
                            i = v.tensor_mul(sqb5[:, pb], yf5[:, dc, :],
                                             yf5[:, dc, :])
                            ctr.inc(i, "dve")
                        yb_done = ctr.c["dve"]

                        @block.tensor
                        def _(t):
                            ctr.wait(t, "dve", yb_done)
                            t.matmul(st5[0:1, 0:P5_CH], onec_b[:], ybf5[:, pb],
                                     start=(dc == 0), stop=False)
                            i = t.matmul(st5[0:1, 256:256 + P5_CH], onec_b[:],
                                         sqb5[:, pb],
                                         start=False, stop=(dc == DC - 1))
                            ctr.inc(i, "pe")

                    rows2 = _ln_rows(block, ctr, rows, st5, st5[:, 256:512], P5_CH,
                                     1.0, cbias[0:1, 2:3])
                    bc2 = _bcast_mr(block, ctr, bc5, ones_f, rows, P5_CH, rows2)

                    @block.vector
                    def _(v):
                        ctr.wait(v, "pe", bc2)
                        ctr.wait(v, "out", out_dma)
                        for dc in range(DC):
                            v.tensor_sub(yf5[:, dc, :], yf5[:, dc, :], bc5[:, 0:P5_CH])
                            v.tensor_mul(yf5[:, dc, :], yf5[:, dc, :],
                                         bc5[:, 256:256 + P5_CH])
                            i = v.tensor_add(outb[:, dc, :], yf5[:, dc, :],
                                             x1f[:, xb, dc, :])
                        ctr.inc(i, "dve")
                    out_done = ctr.c["dve"]
                    x1f_guard[xb] = out_done

                    @block.sync
                    def _(sp):
                        ctr.wait(sp, "dve", out_done)
                        i = sp.dma_start(
                            out=outT[:, t0:t0 + P5_CH].rearrange(
                                "(dc p) t -> p dc t", p=128),
                            in_=outb[:])
                        ctr.inc(i, "out", 16)
                    out_dma = ctr.c["out"]

                @block.sync
                def _(sp):
                    ctr.wait(sp, "out")

    ctr.close()
    return nc


def _host_prep(L0_attn_tokens, L0_attn_norm_w, L0_attn_Wq, L0_attn_Wkv,
               L0_attn_Wout, L0_attn_Wg, L0_ffn_w_gate, L0_ffn_w_up,
               L0_ffn_w_down):
    bf = ml_dtypes.bfloat16
    shared = {
        "wq": np.ascontiguousarray(L0_attn_Wq, dtype=np.float32),
        "wkv": np.ascontiguousarray(L0_attn_Wkv, dtype=np.float32),
        "wgh": np.ascontiguousarray(L0_attn_Wg, dtype=np.float32),
        "wout": np.ascontiguousarray(L0_attn_Wout, dtype=np.float32),
        "normw": np.ascontiguousarray(
            np.asarray(L0_attn_norm_w, dtype=np.float32).reshape(DC, 128).T),
        "wgT": np.ascontiguousarray(np.asarray(L0_ffn_w_gate, dtype=np.int32).T),
        "wuT": np.ascontiguousarray(np.asarray(L0_ffn_w_up, dtype=np.int32).T),
        "wdT": np.ascontiguousarray(np.asarray(L0_ffn_w_down, dtype=np.int32).T),
    }
    k_i = np.arange(128)[:, None]
    q_i = np.arange(QW)[None, :]
    t1 = (q_i >= k_i).astype(bf)
    t2 = (q_i >= 128 + k_i).astype(bf)
    ones = np.ones((128, QW), dtype=bf)
    zeros = np.zeros((128, QW), dtype=bf)
    shared["ident16"] = np.eye(H, dtype=bf)
    masks_by_par = {
        0: np.ascontiguousarray(np.stack([t1, t2, zeros, zeros])),
        1: np.ascontiguousarray(np.stack([ones, ones, t1, t2])),
    }
    toks = np.asarray(L0_attn_tokens, dtype=np.float32)
    own_cols_by_par = {
        par: np.concatenate([np.arange((2 * j + par) * QW, (2 * j + par + 1) * QW)
                             for j in range(NSLOT)])
        for par in range(2)
    }
    in_maps = []
    for c in range(8):
        seq, par = c // 2, c % 2
        xt = np.ascontiguousarray(toks[seq].T)
        m = dict(shared)
        m["xT_kv"] = xt
        m["xT_own"] = np.ascontiguousarray(xt[:, own_cols_by_par[par]])
        m["masks"] = masks_by_par[par]
        in_maps.append(m)
    return in_maps, own_cols_by_par


def kernel(**inputs):
    in_maps, own_cols_by_par = _host_prep(**inputs)
    nc = build_program()
    res = run_bass_kernel_spmd(nc, in_maps, core_ids=list(range(8)))
    out = np.empty((4, S, D), dtype=np.float32)
    for c in range(8):
        seq, par = c // 2, c % 2
        out[seq, own_cols_by_par[par], :] = res.results[c]["outT"].T
    return out


# version 12
# speedup vs baseline: 1.2244x; 1.0181x over previous
from contextlib import ExitStack
import numpy as np
import ml_dtypes

import concourse.bass as bass
import concourse.mybir as mybir
from concourse.bass_utils import run_bass_kernel_spmd

F32 = mybir.dt.float32
BF16 = mybir.dt.bfloat16
I32 = mybir.dt.int32
AF = mybir.ActivationFunctionType
ALU = mybir.AluOpType

D = 1024
H = 16
S = 4096
T = 2048
HID = 3072
DC = 8
HC = 24
NSLOT = 8
QW = 256
NKB = 32
EPS_L = 1e-5
EPS_R = float(np.finfo(np.float32).eps)
DEQ = 1.0 / 127.0
C2EPS = float((127.0 * 127.0) ** 2 * EPS_L)

P1_CH = 512
P4_CH = 256
P5_CH = 256


class Ctr:
    def __init__(self, nc, names):
        self.sems = {}
        self.c = {}
        self.guards = []
        for n in names:
            g = nc.semaphore(n)
            self.guards.append(g)
            self.sems[n] = g.__enter__()
            self.c[n] = 0

    def inc(self, instr, name, n=1):
        instr.then_inc(self.sems[name], n)
        self.c[name] += n
        return self.c[name]

    def wait(self, eng, name, val=None):
        v = self.c[name] if val is None else val
        if v > 0:
            eng.wait_ge(self.sems[name], v)

    def close(self):
        for g in reversed(self.guards):
            g.__exit__(None, None, None)


R_M, R_MSQ, R_MM, R_V, R_SD, R_R = (k * 512 for k in range(6))


def _ln_rows(block, ctr, rows, st_s, st_q, n, sqrt_scale, sqrt_bias):
    @block.vector
    def _(v):
        ctr.wait(v, "pe")
        v.tensor_scalar_mul(rows[0:1, R_M:R_M + n], st_s[0:1, :n], 1.0 / D)
        v.tensor_scalar_mul(rows[0:1, R_MSQ:R_MSQ + n], st_q[0:1, :n], 1.0 / D)
        v.tensor_mul(rows[0:1, R_MM:R_MM + n], rows[0:1, R_M:R_M + n],
                     rows[0:1, R_M:R_M + n])
        i = v.tensor_sub(rows[0:1, R_V:R_V + n], rows[0:1, R_MSQ:R_MSQ + n],
                         rows[0:1, R_MM:R_MM + n])
        ctr.inc(i, "dve")

    @block.scalar
    def _(a):
        ctr.wait(a, "dve")
        i = a.activation(out=rows[0:1, R_SD:R_SD + n], in_=rows[0:1, R_V:R_V + n],
                         func=AF.Sqrt, scale=sqrt_scale, bias=sqrt_bias)
        ctr.inc(i, "act")

    @block.vector
    def _(v):
        ctr.wait(v, "act")
        i = v.reciprocal(rows[0:1, R_R:R_R + n], rows[0:1, R_SD:R_SD + n])
        ctr.inc(i, "dve")
    return ctr.c["dve"]


def _bcast_mr(block, ctr, bc, ones_f, rows, n, rows_done):
    @block.tensor
    def _(t):
        ctr.wait(t, "dve", rows_done)
        t.matmul(bc[:, 0:n], ones_f[:], rows[0:1, R_M:R_M + n],
                 start=True, stop=True)
        i = t.matmul(bc[:, 256:256 + n], ones_f[:], rows[0:1, R_R:R_R + n],
                     start=True, stop=True)
        ctr.inc(i, "pe")
    return ctr.c["pe"]


def build_program(debug=False, ngroups=8):
    nc = bass.Bass("TRN2", target_bir_lowering=False)

    xT_own = nc.dram_tensor("xT_own", [D, T], F32, kind="ExternalInput").ap()
    xT_kv = nc.dram_tensor("xT_kv", [D, S], F32, kind="ExternalInput").ap()
    wq = nc.dram_tensor("wq", [D, D], F32, kind="ExternalInput").ap()
    wkv = nc.dram_tensor("wkv", [D, 2 * D], F32, kind="ExternalInput").ap()
    wgh = nc.dram_tensor("wgh", [D, H], F32, kind="ExternalInput").ap()
    wout = nc.dram_tensor("wout", [D, D], F32, kind="ExternalInput").ap()
    normw = nc.dram_tensor("normw", [128, DC], F32, kind="ExternalInput").ap()
    wgT = nc.dram_tensor("wgT", [D, HID], I32, kind="ExternalInput").ap()
    wuT = nc.dram_tensor("wuT", [D, HID], I32, kind="ExternalInput").ap()
    wdT = nc.dram_tensor("wdT", [HID, D], I32, kind="ExternalInput").ap()
    masks = nc.dram_tensor("masks", [4, 128, QW], BF16, kind="ExternalInput").ap()
    ident16 = nc.dram_tensor("ident16", [H, H], BF16, kind="ExternalInput").ap()
    outT = nc.dram_tensor("outT", [D, T], F32, kind="ExternalOutput").ap()
    x1T = nc.dram_tensor("x1T", [D, T], F32, kind="Internal").ap()
    wdTb = nc.dram_tensor("wdTb", [128, DC, HC, 128], BF16, kind="Internal").ap()
    wgb = nc.dram_tensor("wgb", [128, HC // 2, DC, 256], BF16, kind="Internal").ap()
    wub = nc.dram_tensor("wub", [128, HC // 2, DC, 256], BF16, kind="Internal").ap()
    if debug:
        dbg_xatt_own = nc.dram_tensor("dbg_xatt_own", [128, DC, T], BF16,
                                      kind="ExternalOutput").ap()
        dbg_xatt_kv = nc.dram_tensor("dbg_xatt_kv", [128, DC, S], BF16,
                                     kind="ExternalOutput").ap()
        dbg_kt = nc.dram_tensor("dbg_kt", [128, S], BF16, kind="ExternalOutput").ap()
        dbg_qt = nc.dram_tensor("dbg_qt", [128, T], BF16, kind="ExternalOutput").ap()
        dbg_vg = nc.dram_tensor("dbg_vg", [128, NKB * 130], BF16,
                                kind="ExternalOutput").ap()
        dbg_gh = nc.dram_tensor("dbg_gh", [H, T], BF16, kind="ExternalOutput").ap()
        dbg_wsl = nc.dram_tensor("dbg_wsl", [128, 2, 3, DC, 128], BF16,
                                 kind="ExternalOutput").ap()
        dbg_wsl2 = nc.dram_tensor("dbg_wsl2", [128, 2, 3, DC, 128], BF16,
                                  kind="ExternalOutput").ap()
        dbg_oall = nc.dram_tensor("dbg_oall", [128, DC, T], BF16,
                                  kind="ExternalOutput").ap()
        dbg_x1 = nc.dram_tensor("dbg_x1", [D, T], F32, kind="ExternalOutput").ap()

    ctr = Ctr(nc, ["lsp", "lgp", "pe", "act", "dve", "gp", "out",
                   "lw0", "lw1", "lx0", "lx1", "l40", "l41",
                   "l50", "l51", "ld0", "ld1", "sd0", "sd1"])

    with ExitStack() as _st0:
        ones_f = _st0.enter_context(nc.sbuf_tensor("ones_f", [1, 128], F32))
        onec_b = _st0.enter_context(nc.sbuf_tensor("onec_b", [128, 1], BF16))
        normw_sb = _st0.enter_context(nc.sbuf_tensor("normw_sb", [128, DC], F32))
        rows = _st0.enter_context(nc.sbuf_tensor("rows", [1, 6 * 512], F32))
        oall = _st0.enter_context(nc.sbuf_tensor("oall", [128, DC, T], BF16))
        masks_sb = _st0.enter_context(nc.sbuf_tensor("masks_sb", [128, 4 * QW], BF16))
        cbias = _st0.enter_context(nc.sbuf_tensor("cbias", [1, 4], F32))
        block = _st0.enter_context(nc.Block())
        @block.sync
        def _(sp):
            i = sp.dma_start(out=normw_sb[:], in_=normw[:])
            ctr.inc(i, "lsp", 16)
            i = sp.dma_start(
                out=masks_sb[:, :].rearrange("p (m q) -> p m q", m=4),
                in_=masks.rearrange("m p q -> p m q"))
            ctr.inc(i, "lsp", 16)

        @block.vector
        def _(v):
            v.memset(ones_f[:], 1.0)
            v.memset(onec_b[:], 1.0)
            v.memset(cbias[0:1, 0:1], EPS_R * EPS_L)
            v.memset(cbias[0:1, 1:2], EPS_L)
            i = v.memset(cbias[0:1, 2:3], C2EPS)
            ctr.inc(i, "dve")

        with ExitStack() as _st1:
            xatt_kv = _st1.enter_context(nc.sbuf_tensor("xatt_kv", [128, DC, S], BF16))
            xatt_own = _st1.enter_context(nc.sbuf_tensor("xatt_own", [128, DC, T], BF16))
            with ExitStack() as _st2:
                xraw = _st2.enter_context(nc.sbuf_tensor("xraw", [128, 2, DC, P1_CH], F32))
                xbf = _st2.enter_context(nc.sbuf_tensor("xbf", [128, 2, P1_CH], BF16))
                sqbf = _st2.enter_context(nc.sbuf_tensor("sqbf", [128, 2, P1_CH], BF16))
                tbuf = _st2.enter_context(nc.sbuf_tensor("tbuf", [128, P1_CH], F32))
                st_s = _st2.enter_context(nc.psum_tensor("st_s", [128, 512], F32))
                st_q = _st2.enter_context(nc.psum_tensor("st_q", [128, 512], F32))
                bc1 = _st2.enter_context(nc.psum_tensor("bc1", [128, 512], F32))
                chunks = [("kv", i) for i in range(S // P1_CH)] + [
                    ("own", i) for i in range(T // P1_CH)]
                apply_done = {}
                buf_guard = {0: 0, 1: 0}
                for ci, (kind, i0) in enumerate(chunks):
                    buf = ci % 2
                    src = xT_kv if kind == "kv" else xT_own
                    dst = xatt_kv if kind == "kv" else xatt_own
                    t0 = i0 * P1_CH

                    lxs = f"lx{buf}"

                    @block.sync
                    def _(sp):
                        if ci >= 2:
                            ctr.wait(sp, "dve", apply_done[ci - 2])
                        i = sp.dma_start(
                            out=xraw[:, buf],
                            in_=src[:, t0:t0 + P1_CH].rearrange(
                                "(dc p) t -> p dc t", p=128))
                        ctr.inc(i, lxs, 16)
                    ld_here = ctr.c[lxs]

                    for dc in range(DC):
                        b2 = dc % 2

                        @block.vector
                        def _(v):
                            ctr.wait(v, lxs, ld_here)
                            ctr.wait(v, "pe", buf_guard[b2])
                            v.tensor_copy(xbf[:, b2], xraw[:, buf, dc, :])
                            i = v.tensor_mul(sqbf[:, b2], xraw[:, buf, dc, :],
                                             xraw[:, buf, dc, :])
                            ctr.inc(i, "dve")
                        xb_done = ctr.c["dve"]

                        @block.tensor
                        def _(t):
                            ctr.wait(t, "dve", xb_done)
                            i = t.matmul(st_s[0:1, :], onec_b[:], xbf[:, b2],
                                         start=(dc == 0), stop=(dc == DC - 1))
                            i = t.matmul(st_q[0:1, :], onec_b[:], sqbf[:, b2],
                                         start=(dc == 0), stop=(dc == DC - 1))
                            ctr.inc(i, "pe")
                        buf_guard[b2] = ctr.c["pe"]

                    rows_done = _ln_rows(block, ctr, rows, st_s, st_q, P1_CH,
                                         1.0 + EPS_R, cbias[0:1, 0:1])

                    for half in range(P1_CH // 256):
                        hs = slice(half * 256, half * 256 + 256)

                        @block.tensor
                        def _(t):
                            if half == 0:
                                ctr.wait(t, "dve", rows_done)
                            else:
                                ctr.wait(t, "dve")
                            t.matmul(bc1[:, 0:256], ones_f[:],
                                     rows[0:1, R_M + half * 256:R_M + half * 256 + 256],
                                     start=True, stop=True)
                            i = t.matmul(bc1[:, 256:512], ones_f[:],
                                         rows[0:1, R_R + half * 256:R_R + half * 256 + 256],
                                         start=True, stop=True)
                            ctr.inc(i, "pe")
                        bc_now = ctr.c["pe"]

                        @block.vector
                        def _(v):
                            ctr.wait(v, "pe", bc_now)
                            for dc in range(DC):
                                v.tensor_sub(tbuf[:, 0:256],
                                             xraw[:, buf, dc, hs], bc1[:, 0:256])
                                i = v.scalar_tensor_tensor(
                                    dst[:, dc, t0 + half * 256:t0 + half * 256 + 256],
                                    tbuf[:, 0:256], normw_sb[:, dc:dc + 1],
                                    bc1[:, 256:512], ALU.mult, ALU.mult)
                            ctr.inc(i, "dve")
                        apply_done[ci] = ctr.c["dve"]

            if debug:
                @block.sync
                def _(sp):
                    ctr.wait(sp, "dve")
                    i = sp.dma_start(out=dbg_xatt_own[:], in_=xatt_own[:])
                    ctr.inc(i, "out", 16)
                    i = sp.dma_start(out=dbg_xatt_kv[:], in_=xatt_kv[:])
                    ctr.inc(i, "out", 16)

            with ExitStack() as _st3:
                g_heads = _st3.enter_context(nc.sbuf_tensor("g_heads", [H, T], BF16))
                kt = _st3.enter_context(nc.sbuf_tensor("kt", [128, S], BF16))
                vg = _st3.enter_context(nc.sbuf_tensor("vg", [128, NKB * 130], BF16))
                qt = _st3.enter_context(nc.sbuf_tensor("qt", [128, T], BF16))
                pt = _st3.enter_context(nc.sbuf_tensor("pt", [128, 4, 2, 512], BF16))
                wsl = _st3.enter_context(nc.sbuf_tensor("wsl", [128, 2, 3, DC, 128], BF16))
                wstg = _st3.enter_context(nc.sbuf_tensor("wstg", [128, DC, 128], F32))
                wgh_sb = _st3.enter_context(nc.sbuf_tensor("wgh_sb", [128, DC, H], BF16))
                wghstg = _st3.enter_context(nc.sbuf_tensor("wghstg", [128, DC, H], F32))
                obst = _st3.enter_context(nc.sbuf_tensor("obst", [64, T], BF16))
                rnsb = _st3.enter_context(nc.sbuf_tensor("rnsb", [65, 512], F32))
                ssb = _st3.enter_context(nc.sbuf_tensor("ssb", [65, 512], F32))
                otsb = _st3.enter_context(nc.sbuf_tensor("otsb", [65, 2, 512], F32))
                ones65 = _st3.enter_context(nc.sbuf_tensor("ones65", [65, 128], F32))
                oneh = _st3.enter_context(nc.sbuf_tensor("oneh", [16, 16], BF16))
                bcs_sb = _st3.enter_context(nc.sbuf_tensor("bcs_sb", [128, 512], F32))
                vgv = vg[:, :].rearrange("p (kb two c) -> p kb two c", two=2, c=65)

                @block.vector
                def _(v):
                    i = v.memset(ones65[:], 1.0)
                    ctr.inc(i, "dve")

                @block.sync
                def _(sp):
                    i = sp.dma_start(out=oneh[:], in_=ident16[:])
                    ctr.inc(i, "lsp", 16)
                oneh_ld = ctr.c["lsp"]

                proj_done = {}
                w_ready = {}

                wcvt_done = {}

                def load_group_weights(g):
                    lws = f"lw{g % 2}"
                    srcs = [wq[:, g * 128:(g + 1) * 128],
                            wkv[:, g * 128:(g + 1) * 128],
                            wkv[:, D + g * 128:D + (g + 1) * 128]]
                    for s3 in range(3):
                        @block.sync
                        def _(sp):
                            prev = wcvt_done.get((g, s3 - 1),
                                                 wcvt_done.get((g - 1, 2), 0))
                            ctr.wait(sp, "dve", prev)
                            i = sp.dma_start(
                                out=wstg[:],
                                in_=srcs[s3].rearrange("(dc p) c -> p dc c", p=128))
                            ctr.inc(i, lws, 16)
                        ld = ctr.c[lws]

                        @block.vector
                        def _(v):
                            ctr.wait(v, lws, ld)
                            if g >= 2 and s3 == 0:
                                ctr.wait(v, "pe", proj_done[g - 2])
                            i = v.tensor_copy(wsl[:, g % 2, s3], wstg[:])
                            wcvt_done[(g, s3)] = ctr.inc(i, "dve")
                    w_ready[g] = wcvt_done[(g, 2)]

                load_group_weights(0)
                if debug:
                    @block.sync
                    def _(sp):
                        ctr.wait(sp, "dve", w_ready[0])
                        i = sp.dma_start(out=dbg_wsl2[:], in_=wsl[:])
                        ctr.inc(i, "out", 16)

                @block.sync
                def _(sp):
                    i = sp.dma_start(
                        out=wghstg[:],
                        in_=wgh.rearrange("(dc p) h -> p dc h", p=128))
                    ctr.inc(i, "lgp", 16)
                wgh_raw = ctr.c["lgp"]

                @block.vector
                def _(v):
                    ctr.wait(v, "lgp", wgh_raw)
                    i = v.tensor_copy(wgh_sb[:], wghstg[:])
                    ctr.inc(i, "dve")
                wgh_ld = ctr.c["dve"]

                obst_dma = 0
                epi_done = 0
                otsb_guard = {0: 0, 1: 0}
                rn_done_by = {}
                prev_exp = 0
                exp_by_stbuf = {0: 0, 1: 0}
                pv_by_pbuf = {0: 0, 1: 0, 2: 0, 3: 0}
                pbuf_next = 0
                for g in range(ngroups):
                    wb = g % 2

                    with ExitStack() as _st4:
                        prj0 = _st4.enter_context(nc.psum_tensor(f"prj0_{g}", [128, 512], F32))
                        prj1 = _st4.enter_context(nc.psum_tensor(f"prj1_{g}", [128, 512], F32))
                        gat = _st4.enter_context(nc.psum_tensor(f"gat_{g}", [128, 512], F32))
                        prj = [prj0, prj1]
                        prj_guard = {0: 0, 1: 0}

                        def proj_block(ops_fn, n_pieces, evac_fn, first_fn=None):
                            for piece in range(n_pieces):
                                pb = piece % 2

                                @block.tensor
                                def _(t):
                                    if piece == 0:
                                        ctr.wait(t, "dve", w_ready[g])
                                        if first_fn is not None:
                                            first_fn(t)
                                    ctr.wait(t, "dve", prj_guard[pb])
                                    for dc in range(DC):
                                        out_ap, lhsT, rhs = ops_fn(piece, pb, dc)
                                        i = t.matmul(out_ap, lhsT, rhs,
                                                     start=(dc == 0),
                                                     stop=(dc == DC - 1))
                                    ctr.inc(i, "pe")
                                mm_done = ctr.c["pe"]

                                @block.vector
                                def _(v):
                                    ctr.wait(v, "pe", mm_done)
                                    i = evac_fn(v, piece, pb)
                                    prj_guard[pb] = ctr.inc(i, "dve")

                        def k_first(t):
                            ctr.wait(t, "dve")

                        def k_ops(piece, pb, dc):
                            return (prj[pb][:, 0:512],
                                    wsl[:, wb, 1, dc, :],
                                    xatt_kv[:, dc, piece * 512:(piece + 1) * 512])

                        def k_evac(v, piece, pb):
                            return v.tensor_copy(
                                kt[:, piece * 512:(piece + 1) * 512], prj[pb][:, 0:512])

                        proj_block(k_ops, S // 512, k_evac, first_fn=k_first)

                        @block.vector
                        def _(v):
                            i = v.memset(vgv[:, :, :, 64:65], 1.0)
                            ctr.inc(i, "dve")

                        def v_ops(kb, pb, dc):
                            return (prj[pb][:, 0:128],
                                    xatt_kv[:, dc, kb * 128:(kb + 1) * 128],
                                    wsl[:, wb, 2, dc, :])

                        def v_evac(v, kb, pb):
                            return v.tensor_copy(
                                vgv[:, kb, :, 0:64],
                                prj[pb][:, 0:128].rearrange("p (two c) -> p two c", two=2))

                        proj_block(v_ops, NKB, v_evac)

                        def q_ops(piece, pb, dc):
                            return (prj[pb][:, 0:512],
                                    wsl[:, wb, 0, dc, :],
                                    xatt_own[:, dc, piece * 512:(piece + 1) * 512])

                        def q_evac(v, piece, pb):
                            return v.tensor_copy(
                                qt[:, piece * 512:(piece + 1) * 512], prj[pb][:, 0:512])

                        proj_block(q_ops, T // 512, q_evac)

                        if g == 0:
                            sig_done = 0
                            for piece in range(T // 512):
                                @block.tensor
                                def _(t):
                                    ctr.wait(t, "dve", wgh_ld)
                                    ctr.wait(t, "act", sig_done)
                                    for dc in range(DC):
                                        i = t.matmul(gat[0:16, 0:512],
                                                     wgh_sb[:, dc, :],
                                                     xatt_own[:, dc, piece * 512:(piece + 1) * 512],
                                                     start=(dc == 0), stop=(dc == DC - 1))
                                    ctr.inc(i, "pe")
                                mm_done = ctr.c["pe"]

                                @block.scalar
                                def _(a):
                                    ctr.wait(a, "pe", mm_done)
                                    i = a.activation(
                                        out=g_heads[:, piece * 512:(piece + 1) * 512],
                                        in_=gat[0:16, 0:512], func=AF.Sigmoid)
                                    ctr.inc(i, "act")
                                sig_done = ctr.c["act"]

                        proj_done[g] = ctr.c["pe"]
                        if g < 7 and not (debug and g + 1 >= ngroups):
                            load_group_weights(g + 1)
                        all_proj = ctr.c["dve"]

                    with ExitStack() as _st5:
                        stf0 = _st5.enter_context(nc.psum_tensor(f"stf0_{g}", [128, 1024], F32))
                        stf1 = _st5.enter_context(nc.psum_tensor(f"stf1_{g}", [128, 1024], F32))
                        stfl = [stf0, stf1]
                        ot0 = _st5.enter_context(nc.psum_tensor(f"ot0_{g}", [128, 512], F32))
                        ot1 = _st5.enter_context(nc.psum_tensor(f"ot1_{g}", [128, 512], F32))
                        bcs = _st5.enter_context(nc.psum_tensor(f"bcs_{g}", [128, 512], F32))
                        for j in range(NSLOT):
                            C = 4 * (j + 1)
                            ng2 = C // 2
                            if j > 0:
                                @block.vector
                                def _(v):
                                    i = v.reciprocal(rnsb[64:65, 0:512],
                                                     otsb[64:65, (j - 1) % 2, 0:512])
                                    ctr.inc(i, "dve")
                                rn_done_by[j - 1] = ctr.c["dve"]
                            P0 = ctr.c["pe"]
                            A0 = ctr.c["act"]
                            M0 = ctr.c["dve"]
                            pb0 = pbuf_next
                            pbuf_next = (pb0 + ng2) % 4
                            entry_pv = dict(pv_by_pbuf)
                            entry_exp = dict(exp_by_stbuf)

                            def p_st(gi):
                                return P0 + 2 * gi + 1

                            def p_pv(gi):
                                return P0 + 2 * gi + 2

                            def a_exp2(gi):
                                return A0 + gi + 1

                            @block.tensor
                            def _(t):
                                for gi in range(ng2):
                                    sb2 = gi % 2
                                    if j == 0 and gi == 0:
                                        ctr.wait(t, "dve", all_proj)
                                        ctr.wait(t, "act", A0)
                                    if gi >= 2:
                                        ctr.wait(t, "act", a_exp2(gi - 2))
                                    elif entry_exp[sb2]:
                                        ctr.wait(t, "act", entry_exp[sb2])
                                    stf = stfl[sb2]
                                    for kb2 in range(2):
                                        kb = gi * 2 + kb2
                                        sl = slice(kb2 * 256, kb2 * 256 + 256)
                                        slb = slice(512 + kb2 * 256,
                                                    512 + kb2 * 256 + 256)
                                        t.matmul(stf[:, sl],
                                                 kt[0:64, kb * 128:(kb + 1) * 128],
                                                 qt[0:64, j * QW:(j + 1) * QW],
                                                 start=True, stop=True,
                                                 tile_position=(0, 0))
                                        i = t.matmul(stf[:, slb],
                                                     kt[64:128, kb * 128:(kb + 1) * 128],
                                                     qt[64:128, j * QW:(j + 1) * QW],
                                                     start=True, stop=True,
                                                     tile_position=(64, 0))
                                    ctr.inc(i, "pe")
                                    pbuf = (pb0 + gi) % 4
                                    if gi >= ng2 - 2:
                                        ctr.wait(t, "dve", M0 + (gi - (ng2 - 2)) + 1)
                                    else:
                                        ctr.wait(t, "act", a_exp2(gi))
                                    if gi == 0:
                                        ctr.wait(t, "dve", epi_done)
                                    for kb2 in range(2):
                                        kb = gi * 2 + kb2
                                        sl = slice(kb2 * 256, kb2 * 256 + 256)
                                        t.matmul(ot0[0:65, 0:256],
                                                 vg[:, kb * 130:kb * 130 + 65],
                                                 pt[:, pbuf, 0][:, sl],
                                                 start=(kb == 0), stop=(kb == C - 1))
                                        i = t.matmul(ot1[0:65, 0:256],
                                                     vg[:, kb * 130 + 65:kb * 130 + 130],
                                                     pt[:, pbuf, 1][:, sl],
                                                     start=(kb == 0), stop=(kb == C - 1))
                                    ctr.inc(i, "pe")

                            @block.scalar
                            def _(a):
                                for gi in range(ng2):
                                    sb2 = gi % 2
                                    pbuf = (pb0 + gi) % 4
                                    a.wait_ge(ctr.sems["pe"], p_st(gi))
                                    i = a.activation(
                                        out=pt[:, pbuf],
                                        in_=stfl[sb2][:, :].rearrange(
                                            "p (two q) -> p two q", two=2),
                                        func=AF.Exp, scale=0.125)
                                    ctr.inc(i, "act")

                            @block.vector
                            def _(v):
                                for mi in range(2):
                                    gi = ng2 - 2 + mi
                                    pbuf = (pb0 + gi) % 4
                                    v.wait_ge(ctr.sems["act"], a_exp2(gi))
                                    if j == 0:
                                        ctr.wait(v, "lsp", 32)
                                    msl = slice(mi * 512, mi * 512 + 512)
                                    v.tensor_mul(pt[:, pbuf, 0], pt[:, pbuf, 0],
                                                 masks_sb[:, msl])
                                    i = v.tensor_mul(pt[:, pbuf, 1], pt[:, pbuf, 1],
                                                     masks_sb[:, msl])
                                    ctr.inc(i, "dve")

                            for gi in range(ng2):
                                pv_by_pbuf[(pb0 + gi) % 4] = p_pv(gi)
                            exp_by_stbuf[0] = a_exp2(ng2 - 2)
                            exp_by_stbuf[1] = a_exp2(ng2 - 1)
                            assert ctr.c["pe"] == P0 + 2 * ng2
                            assert ctr.c["act"] == A0 + ng2
                            assert ctr.c["dve"] == M0 + 2
                            pv_last = ctr.c["pe"]

                            ob = j % 2

                            @block.vector
                            def _(v):
                                ctr.wait(v, "pe", pv_last)
                                ctr.wait(v, "dve", otsb_guard[ob])
                                v.tensor_copy(otsb[0:65, ob, 0:256], ot0[0:65, 0:256])
                                i = v.tensor_copy(otsb[0:65, ob, 256:512],
                                                  ot1[0:65, 0:256])
                                ctr.inc(i, "dve")
                            epi_done = ctr.c["dve"]

                            def emit_tail_rest(jj):
                                obb = jj % 2

                                @block.tensor
                                def _(t):
                                    ctr.wait(t, "dve", rn_done_by[jj])
                                    ctr.wait(t, "lsp", oneh_ld)
                                    t.matmul(bcs[64:65, 0:256],
                                             oneh[:, 2 * g:2 * g + 1],
                                             g_heads[:, jj * QW:(jj + 1) * QW],
                                             start=True, stop=True,
                                             tile_position=(0, 64))
                                    i = t.matmul(bcs[64:65, 256:512],
                                                 oneh[:, 2 * g + 1:2 * g + 2],
                                                 g_heads[:, jj * QW:(jj + 1) * QW],
                                                 start=True, stop=True,
                                                 tile_position=(0, 64))
                                    ctr.inc(i, "pe")
                                grow_done = ctr.c["pe"]

                                @block.vector
                                def _(v):
                                    ctr.wait(v, "pe", grow_done)
                                    i = v.tensor_mul(ssb[64:65, :], bcs[64:65, :],
                                                     rnsb[64:65, :])
                                    ctr.inc(i, "dve")
                                s_done = ctr.c["dve"]

                                @block.tensor
                                def _(t):
                                    ctr.wait(t, "dve", s_done)
                                    t.matmul(bcs[:, 0:256], ones65[64:65, :],
                                             ssb[64:65, 0:256], start=True, stop=True)
                                    i = t.matmul(bcs[:, 256:512], ones65[64:65, :],
                                                 ssb[64:65, 256:512],
                                                 start=True, stop=True)
                                    ctr.inc(i, "pe")
                                bcs_done = ctr.c["pe"]

                                @block.vector
                                def _(v):
                                    ctr.wait(v, "pe", bcs_done)
                                    if jj == 0:
                                        ctr.wait(v, "lsp", obst_dma)
                                    v.tensor_mul(oall[0:64, g, jj * QW:(jj + 1) * QW],
                                                 otsb[0:64, obb, 0:256],
                                                 bcs[0:64, 0:256])
                                    i = v.tensor_mul(obst[0:64, jj * QW:(jj + 1) * QW],
                                                     otsb[0:64, obb, 256:512],
                                                     bcs[0:64, 256:512])
                                    ctr.inc(i, "dve")
                                otsb_guard[obb] = ctr.c["dve"]

                            if j > 0:
                                emit_tail_rest(j - 1)
                            if j == NSLOT - 1:
                                @block.vector
                                def _(v):
                                    i = v.reciprocal(rnsb[64:65, 0:512],
                                                     otsb[64:65, ob, 0:512])
                                    ctr.inc(i, "dve")
                                rn_done_by[j] = ctr.c["dve"]
                                emit_tail_rest(j)
                            epi_apply = ctr.c["dve"]

                        @block.sync
                        def _(sp):
                            ctr.wait(sp, "dve", epi_apply)
                            i = sp.dma_start(out=oall[64:128, g, :], in_=obst[:])
                            ctr.inc(i, "lsp", 16)
                        obst_dma = ctr.c["lsp"]

                        if debug and g == 0:
                            @block.sync
                            def _(sp):
                                i = sp.dma_start(out=dbg_wsl[:], in_=wsl[:])
                                ctr.inc(i, "out", 16)
                                i = sp.dma_start(out=dbg_kt[:], in_=kt[:])
                                ctr.inc(i, "out", 16)
                                i = sp.dma_start(out=dbg_qt[:], in_=qt[:])
                                ctr.inc(i, "out", 16)
                                i = sp.dma_start(out=dbg_vg[:], in_=vg[:])
                                ctr.inc(i, "out", 16)
                                i = sp.dma_start(out=dbg_gh[:], in_=g_heads[:])
                                ctr.inc(i, "out", 16)

        with ExitStack() as _st6:
            pass
            if debug:
                @block.sync
                def _(sp):
                    ctr.wait(sp, "dve")
                    ctr.wait(sp, "lsp")
                    i = sp.dma_start(out=dbg_oall[:], in_=oall[:])
                    ctr.inc(i, "out", 16)

            with ExitStack() as _st7:
                wout_bf = _st7.enter_context(nc.sbuf_tensor("wout_bf", [128, DC, D], BF16))
                wistg = _st7.enter_context(nc.sbuf_tensor("wistg", [128, 2, HC, 128], I32))
                woutstg = _st7.enter_context(nc.sbuf_tensor("woutstg", [128, DC, D], F32))
                wdstg_b = _st7.enter_context(nc.sbuf_tensor("wdstg_b", [128, 2, HC, 128], BF16))
                x0b = _st7.enter_context(nc.sbuf_tensor("x0b", [128, 2, DC, P4_CH], F32))
                yf = _st7.enter_context(nc.sbuf_tensor("yf", [128, 2, DC, P4_CH], F32))
                ybf = _st7.enter_context(nc.sbuf_tensor("ybf", [128, 2, P4_CH], BF16))
                sqb = _st7.enter_context(nc.sbuf_tensor("sqb", [128, 2, P4_CH], BF16))
                x1b = _st7.enter_context(nc.sbuf_tensor("x1b", [128, DC, P4_CH], F32))
                wo0 = _st7.enter_context(nc.psum_tensor("wo0", [128, 256], F32))
                wo1 = _st7.enter_context(nc.psum_tensor("wo1", [128, 256], F32))
                st_s4 = _st7.enter_context(nc.psum_tensor("st_s4", [128, 512], F32))
                st_q4 = _st7.enter_context(nc.psum_tensor("st_q4", [128, 512], F32))
                bc4 = _st7.enter_context(nc.psum_tensor("bc4", [128, 512], F32))
                wo = [wo0, wo1]

                @block.sync
                def _(sp):
                    ctr.wait(sp, "dve")
                    ctr.wait(sp, "pe")
                    i = sp.dma_start(
                        out=woutstg[:],
                        in_=wout.rearrange("(dc p) c -> p dc c", p=128))
                    ctr.inc(i, "lgp", 16)
                wout_raw = ctr.c["lgp"]

                @block.vector
                def _(v):
                    ctr.wait(v, "lgp", wout_raw)
                    i = v.tensor_copy(wout_bf[:], woutstg[:])
                    ctr.inc(i, "dve")
                wout_ld = ctr.c["dve"]

                prep_ld = {}
                prep_cv = {}
                prep_st = {}

                def emit_w_prep(k):
                    par = k % 2
                    lds = f"ld{par}"
                    sds = f"sd{par}"
                    if k < 16:
                        w_src = wgT if k < 8 else wuT
                        w_dst = wgb if k < 8 else wub
                        dcx = k % 8
                        src_ap = w_src[dcx * 128:(dcx + 1) * 128, :]
                        dst_ap = w_dst[:, :, dcx, :]
                        stg_i = wistg[:, par, :, :].rearrange("p hc c -> p (hc c)")
                        stg_b = wdstg_b[:, par, :, :].rearrange(
                            "p hc c -> p (hcp c)", hcp=HC // 2) if False else                             wdstg_b[:, par, :, :].rearrange("p hc c -> p (hc c)").rearrange(
                                "p (hcp c) -> p hcp c", c=256)
                    else:
                        dcx = k - 16
                        src_ap = wdT[:, dcx * 128:(dcx + 1) * 128].rearrange(
                            "(hc p) c -> p hc c", p=128)
                        dst_ap = wdTb[:, dcx]
                        stg_i = wistg[:, par]
                        stg_b = wdstg_b[:, par]

                    @block.sync
                    def _(sp):
                        if k >= 2:
                            ctr.wait(sp, "act", prep_cv[k - 2])
                        i = sp.dma_start(out=stg_i, in_=src_ap)
                        prep_ld[k] = ctr.inc(i, lds, 16)

                    @block.scalar
                    def _(a):
                        ctr.wait(a, lds, prep_ld[k])
                        if k >= 2:
                            ctr.wait(a, sds, prep_st[k - 2])
                        i = a.activation(out=stg_b, in_=stg_i, func=AF.Copy)
                        prep_cv[k] = ctr.inc(i, "act")

                    @block.sync
                    def _(sp):
                        ctr.wait(sp, "act", prep_cv[k])
                        i = sp.dma_start(out=dst_ap, in_=stg_b)
                        prep_st[k] = ctr.inc(i, sds, 16)


                st4l = [st_s4, st_q4]
                evac_entry = {0: 0, 1: 0}
                stat_entry = {0: 0, 1: 0}
                yf_guard = {0: 0, 1: 0}
                x0_guard = {0: 0, 1: 0}
                x1_dma = [0]
                nch4 = T // P4_CH

                def emit_p4_tail(cc, p_apply_deps):
                    st4 = st4l[cc % 2]
                    rows_done = _ln_rows(block, ctr, rows, st4, st4[:, 256:512],
                                         P4_CH, 1.0, cbias[0:1, 1:2])
                    bc_done = _bcast_mr(block, ctr, bc4, ones_f, rows, P4_CH,
                                        rows_done)
                    x0_ld_cc, l4s_cc = p_apply_deps
                    t0c = cc * P4_CH

                    @block.vector
                    def _(v):
                        ctr.wait(v, "pe", bc_done)
                        ctr.wait(v, l4s_cc, x0_ld_cc)
                        ctr.wait(v, "lsp", x1_dma[0])
                        for dc in range(DC):
                            v.tensor_sub(yf[:, cc % 2, dc, :], yf[:, cc % 2, dc, :],
                                         bc4[:, 0:P4_CH])
                            v.tensor_mul(yf[:, cc % 2, dc, :], yf[:, cc % 2, dc, :],
                                         bc4[:, 256:256 + P4_CH])
                            i = v.tensor_add(x1b[:, dc, :], yf[:, cc % 2, dc, :],
                                             x0b[:, cc % 2, dc, :])
                        ctr.inc(i, "dve")
                    x1_done = ctr.c["dve"]
                    yf_guard[cc % 2] = x1_done
                    x0_guard[cc % 2] = x1_done

                    @block.sync
                    def _(sp):
                        ctr.wait(sp, "dve", x1_done)
                        i = sp.dma_start(
                            out=x1T[:, t0c:t0c + P4_CH].rearrange(
                                "(dc p) t -> p dc t", p=128),
                            in_=x1b[:])
                        ctr.inc(i, "lsp", 16)
                        if debug:
                            i = sp.dma_start(
                                out=dbg_x1[:, t0c:t0c + P4_CH].rearrange(
                                    "(dc p) t -> p dc t", p=128),
                                in_=x1b[:])
                            ctr.inc(i, "out", 16)
                    x1_dma[0] = ctr.c["lsp"]

                prev_deps = None
                for ch in range(nch4):
                    for k3 in range(3):
                        emit_w_prep(3 * ch + k3)
                    t0 = ch * P4_CH
                    xb = ch % 2
                    l4s = f"l4{xb}"

                    @block.sync
                    def _(sp):
                        if ch >= 2:
                            ctr.wait(sp, "dve", x0_guard[xb])
                        i = sp.dma_start(
                            out=x0b[:, xb],
                            in_=xT_own[:, t0:t0 + P4_CH].rearrange(
                                "(dc p) t -> p dc t", p=128))
                        ctr.inc(i, l4s, 16)
                    x0_ld = ctr.c[l4s]

                    Pw0 = ctr.c["pe"]
                    Aw0 = ctr.c["act"]
                    Dw0 = ctr.c["dve"]
                    st4 = st4l[xb]

                    @block.tensor
                    def _(t):
                        for dc in range(DC):
                            pb = dc % 2
                            if ch == 0 and dc == 0:
                                ctr.wait(t, "dve", wout_ld)
                                ctr.wait(t, "lsp")
                                ctr.wait(t, "dve")
                            if dc >= 2:
                                t.wait_ge(ctr.sems["act"], Aw0 + dc - 1)
                            elif evac_entry[pb]:
                                t.wait_ge(ctr.sems["act"], evac_entry[pb])
                            for ic in range(DC):
                                i = t.matmul(wo[pb][:, 0:P4_CH],
                                             wout_bf[:, ic, dc * 128:(dc + 1) * 128],
                                             oall[:, ic, t0:t0 + P4_CH],
                                             start=(ic == 0), stop=(ic == DC - 1))
                            ctr.inc(i, "pe")
                        for dc in range(DC):
                            pb = dc % 2
                            t.wait_ge(ctr.sems["dve"], Dw0 + dc + 1)
                            i = t.matmul(st4[0:1, 0:P4_CH], onec_b[:], ybf[:, pb],
                                         start=(dc == 0), stop=False)
                            i = t.matmul(st4[0:1, 256:256 + P4_CH], onec_b[:],
                                         sqb[:, pb],
                                         start=False, stop=(dc == DC - 1))
                            ctr.inc(i, "pe")

                    @block.scalar
                    def _(a):
                        for dc in range(DC):
                            pb = dc % 2
                            a.wait_ge(ctr.sems["pe"], Pw0 + dc + 1)
                            if dc == 0 and yf_guard[xb]:
                                a.wait_ge(ctr.sems["dve"], yf_guard[xb])
                            i = a.copy(yf[:, xb, dc, :], wo[pb][:, 0:P4_CH])
                            ctr.inc(i, "act")

                    @block.vector
                    def _(v):
                        for dc in range(DC):
                            pb = dc % 2
                            v.wait_ge(ctr.sems["act"], Aw0 + dc + 1)
                            if dc >= 2:
                                v.wait_ge(ctr.sems["pe"], Pw0 + 8 + dc - 1)
                            elif stat_entry[pb]:
                                v.wait_ge(ctr.sems["pe"], stat_entry[pb])
                            v.tensor_copy(ybf[:, pb], yf[:, xb, dc, :])
                            i = v.tensor_mul(sqb[:, pb], yf[:, xb, dc, :],
                                             yf[:, xb, dc, :])
                            ctr.inc(i, "dve")

                    evac_entry[0] = Aw0 + 7
                    evac_entry[1] = Aw0 + 8
                    stat_entry[0] = Pw0 + 8 + 7
                    stat_entry[1] = Pw0 + 8 + 8
                    assert ctr.c["pe"] == Pw0 + 16
                    assert ctr.c["act"] == Aw0 + 8
                    assert ctr.c["dve"] == Dw0 + 8

                    if prev_deps is not None:
                        emit_p4_tail(ch - 1, prev_deps)
                    prev_deps = (x0_ld, l4s)
                emit_p4_tail(nch4 - 1, prev_deps)
                x1_all = ctr.c["lsp"]

            with ExitStack() as _st8:
                wd_bf = _st8.enter_context(nc.sbuf_tensor("wd_bf", [128, 2, HC, 128], BF16))
                wgu = _st8.enter_context(nc.sbuf_tensor("wgu", [128, 2, 2, DC, 256], BF16))
                x1f = _st8.enter_context(nc.sbuf_tensor("x1f", [128, 2, DC, P5_CH], F32))
                x2n = _st8.enter_context(nc.sbuf_tensor("x2n", [128, DC, P5_CH], BF16))
                hbuf = _st8.enter_context(nc.sbuf_tensor("hbuf", [128, HC, P5_CH], BF16))
                silu_b = _st8.enter_context(nc.sbuf_tensor("silu_b", [128, 2, P5_CH], F32))
                yf5 = _st8.enter_context(nc.sbuf_tensor("yf5", [128, 2, DC, P5_CH], F32))
                bcsb5 = _st8.enter_context(nc.sbuf_tensor("bcsb5", [128, 512], F32))
                tb5 = _st8.enter_context(nc.sbuf_tensor("tb5", [128, P5_CH], F32))
                ybf5 = _st8.enter_context(nc.sbuf_tensor("ybf5", [128, 2, P5_CH], BF16))
                sqb5 = _st8.enter_context(nc.sbuf_tensor("sqb5", [128, 2, P5_CH], BF16))
                outb = _st8.enter_context(nc.sbuf_tensor("outb", [128, DC, P5_CH], F32))
                gu0a = _st8.enter_context(nc.psum_tensor("gu0a", [128, 512], F32))
                gu1a = _st8.enter_context(nc.psum_tensor("gu1a", [128, 512], F32))
                gu0b = _st8.enter_context(nc.psum_tensor("gu0b", [128, 512], F32))
                gu1b = _st8.enter_context(nc.psum_tensor("gu1b", [128, 512], F32))
                gu0l = [gu0a, gu0b]
                gu1l = [gu1a, gu1b]
                dn0 = _st8.enter_context(nc.psum_tensor("dn0", [128, 256], F32))
                dn1 = _st8.enter_context(nc.psum_tensor("dn1", [128, 256], F32))
                st5a = _st8.enter_context(nc.psum_tensor("st5a", [128, 512], F32))
                st5b = _st8.enter_context(nc.psum_tensor("st5b", [128, 512], F32))

                dn = [dn0, dn1]
                nch = T // P5_CH
                st5l = [st5a, st5b]
                wprep_s0 = 12 * 16
                wprep_s1 = 12 * 16
                wgu_guard = {0: 0, 1: 0}
                gu_entry = {0: 0, 1: 0}
                wgu_ld = {}
                stat_entry5 = {0: 0, 1: 0}
                dstat_entry5 = {0: 0, 1: 0}
                ev5_entry = {0: 0, 1: 0}
                wd_guard = {0: 0, 1: 0}
                x1f_guard = {0: 0, 1: 0}
                tail_guard = {0: 0, 1: 0}
                out_dma = [0]

                def emit_p5_tail(cc):
                    ccb = cc % 2
                    stX = st5l[ccb]
                    rows2 = _ln_rows(block, ctr, rows, stX, stX[:, 256:512],
                                     P5_CH, 1.0, cbias[0:1, 2:3])
                    bc2 = _bcast_mr(block, ctr, stX, ones_f, rows, P5_CH, rows2)
                    t0c = cc * P5_CH

                    @block.vector
                    def _(v):
                        ctr.wait(v, "pe", bc2)
                        ctr.wait(v, "out", out_dma[0])
                        v.tensor_copy(bcsb5[:], stX[:])
                        for dc in range(DC):
                            v.tensor_sub(yf5[:, ccb, dc, :], yf5[:, ccb, dc, :],
                                         bcsb5[:, 0:P5_CH])
                            v.tensor_mul(yf5[:, ccb, dc, :], yf5[:, ccb, dc, :],
                                         bcsb5[:, 256:256 + P5_CH])
                            i = v.tensor_add(outb[:, dc, :], yf5[:, ccb, dc, :],
                                             x1f[:, ccb, dc, :])
                        ctr.inc(i, "dve")
                    out_done = ctr.c["dve"]
                    x1f_guard[ccb] = out_done
                    tail_guard[ccb] = out_done

                    @block.sync
                    def _(sp):
                        ctr.wait(sp, "dve", out_done)
                        i = sp.dma_start(
                            out=outT[:, t0c:t0c + P5_CH].rearrange(
                                "(dc p) t -> p dc t", p=128),
                            in_=outb[:])
                        ctr.inc(i, "out", 16)
                    out_dma[0] = ctr.c["out"]

                for ch in range(nch):
                    t0 = ch * P5_CH
                    xb = ch % 2
                    st5 = st5l[xb]
                    l5s = f"l5{xb}"

                    @block.sync
                    def _(sp):
                        if ch == 0:
                            ctr.wait(sp, "lsp", x1_all)
                        ctr.wait(sp, "dve", x1f_guard[xb])
                        i = sp.dma_start(
                            out=x1f[:, xb],
                            in_=x1T[:, t0:t0 + P5_CH].rearrange(
                                "(dc p) t -> p dc t", p=128))
                        ctr.inc(i, l5s, 16)
                    x1_ld = ctr.c[l5s]

                    Dl0 = ctr.c["dve"]
                    Pl0 = ctr.c["pe"]

                    @block.vector
                    def _(v):
                        for dc in range(DC):
                            b2 = dc % 2
                            if dc == 0:
                                ctr.wait(v, l5s, x1_ld)
                            if dc >= 2:
                                v.wait_ge(ctr.sems["pe"], Pl0 + dc - 1)
                            elif stat_entry5[b2]:
                                v.wait_ge(ctr.sems["pe"], stat_entry5[b2])
                            v.tensor_copy(ybf5[:, b2], x1f[:, xb, dc, :])
                            i = v.tensor_mul(sqb5[:, b2], x1f[:, xb, dc, :],
                                             x1f[:, xb, dc, :])
                            ctr.inc(i, "dve")

                    @block.tensor
                    def _(t):
                        for dc in range(DC):
                            b2 = dc % 2
                            t.wait_ge(ctr.sems["dve"], Dl0 + dc + 1)
                            if dc == 0 and tail_guard[xb]:
                                t.wait_ge(ctr.sems["dve"], tail_guard[xb])
                            t.matmul(st5[0:1, 0:P5_CH], onec_b[:], ybf5[:, b2],
                                     start=(dc == 0), stop=False)
                            i = t.matmul(st5[0:1, 256:256 + P5_CH], onec_b[:],
                                         sqb5[:, b2],
                                         start=False, stop=(dc == DC - 1))
                            ctr.inc(i, "pe")
                    stat_entry5[0] = Pl0 + 7
                    stat_entry5[1] = Pl0 + 8

                    rows_done = _ln_rows(block, ctr, rows, st5, st5[:, 256:512],
                                         P5_CH, 1.0, cbias[0:1, 1:2])
                    bc_done = _bcast_mr(block, ctr, st5, ones_f, rows, P5_CH,
                                        rows_done)

                    @block.vector
                    def _(v):
                        ctr.wait(v, "pe", bc_done)
                        v.tensor_copy(bcsb5[:], st5[:])
                        for dc in range(DC):
                            v.tensor_sub(tb5[:], x1f[:, xb, dc, :],
                                         bcsb5[:, 0:P5_CH])
                            i = v.tensor_mul(x2n[:, dc, :], tb5[:],
                                             bcsb5[:, 256:256 + P5_CH])
                        ctr.inc(i, "dve")
                    x2n_done = ctr.c["dve"]
                    bcc_done = x2n_done

                    Pg0 = ctr.c["pe"]
                    Ag0 = ctr.c["act"]
                    Dg0 = ctr.c["dve"]

                    def g_gu(hc):
                        return Pg0 + hc + 1

                    def g_silu(hc):
                        return Ag0 + hc + 1

                    def g_h(hc):
                        return Dg0 + hc + 1

                    for hcp in range(HC // 2):
                        wpar = hcp % 2
                        lws5 = f"lw{wpar}"

                        @block.sync
                        def _(sp):
                            if ch == 0 and hcp < 2:
                                ctr.wait(sp, "sd0", wprep_s0)
                                ctr.wait(sp, "sd1", wprep_s1)
                            ctr.wait(sp, "pe", wgu_guard[wpar])
                            i = sp.dma_start(out=wgu[:, wpar, 0],
                                             in_=wgb[:, hcp])
                            ctr.inc(i, lws5, 16)
                            i = sp.dma_start(out=wgu[:, wpar, 1],
                                             in_=wub[:, hcp])
                            ctr.inc(i, lws5, 16)
                        wgu_ld[(ch, hcp)] = ctr.c[lws5]

                        @block.tensor
                        def _(t):
                            for hc in (2 * hcp, 2 * hcp + 1):
                                gb = hc % 2
                                cc = (hc % 2) * 128
                                if hc == 2 * hcp:
                                    ctr.wait(t, lws5, wgu_ld[(ch, hcp)])
                                if hc == 0:
                                    ctr.wait(t, "dve", x2n_done)
                                if hc >= 2:
                                    ctr.wait(t, "dve", g_h(hc - 2))
                                elif gu_entry[gb]:
                                    ctr.wait(t, "dve", gu_entry[gb])
                                for ic in range(DC):
                                    t.matmul(gu0l[gb][:, 0:P5_CH],
                                             wgu[:, wpar, 0, ic, cc:cc + 128],
                                             x2n[:, ic, :],
                                             start=(ic == 0), stop=(ic == DC - 1))
                                for ic in range(DC):
                                    i = t.matmul(gu1l[gb][:, 0:P5_CH],
                                                 wgu[:, wpar, 1, ic, cc:cc + 128],
                                                 x2n[:, ic, :],
                                                 start=(ic == 0), stop=(ic == DC - 1))
                                ctr.inc(i, "pe")
                        wgu_guard[wpar] = ctr.c["pe"]

                    @block.scalar
                    def _(a):
                        for hc in range(HC):
                            gb = hc % 2
                            a.wait_ge(ctr.sems["pe"], g_gu(hc))
                            if hc >= 2:
                                a.wait_ge(ctr.sems["dve"], g_h(hc - 2))
                            i = a.activation(out=silu_b[:, gb],
                                             in_=gu0l[gb][:, 0:P5_CH],
                                             func=AF.Silu, scale=DEQ)
                            ctr.inc(i, "act")

                    @block.vector
                    def _(v):
                        for hc in range(HC):
                            gb = hc % 2
                            v.wait_ge(ctr.sems["act"], g_silu(hc))
                            i = v.tensor_mul(hbuf[:, hc, :], silu_b[:, gb],
                                             gu1l[gb][:, 0:P5_CH])
                            ctr.inc(i, "dve")

                    gu_entry[0] = g_h(HC - 2)
                    gu_entry[1] = g_h(HC - 1)
                    assert ctr.c["pe"] == Pg0 + HC
                    assert ctr.c["act"] == Ag0 + HC
                    assert ctr.c["dve"] == Dg0 + HC
                    h_all = ctr.c["dve"]

                    if ch > 0:
                        emit_p5_tail(ch - 1)

                    Pd0 = ctr.c["pe"]
                    Ad0 = ctr.c["act"]
                    Dd0 = ctr.c["dve"]
                    for dc in range(DC):
                        pb = dc % 2
                        lds = f"ld{pb}"

                        @block.sync
                        def _(sp):
                            if ch == 0 and dc < 2:
                                ctr.wait(sp, "sd0", wprep_s0)
                                ctr.wait(sp, "sd1", wprep_s1)
                            ctr.wait(sp, "pe", wd_guard[pb])
                            i = sp.dma_start(out=wd_bf[:, pb], in_=wdTb[:, dc])
                            ctr.inc(i, lds, 16)
                        wd_ld = ctr.c[lds]

                        @block.tensor
                        def _(t):
                            ctr.wait(t, lds, wd_ld)
                            if dc == 0:
                                ctr.wait(t, "dve", h_all)
                                ctr.wait(t, "dve", bcc_done)
                            if dc >= 2:
                                t.wait_ge(ctr.sems["act"], Ad0 + dc - 1)
                            elif ev5_entry[pb]:
                                t.wait_ge(ctr.sems["act"], ev5_entry[pb])
                            for hc in range(HC):
                                i = t.matmul(dn[pb][:, 0:P5_CH],
                                             wd_bf[:, pb, hc, :],
                                             hbuf[:, hc, :],
                                             start=(hc == 0), stop=(hc == HC - 1))
                            ctr.inc(i, "pe")
                            t.wait_ge(ctr.sems["dve"], Dd0 + dc + 1)
                            t.matmul(st5[0:1, 0:P5_CH], onec_b[:], ybf5[:, pb],
                                     start=(dc == 0), stop=False)
                            i = t.matmul(st5[0:1, 256:256 + P5_CH], onec_b[:],
                                         sqb5[:, pb],
                                         start=False, stop=(dc == DC - 1))
                            ctr.inc(i, "pe")
                        wd_guard[pb] = Pd0 + 2 * dc + 1

                    @block.scalar
                    def _(a):
                        for dc in range(DC):
                            pb = dc % 2
                            a.wait_ge(ctr.sems["pe"], Pd0 + 2 * dc + 1)
                            if dc == 0 and tail_guard[xb]:
                                a.wait_ge(ctr.sems["dve"], tail_guard[xb])
                            i = a.copy(yf5[:, xb, dc, :], dn[pb][:, 0:P5_CH])
                            ctr.inc(i, "act")

                    @block.vector
                    def _(v):
                        for dc in range(DC):
                            pb = dc % 2
                            v.wait_ge(ctr.sems["act"], Ad0 + dc + 1)
                            if dc >= 2:
                                v.wait_ge(ctr.sems["pe"], Pd0 + 2 * (dc - 2) + 2)
                            elif dstat_entry5[pb]:
                                v.wait_ge(ctr.sems["pe"], dstat_entry5[pb])
                            v.tensor_copy(ybf5[:, pb], yf5[:, xb, dc, :])
                            i = v.tensor_mul(sqb5[:, pb], yf5[:, xb, dc, :],
                                             yf5[:, xb, dc, :])
                            ctr.inc(i, "dve")

                    ev5_entry[0] = Ad0 + 7
                    ev5_entry[1] = Ad0 + 8
                    dstat_entry5[0] = Pd0 + 2 * 6 + 2
                    dstat_entry5[1] = Pd0 + 2 * 7 + 2
                    stat_entry5[0] = Pd0 + 2 * 6 + 2
                    stat_entry5[1] = Pd0 + 2 * 7 + 2
                    assert ctr.c["pe"] == Pd0 + 16
                    assert ctr.c["act"] == Ad0 + 8
                    assert ctr.c["dve"] == Dd0 + 8
                emit_p5_tail(nch - 1)

                @block.sync
                def _(sp):
                    ctr.wait(sp, "out")

    ctr.close()
    return nc


def _host_prep(L0_attn_tokens, L0_attn_norm_w, L0_attn_Wq, L0_attn_Wkv,
               L0_attn_Wout, L0_attn_Wg, L0_ffn_w_gate, L0_ffn_w_up,
               L0_ffn_w_down):
    bf = ml_dtypes.bfloat16
    shared = {
        "wq": np.ascontiguousarray(L0_attn_Wq, dtype=np.float32),
        "wkv": np.ascontiguousarray(L0_attn_Wkv, dtype=np.float32),
        "wgh": np.ascontiguousarray(L0_attn_Wg, dtype=np.float32),
        "wout": np.ascontiguousarray(L0_attn_Wout, dtype=np.float32),
        "normw": np.ascontiguousarray(
            np.asarray(L0_attn_norm_w, dtype=np.float32).reshape(DC, 128).T),
        "wgT": np.ascontiguousarray(np.asarray(L0_ffn_w_gate, dtype=np.int32).T),
        "wuT": np.ascontiguousarray(np.asarray(L0_ffn_w_up, dtype=np.int32).T),
        "wdT": np.ascontiguousarray(np.asarray(L0_ffn_w_down, dtype=np.int32).T),
    }
    k_i = np.arange(128)[:, None]
    q_i = np.arange(QW)[None, :]
    t1 = (q_i >= k_i).astype(bf)
    t2 = (q_i >= 128 + k_i).astype(bf)
    ones = np.ones((128, QW), dtype=bf)
    zeros = np.zeros((128, QW), dtype=bf)
    shared["ident16"] = np.eye(H, dtype=bf)
    masks_by_par = {
        0: np.ascontiguousarray(np.stack([t1, t2, zeros, zeros])),
        1: np.ascontiguousarray(np.stack([ones, ones, t1, t2])),
    }
    toks = np.asarray(L0_attn_tokens, dtype=np.float32)
    own_cols_by_par = {
        par: np.concatenate([np.arange((2 * j + par) * QW, (2 * j + par + 1) * QW)
                             for j in range(NSLOT)])
        for par in range(2)
    }
    in_maps = []
    for c in range(8):
        seq, par = c // 2, c % 2
        xt = np.ascontiguousarray(toks[seq].T)
        m = dict(shared)
        m["xT_kv"] = xt
        m["xT_own"] = np.ascontiguousarray(xt[:, own_cols_by_par[par]])
        m["masks"] = masks_by_par[par]
        in_maps.append(m)
    return in_maps, own_cols_by_par


def kernel(**inputs):
    in_maps, own_cols_by_par = _host_prep(**inputs)
    nc = build_program()
    res = run_bass_kernel_spmd(nc, in_maps, core_ids=list(range(8)))
    out = np.empty((4, S, D), dtype=np.float32)
    for c in range(8):
        seq, par = c // 2, c % 2
        out[seq, own_cols_by_par[par], :] = res.results[c]["outT"].T
    return out


# version 13
# speedup vs baseline: 1.3993x; 1.1428x over previous
from contextlib import ExitStack
import numpy as np
import ml_dtypes

import concourse.bass as bass
import concourse.mybir as mybir
from concourse.bass_utils import run_bass_kernel_spmd

F32 = mybir.dt.float32
BF16 = mybir.dt.bfloat16
I32 = mybir.dt.int32
AF = mybir.ActivationFunctionType
ALU = mybir.AluOpType

D = 1024
H = 16
S = 4096
T = 2048
HID = 3072
DC = 8
HC = 24
NSLOT = 8
QW = 256
NKB = 32
EPS_L = 1e-5
EPS_R = float(np.finfo(np.float32).eps)
DEQ = 1.0 / 127.0
C2EPS = float((127.0 * 127.0) ** 2 * EPS_L)

P1_CH = 512
P4_CH = 256
P5_CH = 256


class Ctr:
    def __init__(self, nc, names):
        self.sems = {}
        self.c = {}
        self.guards = []
        for n in names:
            g = nc.semaphore(n)
            self.guards.append(g)
            self.sems[n] = g.__enter__()
            self.c[n] = 0

    def inc(self, instr, name, n=1):
        instr.then_inc(self.sems[name], n)
        self.c[name] += n
        return self.c[name]

    def wait(self, eng, name, val=None):
        v = self.c[name] if val is None else val
        if v > 0:
            eng.wait_ge(self.sems[name], v)

    def close(self):
        for g in reversed(self.guards):
            g.__exit__(None, None, None)


R_M, R_MSQ, R_MM, R_V, R_SD, R_R = (k * 512 for k in range(6))


def _ln_rows(block, ctr, rows, st_s, st_q, n, sqrt_scale, sqrt_bias):
    @block.vector
    def _(v):
        ctr.wait(v, "pe")
        v.tensor_scalar_mul(rows[0:1, R_M:R_M + n], st_s[0:1, :n], 1.0 / D)
        v.tensor_scalar_mul(rows[0:1, R_MSQ:R_MSQ + n], st_q[0:1, :n], 1.0 / D)
        v.tensor_mul(rows[0:1, R_MM:R_MM + n], rows[0:1, R_M:R_M + n],
                     rows[0:1, R_M:R_M + n])
        i = v.tensor_sub(rows[0:1, R_V:R_V + n], rows[0:1, R_MSQ:R_MSQ + n],
                         rows[0:1, R_MM:R_MM + n])
        ctr.inc(i, "dve")

    @block.scalar
    def _(a):
        ctr.wait(a, "dve")
        i = a.activation(out=rows[0:1, R_SD:R_SD + n], in_=rows[0:1, R_V:R_V + n],
                         func=AF.Sqrt, scale=sqrt_scale, bias=sqrt_bias)
        ctr.inc(i, "act")

    @block.vector
    def _(v):
        ctr.wait(v, "act")
        i = v.reciprocal(rows[0:1, R_R:R_R + n], rows[0:1, R_SD:R_SD + n])
        ctr.inc(i, "dve")
    return ctr.c["dve"]


def _bcast_mr(block, ctr, bc, ones_f, rows, n, rows_done):
    @block.tensor
    def _(t):
        ctr.wait(t, "dve", rows_done)
        t.matmul(bc[:, 0:n], ones_f[:], rows[0:1, R_M:R_M + n],
                 start=True, stop=True)
        i = t.matmul(bc[:, 256:256 + n], ones_f[:], rows[0:1, R_R:R_R + n],
                     start=True, stop=True)
        ctr.inc(i, "pe")
    return ctr.c["pe"]


def build_program(debug=False, ngroups=8):
    nc = bass.Bass("TRN2", target_bir_lowering=False)

    xT_own = nc.dram_tensor("xT_own", [D, T], F32, kind="ExternalInput").ap()
    xT_kv = nc.dram_tensor("xT_kv", [D, S], F32, kind="ExternalInput").ap()
    wq = nc.dram_tensor("wq", [D, D], F32, kind="ExternalInput").ap()
    wkv = nc.dram_tensor("wkv", [D, 2 * D], F32, kind="ExternalInput").ap()
    wgh = nc.dram_tensor("wgh", [D, H], F32, kind="ExternalInput").ap()
    wout = nc.dram_tensor("wout", [D, D], F32, kind="ExternalInput").ap()
    normw = nc.dram_tensor("normw", [128, DC], F32, kind="ExternalInput").ap()
    wgT = nc.dram_tensor("wgT", [D, HID], I32, kind="ExternalInput").ap()
    wuT = nc.dram_tensor("wuT", [D, HID], I32, kind="ExternalInput").ap()
    wdT = nc.dram_tensor("wdT", [HID, D], I32, kind="ExternalInput").ap()
    masks = nc.dram_tensor("masks", [4, 128, QW], BF16, kind="ExternalInput").ap()
    ident16 = nc.dram_tensor("ident16", [H, H], BF16, kind="ExternalInput").ap()
    outT = nc.dram_tensor("outT", [D, T], F32, kind="ExternalOutput").ap()
    x1T = nc.dram_tensor("x1T", [D, T], F32, kind="Internal").ap()
    wdTb = nc.dram_tensor("wdTb", [128, DC, HC, 128], BF16, kind="Internal").ap()
    wgb = nc.dram_tensor("wgb", [128, HC // 2, DC, 256], BF16, kind="Internal").ap()
    wub = nc.dram_tensor("wub", [128, HC // 2, DC, 256], BF16, kind="Internal").ap()
    if debug:
        dbg_xatt_own = nc.dram_tensor("dbg_xatt_own", [128, DC, T], BF16,
                                      kind="ExternalOutput").ap()
        dbg_xatt_kv = nc.dram_tensor("dbg_xatt_kv", [128, DC, S], BF16,
                                     kind="ExternalOutput").ap()
        dbg_kt = nc.dram_tensor("dbg_kt", [128, S], BF16, kind="ExternalOutput").ap()
        dbg_qt = nc.dram_tensor("dbg_qt", [128, T], BF16, kind="ExternalOutput").ap()
        dbg_vg = nc.dram_tensor("dbg_vg", [128, NKB * 130], BF16,
                                kind="ExternalOutput").ap()
        dbg_gh = nc.dram_tensor("dbg_gh", [H, T], BF16, kind="ExternalOutput").ap()
        dbg_wsl = nc.dram_tensor("dbg_wsl", [128, 2, 3, DC, 128], BF16,
                                 kind="ExternalOutput").ap()
        dbg_wsl2 = nc.dram_tensor("dbg_wsl2", [128, 2, 3, DC, 128], BF16,
                                  kind="ExternalOutput").ap()
        dbg_oall = nc.dram_tensor("dbg_oall", [128, DC, T], BF16,
                                  kind="ExternalOutput").ap()
        dbg_x1 = nc.dram_tensor("dbg_x1", [D, T], F32, kind="ExternalOutput").ap()

    ctr = Ctr(nc, ["lsp", "lgp", "pe", "act", "dve", "gp", "out",
                   "lw0", "lw1", "lx0", "lx1", "l40", "l41",
                   "l50", "l51", "ld0", "ld1", "sd0", "sd1"])

    with ExitStack() as _st0:
        ones_f = _st0.enter_context(nc.sbuf_tensor("ones_f", [1, 128], F32))
        onec_b = _st0.enter_context(nc.sbuf_tensor("onec_b", [128, 1], BF16))
        normw_sb = _st0.enter_context(nc.sbuf_tensor("normw_sb", [128, DC], F32))
        rows = _st0.enter_context(nc.sbuf_tensor("rows", [1, 6 * 512], F32))
        oall = _st0.enter_context(nc.sbuf_tensor("oall", [128, DC, T], BF16))
        masks_sb = _st0.enter_context(nc.sbuf_tensor("masks_sb", [128, 4 * QW], BF16))
        cbias = _st0.enter_context(nc.sbuf_tensor("cbias", [1, 4], F32))
        block = _st0.enter_context(nc.Block())
        @block.sync
        def _(sp):
            i = sp.dma_start(out=normw_sb[:], in_=normw[:])
            ctr.inc(i, "lsp", 16)
            i = sp.dma_start(
                out=masks_sb[:, :].rearrange("p (m q) -> p m q", m=4),
                in_=masks.rearrange("m p q -> p m q"))
            ctr.inc(i, "lsp", 16)

        @block.vector
        def _(v):
            v.memset(ones_f[:], 1.0)
            v.memset(onec_b[:], 1.0)
            v.memset(cbias[0:1, 0:1], EPS_R * EPS_L)
            v.memset(cbias[0:1, 1:2], EPS_L)
            i = v.memset(cbias[0:1, 2:3], C2EPS)
            ctr.inc(i, "dve")

        with ExitStack() as _st1:
            xatt_kv = _st1.enter_context(nc.sbuf_tensor("xatt_kv", [128, DC, S], BF16))
            xatt_own = _st1.enter_context(nc.sbuf_tensor("xatt_own", [128, DC, T], BF16))
            with ExitStack() as _st2:
                xraw = _st2.enter_context(nc.sbuf_tensor("xraw", [128, 2, DC, P1_CH], F32))
                xbf = _st2.enter_context(nc.sbuf_tensor("xbf", [128, 2, P1_CH], BF16))
                sqbf = _st2.enter_context(nc.sbuf_tensor("sqbf", [128, 2, P1_CH], BF16))
                tbuf = _st2.enter_context(nc.sbuf_tensor("tbuf", [128, P1_CH], F32))
                st_s = _st2.enter_context(nc.psum_tensor("st_s", [128, 512], F32))
                st_q = _st2.enter_context(nc.psum_tensor("st_q", [128, 512], F32))
                bc1 = _st2.enter_context(nc.psum_tensor("bc1", [128, 512], F32))
                chunks = [("kv", i) for i in range(S // P1_CH)] + [
                    ("own", i) for i in range(T // P1_CH)]
                apply_done = {}
                buf_guard = {0: 0, 1: 0}
                for ci, (kind, i0) in enumerate(chunks):
                    buf = ci % 2
                    src = xT_kv if kind == "kv" else xT_own
                    dst = xatt_kv if kind == "kv" else xatt_own
                    t0 = i0 * P1_CH

                    lxs = f"lx{buf}"

                    @block.sync
                    def _(sp):
                        if ci >= 2:
                            ctr.wait(sp, "dve", apply_done[ci - 2])
                        i = sp.dma_start(
                            out=xraw[:, buf],
                            in_=src[:, t0:t0 + P1_CH].rearrange(
                                "(dc p) t -> p dc t", p=128))
                        ctr.inc(i, lxs, 16)
                    ld_here = ctr.c[lxs]

                    for dc in range(DC):
                        b2 = dc % 2

                        @block.vector
                        def _(v):
                            ctr.wait(v, lxs, ld_here)
                            ctr.wait(v, "pe", buf_guard[b2])
                            v.tensor_copy(xbf[:, b2], xraw[:, buf, dc, :])
                            i = v.tensor_mul(sqbf[:, b2], xraw[:, buf, dc, :],
                                             xraw[:, buf, dc, :])
                            ctr.inc(i, "dve")
                        xb_done = ctr.c["dve"]

                        @block.tensor
                        def _(t):
                            ctr.wait(t, "dve", xb_done)
                            i = t.matmul(st_s[0:1, :], onec_b[:], xbf[:, b2],
                                         start=(dc == 0), stop=(dc == DC - 1))
                            i = t.matmul(st_q[0:1, :], onec_b[:], sqbf[:, b2],
                                         start=(dc == 0), stop=(dc == DC - 1))
                            ctr.inc(i, "pe")
                        buf_guard[b2] = ctr.c["pe"]

                    rows_done = _ln_rows(block, ctr, rows, st_s, st_q, P1_CH,
                                         1.0 + EPS_R, cbias[0:1, 0:1])

                    for half in range(P1_CH // 256):
                        hs = slice(half * 256, half * 256 + 256)

                        @block.tensor
                        def _(t):
                            if half == 0:
                                ctr.wait(t, "dve", rows_done)
                            else:
                                ctr.wait(t, "dve")
                            t.matmul(bc1[:, 0:256], ones_f[:],
                                     rows[0:1, R_M + half * 256:R_M + half * 256 + 256],
                                     start=True, stop=True)
                            i = t.matmul(bc1[:, 256:512], ones_f[:],
                                         rows[0:1, R_R + half * 256:R_R + half * 256 + 256],
                                         start=True, stop=True)
                            ctr.inc(i, "pe")
                        bc_now = ctr.c["pe"]

                        @block.vector
                        def _(v):
                            ctr.wait(v, "pe", bc_now)
                            for dc in range(DC):
                                v.tensor_sub(tbuf[:, 0:256],
                                             xraw[:, buf, dc, hs], bc1[:, 0:256])
                                i = v.scalar_tensor_tensor(
                                    dst[:, dc, t0 + half * 256:t0 + half * 256 + 256],
                                    tbuf[:, 0:256], normw_sb[:, dc:dc + 1],
                                    bc1[:, 256:512], ALU.mult, ALU.mult)
                            ctr.inc(i, "dve")
                        apply_done[ci] = ctr.c["dve"]

            if debug:
                @block.sync
                def _(sp):
                    ctr.wait(sp, "dve")
                    i = sp.dma_start(out=dbg_xatt_own[:], in_=xatt_own[:])
                    ctr.inc(i, "out", 16)
                    i = sp.dma_start(out=dbg_xatt_kv[:], in_=xatt_kv[:])
                    ctr.inc(i, "out", 16)

            with ExitStack() as _st3:
                g_heads = _st3.enter_context(nc.sbuf_tensor("g_heads", [H, T], BF16))
                kt = _st3.enter_context(nc.sbuf_tensor("kt", [128, S], BF16))
                vg = _st3.enter_context(nc.sbuf_tensor("vg", [128, NKB * 130], BF16))
                qt = _st3.enter_context(nc.sbuf_tensor("qt", [128, T], BF16))
                pt = _st3.enter_context(nc.sbuf_tensor("pt", [128, 4, 2, 512], BF16))
                wsl = _st3.enter_context(nc.sbuf_tensor("wsl", [128, 2, 3, DC, 128], BF16))
                wstg = _st3.enter_context(nc.sbuf_tensor("wstg", [128, DC, 128], F32))
                wgh_sb = _st3.enter_context(nc.sbuf_tensor("wgh_sb", [128, DC, H], BF16))
                wghstg = _st3.enter_context(nc.sbuf_tensor("wghstg", [128, DC, H], F32))
                obst = _st3.enter_context(nc.sbuf_tensor("obst", [64, T], BF16))
                rnsb = _st3.enter_context(nc.sbuf_tensor("rnsb", [65, 512], F32))
                ssb = _st3.enter_context(nc.sbuf_tensor("ssb", [65, 512], F32))
                otsb = _st3.enter_context(nc.sbuf_tensor("otsb", [65, 2, 512], F32))
                ones65 = _st3.enter_context(nc.sbuf_tensor("ones65", [65, 128], F32))
                oneh = _st3.enter_context(nc.sbuf_tensor("oneh", [16, 16], BF16))
                bcs_sb = _st3.enter_context(nc.sbuf_tensor("bcs_sb", [128, 512], F32))
                vgv = vg[:, :].rearrange("p (kb two c) -> p kb two c", two=2, c=65)

                @block.vector
                def _(v):
                    i = v.memset(ones65[:], 1.0)
                    ctr.inc(i, "dve")

                @block.sync
                def _(sp):
                    i = sp.dma_start(out=oneh[:], in_=ident16[:])
                    ctr.inc(i, "lsp", 16)
                oneh_ld = ctr.c["lsp"]

                proj_done = {}
                w_ready = {}

                wcvt_done = {}

                def load_group_weights(g):
                    lws = f"lw{g % 2}"
                    srcs = [wq[:, g * 128:(g + 1) * 128],
                            wkv[:, g * 128:(g + 1) * 128],
                            wkv[:, D + g * 128:D + (g + 1) * 128]]
                    for s3 in range(3):
                        @block.sync
                        def _(sp):
                            prev = wcvt_done.get((g, s3 - 1),
                                                 wcvt_done.get((g - 1, 2), 0))
                            ctr.wait(sp, "dve", prev)
                            i = sp.dma_start(
                                out=wstg[:],
                                in_=srcs[s3].rearrange("(dc p) c -> p dc c", p=128))
                            ctr.inc(i, lws, 16)
                        ld = ctr.c[lws]

                        @block.vector
                        def _(v):
                            ctr.wait(v, lws, ld)
                            if g >= 2 and s3 == 0:
                                ctr.wait(v, "pe", proj_done[g - 2])
                            i = v.tensor_copy(wsl[:, g % 2, s3], wstg[:])
                            wcvt_done[(g, s3)] = ctr.inc(i, "dve")
                    w_ready[g] = wcvt_done[(g, 2)]

                load_group_weights(0)
                if debug:
                    @block.sync
                    def _(sp):
                        ctr.wait(sp, "dve", w_ready[0])
                        i = sp.dma_start(out=dbg_wsl2[:], in_=wsl[:])
                        ctr.inc(i, "out", 16)

                @block.sync
                def _(sp):
                    i = sp.dma_start(
                        out=wghstg[:],
                        in_=wgh.rearrange("(dc p) h -> p dc h", p=128))
                    ctr.inc(i, "lgp", 16)
                wgh_raw = ctr.c["lgp"]

                @block.vector
                def _(v):
                    ctr.wait(v, "lgp", wgh_raw)
                    i = v.tensor_copy(wgh_sb[:], wghstg[:])
                    ctr.inc(i, "dve")
                wgh_ld = ctr.c["dve"]

                obst_dma = 0
                epi_done = 0
                otsb_guard = {0: 0, 1: 0}
                rn_done_by = {}
                prev_exp = 0
                exp_by_stbuf = {0: 0, 1: 0}
                pv_by_pbuf = {0: 0, 1: 0, 2: 0, 3: 0}
                pbuf_next = 0
                for g in range(ngroups):
                    wb = g % 2

                    with ExitStack() as _st4:
                        prj0 = _st4.enter_context(nc.psum_tensor(f"prj0_{g}", [128, 512], F32))
                        prj1 = _st4.enter_context(nc.psum_tensor(f"prj1_{g}", [128, 512], F32))
                        gat = _st4.enter_context(nc.psum_tensor(f"gat_{g}", [128, 512], F32))
                        prj = [prj0, prj1]
                        prj_guard = {0: 0, 1: 0}

                        def proj_block(ops_fn, n_pieces, evac_fn, first_fn=None):
                            for piece in range(n_pieces):
                                pb = piece % 2

                                @block.tensor
                                def _(t):
                                    if piece == 0:
                                        ctr.wait(t, "dve", w_ready[g])
                                        if first_fn is not None:
                                            first_fn(t)
                                    ctr.wait(t, "dve", prj_guard[pb])
                                    for dc in range(DC):
                                        out_ap, lhsT, rhs = ops_fn(piece, pb, dc)
                                        i = t.matmul(out_ap, lhsT, rhs,
                                                     start=(dc == 0),
                                                     stop=(dc == DC - 1))
                                    ctr.inc(i, "pe")
                                mm_done = ctr.c["pe"]

                                @block.vector
                                def _(v):
                                    ctr.wait(v, "pe", mm_done)
                                    i = evac_fn(v, piece, pb)
                                    prj_guard[pb] = ctr.inc(i, "dve")

                        def k_first(t):
                            ctr.wait(t, "dve")

                        def k_ops(piece, pb, dc):
                            return (prj[pb][:, 0:512],
                                    wsl[:, wb, 1, dc, :],
                                    xatt_kv[:, dc, piece * 512:(piece + 1) * 512])

                        def k_evac(v, piece, pb):
                            return v.tensor_copy(
                                kt[:, piece * 512:(piece + 1) * 512], prj[pb][:, 0:512])

                        proj_block(k_ops, S // 512, k_evac, first_fn=k_first)

                        @block.vector
                        def _(v):
                            i = v.memset(vgv[:, :, :, 64:65], 1.0)
                            ctr.inc(i, "dve")

                        def v_ops(kb, pb, dc):
                            return (prj[pb][:, 0:128],
                                    xatt_kv[:, dc, kb * 128:(kb + 1) * 128],
                                    wsl[:, wb, 2, dc, :])

                        def v_evac(v, kb, pb):
                            return v.tensor_copy(
                                vgv[:, kb, :, 0:64],
                                prj[pb][:, 0:128].rearrange("p (two c) -> p two c", two=2))

                        proj_block(v_ops, NKB, v_evac)

                        def q_ops(piece, pb, dc):
                            return (prj[pb][:, 0:512],
                                    wsl[:, wb, 0, dc, :],
                                    xatt_own[:, dc, piece * 512:(piece + 1) * 512])

                        def q_evac(v, piece, pb):
                            return v.tensor_copy(
                                qt[:, piece * 512:(piece + 1) * 512], prj[pb][:, 0:512])

                        proj_block(q_ops, T // 512, q_evac)

                        if g == 0:
                            sig_done = 0
                            for piece in range(T // 512):
                                @block.tensor
                                def _(t):
                                    ctr.wait(t, "dve", wgh_ld)
                                    ctr.wait(t, "act", sig_done)
                                    for dc in range(DC):
                                        i = t.matmul(gat[0:16, 0:512],
                                                     wgh_sb[:, dc, :],
                                                     xatt_own[:, dc, piece * 512:(piece + 1) * 512],
                                                     start=(dc == 0), stop=(dc == DC - 1))
                                    ctr.inc(i, "pe")
                                mm_done = ctr.c["pe"]

                                @block.scalar
                                def _(a):
                                    ctr.wait(a, "pe", mm_done)
                                    i = a.activation(
                                        out=g_heads[:, piece * 512:(piece + 1) * 512],
                                        in_=gat[0:16, 0:512], func=AF.Sigmoid)
                                    ctr.inc(i, "act")
                                sig_done = ctr.c["act"]

                        proj_done[g] = ctr.c["pe"]
                        if g < 7 and not (debug and g + 1 >= ngroups):
                            load_group_weights(g + 1)
                        all_proj = ctr.c["dve"]

                    with ExitStack() as _st5:
                        stf0 = _st5.enter_context(nc.psum_tensor(f"stf0_{g}", [128, 1024], F32))
                        stf1 = _st5.enter_context(nc.psum_tensor(f"stf1_{g}", [128, 1024], F32))
                        stfl = [stf0, stf1]
                        ot0 = _st5.enter_context(nc.psum_tensor(f"ot0_{g}", [128, 512], F32))
                        ot1 = _st5.enter_context(nc.psum_tensor(f"ot1_{g}", [128, 512], F32))
                        bcs = _st5.enter_context(nc.psum_tensor(f"bcs_{g}", [128, 512], F32))
                        for j in range(NSLOT):
                            C = 4 * (j + 1)
                            ng2 = C // 2
                            if j > 0:
                                @block.vector
                                def _(v):
                                    i = v.reciprocal(rnsb[64:65, 0:512],
                                                     otsb[64:65, (j - 1) % 2, 0:512])
                                    ctr.inc(i, "dve")
                                rn_done_by[j - 1] = ctr.c["dve"]
                            P0 = ctr.c["pe"]
                            A0 = ctr.c["act"]
                            M0 = ctr.c["dve"]
                            pb0 = pbuf_next
                            pbuf_next = (pb0 + ng2) % 4
                            entry_pv = dict(pv_by_pbuf)
                            entry_exp = dict(exp_by_stbuf)

                            def p_st(gi):
                                return P0 + 4 * (gi // 2) + (gi % 2) + 1

                            def p_pv(gi):
                                return P0 + 4 * (gi // 2) + (gi % 2) + 3

                            def a_exp2(gi):
                                return A0 + gi + 1

                            @block.tensor
                            def _(t):
                                for gp2 in range(ng2 // 2):
                                    for gi in (2 * gp2, 2 * gp2 + 1):
                                        sb2 = gi % 2
                                        if j == 0 and gi == 0:
                                            ctr.wait(t, "dve", all_proj)
                                            ctr.wait(t, "act", A0)
                                        if gi >= 2:
                                            ctr.wait(t, "act", a_exp2(gi - 2))
                                        elif entry_exp[sb2]:
                                            ctr.wait(t, "act", entry_exp[sb2])
                                        stf = stfl[sb2]
                                        for kb2 in range(2):
                                            kb = gi * 2 + kb2
                                            sl = slice(kb2 * 256, kb2 * 256 + 256)
                                            slb = slice(512 + kb2 * 256,
                                                        512 + kb2 * 256 + 256)
                                            t.matmul(stf[:, sl],
                                                     kt[0:64, kb * 128:(kb + 1) * 128],
                                                     qt[0:64, j * QW:(j + 1) * QW],
                                                     start=True, stop=True,
                                                     tile_position=(0, 0))
                                            i = t.matmul(
                                                stf[:, slb],
                                                kt[64:128, kb * 128:(kb + 1) * 128],
                                                qt[64:128, j * QW:(j + 1) * QW],
                                                start=True, stop=True,
                                                tile_position=(64, 0))
                                        ctr.inc(i, "pe")
                                    for gi in (2 * gp2, 2 * gp2 + 1):
                                        pbuf = (pb0 + gi) % 4
                                        if gi >= ng2 - 2:
                                            ctr.wait(t, "dve",
                                                     M0 + (gi - (ng2 - 2)) + 1)
                                        else:
                                            ctr.wait(t, "act", a_exp2(gi))
                                        if gi == 0:
                                            ctr.wait(t, "dve", epi_done)
                                        for kb2 in range(2):
                                            kb = gi * 2 + kb2
                                            sl = slice(kb2 * 256, kb2 * 256 + 256)
                                            t.matmul(ot0[0:65, 0:256],
                                                     vg[:, kb * 130:kb * 130 + 65],
                                                     pt[:, pbuf, 0][:, sl],
                                                     start=(kb == 0),
                                                     stop=(kb == C - 1))
                                            i = t.matmul(
                                                ot1[0:65, 0:256],
                                                vg[:, kb * 130 + 65:kb * 130 + 130],
                                                pt[:, pbuf, 1][:, sl],
                                                start=(kb == 0),
                                                stop=(kb == C - 1))
                                        ctr.inc(i, "pe")

                            @block.scalar
                            def _(a):
                                for gp2 in range(ng2 // 2):
                                    a.wait_ge(ctr.sems["pe"], p_st(2 * gp2 + 1))
                                    for gi in (2 * gp2, 2 * gp2 + 1):
                                        sb2 = gi % 2
                                        pbuf = (pb0 + gi) % 4
                                        i = a.activation(
                                            out=pt[:, pbuf],
                                            in_=stfl[sb2][:, :].rearrange(
                                                "p (two q) -> p two q", two=2),
                                            func=AF.Exp, scale=0.125)
                                        ctr.inc(i, "act")

                            @block.vector
                            def _(v):
                                for mi in range(2):
                                    gi = ng2 - 2 + mi
                                    pbuf = (pb0 + gi) % 4
                                    v.wait_ge(ctr.sems["act"], a_exp2(gi))
                                    if j == 0:
                                        ctr.wait(v, "lsp", 32)
                                    msl = slice(mi * 512, mi * 512 + 512)
                                    v.tensor_mul(pt[:, pbuf, 0], pt[:, pbuf, 0],
                                                 masks_sb[:, msl])
                                    i = v.tensor_mul(pt[:, pbuf, 1], pt[:, pbuf, 1],
                                                     masks_sb[:, msl])
                                    ctr.inc(i, "dve")

                            for gi in range(ng2):
                                pv_by_pbuf[(pb0 + gi) % 4] = p_pv(gi)
                            exp_by_stbuf[0] = a_exp2(ng2 - 2)
                            exp_by_stbuf[1] = a_exp2(ng2 - 1)
                            assert ctr.c["pe"] == P0 + 2 * ng2
                            assert ctr.c["act"] == A0 + ng2
                            assert ctr.c["dve"] == M0 + 2
                            pv_last = ctr.c["pe"]

                            ob = j % 2

                            @block.vector
                            def _(v):
                                ctr.wait(v, "pe", pv_last)
                                ctr.wait(v, "dve", otsb_guard[ob])
                                v.tensor_copy(otsb[0:65, ob, 0:256], ot0[0:65, 0:256])
                                i = v.tensor_copy(otsb[0:65, ob, 256:512],
                                                  ot1[0:65, 0:256])
                                ctr.inc(i, "dve")
                            epi_done = ctr.c["dve"]

                            def emit_tail_rest(jj):
                                obb = jj % 2

                                @block.tensor
                                def _(t):
                                    ctr.wait(t, "dve", rn_done_by[jj])
                                    ctr.wait(t, "lsp", oneh_ld)
                                    t.matmul(bcs[64:65, 0:256],
                                             oneh[:, 2 * g:2 * g + 1],
                                             g_heads[:, jj * QW:(jj + 1) * QW],
                                             start=True, stop=True,
                                             tile_position=(0, 64))
                                    i = t.matmul(bcs[64:65, 256:512],
                                                 oneh[:, 2 * g + 1:2 * g + 2],
                                                 g_heads[:, jj * QW:(jj + 1) * QW],
                                                 start=True, stop=True,
                                                 tile_position=(0, 64))
                                    ctr.inc(i, "pe")
                                grow_done = ctr.c["pe"]

                                @block.vector
                                def _(v):
                                    ctr.wait(v, "pe", grow_done)
                                    i = v.tensor_mul(ssb[64:65, :], bcs[64:65, :],
                                                     rnsb[64:65, :])
                                    ctr.inc(i, "dve")
                                s_done = ctr.c["dve"]

                                @block.tensor
                                def _(t):
                                    ctr.wait(t, "dve", s_done)
                                    t.matmul(bcs[:, 0:256], ones65[64:65, :],
                                             ssb[64:65, 0:256], start=True, stop=True)
                                    i = t.matmul(bcs[:, 256:512], ones65[64:65, :],
                                                 ssb[64:65, 256:512],
                                                 start=True, stop=True)
                                    ctr.inc(i, "pe")
                                bcs_done = ctr.c["pe"]

                                @block.vector
                                def _(v):
                                    ctr.wait(v, "pe", bcs_done)
                                    if jj == 0:
                                        ctr.wait(v, "lsp", obst_dma)
                                    v.tensor_mul(oall[0:64, g, jj * QW:(jj + 1) * QW],
                                                 otsb[0:64, obb, 0:256],
                                                 bcs[0:64, 0:256])
                                    i = v.tensor_mul(obst[0:64, jj * QW:(jj + 1) * QW],
                                                     otsb[0:64, obb, 256:512],
                                                     bcs[0:64, 256:512])
                                    ctr.inc(i, "dve")
                                otsb_guard[obb] = ctr.c["dve"]

                            if j > 0:
                                emit_tail_rest(j - 1)
                            if j == NSLOT - 1:
                                @block.vector
                                def _(v):
                                    i = v.reciprocal(rnsb[64:65, 0:512],
                                                     otsb[64:65, ob, 0:512])
                                    ctr.inc(i, "dve")
                                rn_done_by[j] = ctr.c["dve"]
                                emit_tail_rest(j)
                            epi_apply = ctr.c["dve"]

                        @block.sync
                        def _(sp):
                            ctr.wait(sp, "dve", epi_apply)
                            i = sp.dma_start(out=oall[64:128, g, :], in_=obst[:])
                            ctr.inc(i, "lsp", 16)
                        obst_dma = ctr.c["lsp"]

                        if debug and g == 0:
                            @block.sync
                            def _(sp):
                                i = sp.dma_start(out=dbg_wsl[:], in_=wsl[:])
                                ctr.inc(i, "out", 16)
                                i = sp.dma_start(out=dbg_kt[:], in_=kt[:])
                                ctr.inc(i, "out", 16)
                                i = sp.dma_start(out=dbg_qt[:], in_=qt[:])
                                ctr.inc(i, "out", 16)
                                i = sp.dma_start(out=dbg_vg[:], in_=vg[:])
                                ctr.inc(i, "out", 16)
                                i = sp.dma_start(out=dbg_gh[:], in_=g_heads[:])
                                ctr.inc(i, "out", 16)

        with ExitStack() as _st6:
            pass
            if debug:
                @block.sync
                def _(sp):
                    ctr.wait(sp, "dve")
                    ctr.wait(sp, "lsp")
                    i = sp.dma_start(out=dbg_oall[:], in_=oall[:])
                    ctr.inc(i, "out", 16)

            with ExitStack() as _st7:
                wout_bf = _st7.enter_context(nc.sbuf_tensor("wout_bf", [128, DC, D], BF16))
                wistg = _st7.enter_context(nc.sbuf_tensor("wistg", [128, 2, HC, 128], I32))
                woutstg = _st7.enter_context(nc.sbuf_tensor("woutstg", [128, DC, D], F32))
                wdstg_b = _st7.enter_context(nc.sbuf_tensor("wdstg_b", [128, 2, HC, 128], BF16))
                x0b = _st7.enter_context(nc.sbuf_tensor("x0b", [128, 2, DC, P4_CH], F32))
                yf = _st7.enter_context(nc.sbuf_tensor("yf", [128, 2, DC, P4_CH], F32))
                ybf = _st7.enter_context(nc.sbuf_tensor("ybf", [128, 2, P4_CH], BF16))
                sqb = _st7.enter_context(nc.sbuf_tensor("sqb", [128, 2, P4_CH], BF16))
                x1b = _st7.enter_context(nc.sbuf_tensor("x1b", [128, DC, P4_CH], F32))
                wo0 = _st7.enter_context(nc.psum_tensor("wo0", [128, 256], F32))
                wo1 = _st7.enter_context(nc.psum_tensor("wo1", [128, 256], F32))
                st_s4 = _st7.enter_context(nc.psum_tensor("st_s4", [128, 512], F32))
                st_q4 = _st7.enter_context(nc.psum_tensor("st_q4", [128, 512], F32))
                bc4 = _st7.enter_context(nc.psum_tensor("bc4", [128, 512], F32))
                wo = [wo0, wo1]

                @block.sync
                def _(sp):
                    ctr.wait(sp, "dve")
                    ctr.wait(sp, "pe")
                    i = sp.dma_start(
                        out=woutstg[:],
                        in_=wout.rearrange("(dc p) c -> p dc c", p=128))
                    ctr.inc(i, "lgp", 16)
                wout_raw = ctr.c["lgp"]

                @block.vector
                def _(v):
                    ctr.wait(v, "lgp", wout_raw)
                    i = v.tensor_copy(wout_bf[:], woutstg[:])
                    ctr.inc(i, "dve")
                wout_ld = ctr.c["dve"]

                prep_ld = {}
                prep_cv = {}
                prep_st = {}

                def emit_w_prep(k):
                    par = k % 2
                    lds = f"ld{par}"
                    sds = f"sd{par}"
                    if k < 16:
                        w_src = wgT if k < 8 else wuT
                        w_dst = wgb if k < 8 else wub
                        dcx = k % 8
                        src_ap = w_src[dcx * 128:(dcx + 1) * 128, :]
                        dst_ap = w_dst[:, :, dcx, :]
                        stg_i = wistg[:, par, :, :].rearrange("p hc c -> p (hc c)")
                        stg_b = wdstg_b[:, par, :, :].rearrange(
                            "p hc c -> p (hcp c)", hcp=HC // 2) if False else                             wdstg_b[:, par, :, :].rearrange("p hc c -> p (hc c)").rearrange(
                                "p (hcp c) -> p hcp c", c=256)
                    else:
                        dcx = k - 16
                        src_ap = wdT[:, dcx * 128:(dcx + 1) * 128].rearrange(
                            "(hc p) c -> p hc c", p=128)
                        dst_ap = wdTb[:, dcx]
                        stg_i = wistg[:, par]
                        stg_b = wdstg_b[:, par]

                    @block.sync
                    def _(sp):
                        if k >= 2:
                            ctr.wait(sp, "act", prep_cv[k - 2])
                        i = sp.dma_start(out=stg_i, in_=src_ap)
                        prep_ld[k] = ctr.inc(i, lds, 16)

                    @block.scalar
                    def _(a):
                        ctr.wait(a, lds, prep_ld[k])
                        if k >= 2:
                            ctr.wait(a, sds, prep_st[k - 2])
                        i = a.activation(out=stg_b, in_=stg_i, func=AF.Copy)
                        prep_cv[k] = ctr.inc(i, "act")

                    @block.sync
                    def _(sp):
                        ctr.wait(sp, "act", prep_cv[k])
                        i = sp.dma_start(out=dst_ap, in_=stg_b)
                        prep_st[k] = ctr.inc(i, sds, 16)


                st4l = [st_s4, st_q4]
                evac_entry = {0: 0, 1: 0}
                stat_entry = {0: 0, 1: 0}
                yf_guard = {0: 0, 1: 0}
                x0_guard = {0: 0, 1: 0}
                x1_dma = [0]
                nch4 = T // P4_CH

                def emit_p4_tail(cc, p_apply_deps):
                    st4 = st4l[cc % 2]
                    rows_done = _ln_rows(block, ctr, rows, st4, st4[:, 256:512],
                                         P4_CH, 1.0, cbias[0:1, 1:2])
                    bc_done = _bcast_mr(block, ctr, bc4, ones_f, rows, P4_CH,
                                        rows_done)
                    x0_ld_cc, l4s_cc = p_apply_deps
                    t0c = cc * P4_CH

                    @block.vector
                    def _(v):
                        ctr.wait(v, "pe", bc_done)
                        ctr.wait(v, l4s_cc, x0_ld_cc)
                        ctr.wait(v, "lsp", x1_dma[0])
                        for dc in range(DC):
                            v.tensor_sub(yf[:, cc % 2, dc, :], yf[:, cc % 2, dc, :],
                                         bc4[:, 0:P4_CH])
                            v.tensor_mul(yf[:, cc % 2, dc, :], yf[:, cc % 2, dc, :],
                                         bc4[:, 256:256 + P4_CH])
                            i = v.tensor_add(x1b[:, dc, :], yf[:, cc % 2, dc, :],
                                             x0b[:, cc % 2, dc, :])
                        ctr.inc(i, "dve")
                    x1_done = ctr.c["dve"]
                    yf_guard[cc % 2] = x1_done
                    x0_guard[cc % 2] = x1_done

                    @block.sync
                    def _(sp):
                        ctr.wait(sp, "dve", x1_done)
                        i = sp.dma_start(
                            out=x1T[:, t0c:t0c + P4_CH].rearrange(
                                "(dc p) t -> p dc t", p=128),
                            in_=x1b[:])
                        ctr.inc(i, "lsp", 16)
                        if debug:
                            i = sp.dma_start(
                                out=dbg_x1[:, t0c:t0c + P4_CH].rearrange(
                                    "(dc p) t -> p dc t", p=128),
                                in_=x1b[:])
                            ctr.inc(i, "out", 16)
                    x1_dma[0] = ctr.c["lsp"]

                prev_deps = None
                for ch in range(nch4):
                    for k3 in range(3):
                        emit_w_prep(3 * ch + k3)
                    t0 = ch * P4_CH
                    xb = ch % 2
                    l4s = f"l4{xb}"

                    @block.sync
                    def _(sp):
                        if ch >= 2:
                            ctr.wait(sp, "dve", x0_guard[xb])
                        i = sp.dma_start(
                            out=x0b[:, xb],
                            in_=xT_own[:, t0:t0 + P4_CH].rearrange(
                                "(dc p) t -> p dc t", p=128))
                        ctr.inc(i, l4s, 16)
                    x0_ld = ctr.c[l4s]

                    Pw0 = ctr.c["pe"]
                    Aw0 = ctr.c["act"]
                    Dw0 = ctr.c["dve"]
                    st4 = st4l[xb]

                    @block.tensor
                    def _(t):
                        for dc in range(DC):
                            pb = dc % 2
                            if ch == 0 and dc == 0:
                                ctr.wait(t, "dve", wout_ld)
                                ctr.wait(t, "lsp")
                                ctr.wait(t, "dve")
                            if dc >= 2:
                                t.wait_ge(ctr.sems["act"], Aw0 + dc - 1)
                            elif evac_entry[pb]:
                                t.wait_ge(ctr.sems["act"], evac_entry[pb])
                            for ic in range(DC):
                                i = t.matmul(wo[pb][:, 0:P4_CH],
                                             wout_bf[:, ic, dc * 128:(dc + 1) * 128],
                                             oall[:, ic, t0:t0 + P4_CH],
                                             start=(ic == 0), stop=(ic == DC - 1))
                            ctr.inc(i, "pe")
                        for dc in range(DC):
                            pb = dc % 2
                            t.wait_ge(ctr.sems["dve"], Dw0 + dc + 1)
                            i = t.matmul(st4[0:1, 0:P4_CH], onec_b[:], ybf[:, pb],
                                         start=(dc == 0), stop=False)
                            i = t.matmul(st4[0:1, 256:256 + P4_CH], onec_b[:],
                                         sqb[:, pb],
                                         start=False, stop=(dc == DC - 1))
                            ctr.inc(i, "pe")

                    @block.scalar
                    def _(a):
                        for dc in range(DC):
                            pb = dc % 2
                            a.wait_ge(ctr.sems["pe"], Pw0 + dc + 1)
                            if dc == 0 and yf_guard[xb]:
                                a.wait_ge(ctr.sems["dve"], yf_guard[xb])
                            i = a.copy(yf[:, xb, dc, :], wo[pb][:, 0:P4_CH])
                            ctr.inc(i, "act")

                    @block.vector
                    def _(v):
                        for dc in range(DC):
                            pb = dc % 2
                            v.wait_ge(ctr.sems["act"], Aw0 + dc + 1)
                            if dc >= 2:
                                v.wait_ge(ctr.sems["pe"], Pw0 + 8 + dc - 1)
                            elif stat_entry[pb]:
                                v.wait_ge(ctr.sems["pe"], stat_entry[pb])
                            v.tensor_copy(ybf[:, pb], yf[:, xb, dc, :])
                            i = v.tensor_mul(sqb[:, pb], yf[:, xb, dc, :],
                                             yf[:, xb, dc, :])
                            ctr.inc(i, "dve")

                    evac_entry[0] = Aw0 + 7
                    evac_entry[1] = Aw0 + 8
                    stat_entry[0] = Pw0 + 8 + 7
                    stat_entry[1] = Pw0 + 8 + 8
                    assert ctr.c["pe"] == Pw0 + 16
                    assert ctr.c["act"] == Aw0 + 8
                    assert ctr.c["dve"] == Dw0 + 8

                    if prev_deps is not None:
                        emit_p4_tail(ch - 1, prev_deps)
                    prev_deps = (x0_ld, l4s)
                emit_p4_tail(nch4 - 1, prev_deps)
                x1_all = ctr.c["lsp"]

            with ExitStack() as _st8:
                wd_bf = _st8.enter_context(nc.sbuf_tensor("wd_bf", [128, 2, HC, 128], BF16))
                wgu = _st8.enter_context(nc.sbuf_tensor("wgu", [128, 2, 2, DC, 256], BF16))
                x1f = _st8.enter_context(nc.sbuf_tensor("x1f", [128, 2, DC, P5_CH], F32))
                x2n = _st8.enter_context(nc.sbuf_tensor("x2n", [128, DC, P5_CH], BF16))
                hbuf = _st8.enter_context(nc.sbuf_tensor("hbuf", [128, HC, P5_CH], BF16))
                silu_b = _st8.enter_context(nc.sbuf_tensor("silu_b", [128, 2, P5_CH], F32))
                yf5 = _st8.enter_context(nc.sbuf_tensor("yf5", [128, 2, DC, P5_CH], F32))
                bcsb5 = _st8.enter_context(nc.sbuf_tensor("bcsb5", [128, 512], F32))
                tb5 = _st8.enter_context(nc.sbuf_tensor("tb5", [128, P5_CH], F32))
                ybf5 = _st8.enter_context(nc.sbuf_tensor("ybf5", [128, 2, P5_CH], BF16))
                sqb5 = _st8.enter_context(nc.sbuf_tensor("sqb5", [128, 2, P5_CH], BF16))
                outb = _st8.enter_context(nc.sbuf_tensor("outb", [128, DC, P5_CH], F32))
                gu0a = _st8.enter_context(nc.psum_tensor("gu0a", [128, 512], F32))
                gu1a = _st8.enter_context(nc.psum_tensor("gu1a", [128, 512], F32))
                gu0b = _st8.enter_context(nc.psum_tensor("gu0b", [128, 512], F32))
                gu1b = _st8.enter_context(nc.psum_tensor("gu1b", [128, 512], F32))
                gu0l = [gu0a, gu0b]
                gu1l = [gu1a, gu1b]
                dn0 = _st8.enter_context(nc.psum_tensor("dn0", [128, 256], F32))
                dn1 = _st8.enter_context(nc.psum_tensor("dn1", [128, 256], F32))
                st5a = _st8.enter_context(nc.psum_tensor("st5a", [128, 512], F32))
                st5b = _st8.enter_context(nc.psum_tensor("st5b", [128, 512], F32))

                dn = [dn0, dn1]
                nch = T // P5_CH
                st5l = [st5a, st5b]
                wprep_s0 = 12 * 16
                wprep_s1 = 12 * 16
                wgu_guard = {0: 0, 1: 0}
                gu_entry = {0: 0, 1: 0}
                wgu_ld = {}
                stat_entry5 = {0: 0, 1: 0}
                dstat_entry5 = {0: 0, 1: 0}
                ev5_entry = {0: 0, 1: 0}
                wd_guard = {0: 0, 1: 0}
                x1f_guard = {0: 0, 1: 0}
                tail_guard = {0: 0, 1: 0}
                out_dma = [0]

                def emit_p5_tail(cc):
                    ccb = cc % 2
                    stX = st5l[ccb]
                    rows2 = _ln_rows(block, ctr, rows, stX, stX[:, 256:512],
                                     P5_CH, 1.0, cbias[0:1, 2:3])
                    bc2 = _bcast_mr(block, ctr, stX, ones_f, rows, P5_CH, rows2)
                    t0c = cc * P5_CH

                    @block.vector
                    def _(v):
                        ctr.wait(v, "pe", bc2)
                        ctr.wait(v, "out", out_dma[0])
                        v.tensor_copy(bcsb5[:], stX[:])
                        for dc in range(DC):
                            v.tensor_sub(yf5[:, ccb, dc, :], yf5[:, ccb, dc, :],
                                         bcsb5[:, 0:P5_CH])
                            v.tensor_mul(yf5[:, ccb, dc, :], yf5[:, ccb, dc, :],
                                         bcsb5[:, 256:256 + P5_CH])
                            i = v.tensor_add(outb[:, dc, :], yf5[:, ccb, dc, :],
                                             x1f[:, ccb, dc, :])
                        ctr.inc(i, "dve")
                    out_done = ctr.c["dve"]
                    x1f_guard[ccb] = out_done
                    tail_guard[ccb] = out_done

                    @block.sync
                    def _(sp):
                        ctr.wait(sp, "dve", out_done)
                        i = sp.dma_start(
                            out=outT[:, t0c:t0c + P5_CH].rearrange(
                                "(dc p) t -> p dc t", p=128),
                            in_=outb[:])
                        ctr.inc(i, "out", 16)
                    out_dma[0] = ctr.c["out"]

                for ch in range(nch):
                    t0 = ch * P5_CH
                    xb = ch % 2
                    st5 = st5l[xb]
                    l5s = f"l5{xb}"

                    @block.sync
                    def _(sp):
                        if ch == 0:
                            ctr.wait(sp, "lsp", x1_all)
                        ctr.wait(sp, "dve", x1f_guard[xb])
                        i = sp.dma_start(
                            out=x1f[:, xb],
                            in_=x1T[:, t0:t0 + P5_CH].rearrange(
                                "(dc p) t -> p dc t", p=128))
                        ctr.inc(i, l5s, 16)
                    x1_ld = ctr.c[l5s]

                    Dl0 = ctr.c["dve"]
                    Pl0 = ctr.c["pe"]

                    @block.vector
                    def _(v):
                        for dc in range(DC):
                            b2 = dc % 2
                            if dc == 0:
                                ctr.wait(v, l5s, x1_ld)
                            if dc >= 2:
                                v.wait_ge(ctr.sems["pe"], Pl0 + dc - 1)
                            elif stat_entry5[b2]:
                                v.wait_ge(ctr.sems["pe"], stat_entry5[b2])
                            v.tensor_copy(ybf5[:, b2], x1f[:, xb, dc, :])
                            i = v.tensor_mul(sqb5[:, b2], x1f[:, xb, dc, :],
                                             x1f[:, xb, dc, :])
                            ctr.inc(i, "dve")

                    @block.tensor
                    def _(t):
                        for dc in range(DC):
                            b2 = dc % 2
                            t.wait_ge(ctr.sems["dve"], Dl0 + dc + 1)
                            if dc == 0 and tail_guard[xb]:
                                t.wait_ge(ctr.sems["dve"], tail_guard[xb])
                            t.matmul(st5[0:1, 0:P5_CH], onec_b[:], ybf5[:, b2],
                                     start=(dc == 0), stop=False)
                            i = t.matmul(st5[0:1, 256:256 + P5_CH], onec_b[:],
                                         sqb5[:, b2],
                                         start=False, stop=(dc == DC - 1))
                            ctr.inc(i, "pe")
                    stat_entry5[0] = Pl0 + 7
                    stat_entry5[1] = Pl0 + 8

                    rows_done = _ln_rows(block, ctr, rows, st5, st5[:, 256:512],
                                         P5_CH, 1.0, cbias[0:1, 1:2])
                    bc_done = _bcast_mr(block, ctr, st5, ones_f, rows, P5_CH,
                                        rows_done)

                    @block.vector
                    def _(v):
                        ctr.wait(v, "pe", bc_done)
                        v.tensor_copy(bcsb5[:], st5[:])
                        for dc in range(DC):
                            v.tensor_sub(tb5[:], x1f[:, xb, dc, :],
                                         bcsb5[:, 0:P5_CH])
                            i = v.tensor_mul(x2n[:, dc, :], tb5[:],
                                             bcsb5[:, 256:256 + P5_CH])
                        ctr.inc(i, "dve")
                    x2n_done = ctr.c["dve"]
                    bcc_done = x2n_done

                    Pg0 = ctr.c["pe"]
                    Ag0 = ctr.c["act"]
                    Dg0 = ctr.c["dve"]

                    def g_gu(hc):
                        return Pg0 + hc + 1

                    def g_silu(hc):
                        return Ag0 + hc + 1

                    def g_h(hc):
                        return Dg0 + hc + 1

                    for hcp in range(HC // 2):
                        wpar = hcp % 2
                        lws5 = f"lw{wpar}"

                        @block.sync
                        def _(sp):
                            if ch == 0 and hcp < 2:
                                ctr.wait(sp, "sd0", wprep_s0)
                                ctr.wait(sp, "sd1", wprep_s1)
                            ctr.wait(sp, "pe", wgu_guard[wpar])
                            i = sp.dma_start(out=wgu[:, wpar, 0],
                                             in_=wgb[:, hcp])
                            ctr.inc(i, lws5, 16)
                            i = sp.dma_start(out=wgu[:, wpar, 1],
                                             in_=wub[:, hcp])
                            ctr.inc(i, lws5, 16)
                        wgu_ld[(ch, hcp)] = ctr.c[lws5]

                        @block.tensor
                        def _(t):
                            for hc in (2 * hcp, 2 * hcp + 1):
                                gb = hc % 2
                                cc = (hc % 2) * 128
                                if hc == 2 * hcp:
                                    ctr.wait(t, lws5, wgu_ld[(ch, hcp)])
                                if hc == 0:
                                    ctr.wait(t, "dve", x2n_done)
                                if hc >= 2:
                                    ctr.wait(t, "dve", g_h(hc - 2))
                                elif gu_entry[gb]:
                                    ctr.wait(t, "dve", gu_entry[gb])
                                for ic in range(DC):
                                    t.matmul(gu0l[gb][:, 0:P5_CH],
                                             wgu[:, wpar, 0, ic, cc:cc + 128],
                                             x2n[:, ic, :],
                                             start=(ic == 0), stop=(ic == DC - 1))
                                for ic in range(DC):
                                    i = t.matmul(gu1l[gb][:, 0:P5_CH],
                                                 wgu[:, wpar, 1, ic, cc:cc + 128],
                                                 x2n[:, ic, :],
                                                 start=(ic == 0), stop=(ic == DC - 1))
                                ctr.inc(i, "pe")
                        wgu_guard[wpar] = ctr.c["pe"]

                    @block.scalar
                    def _(a):
                        for hc in range(HC):
                            gb = hc % 2
                            a.wait_ge(ctr.sems["pe"], g_gu(hc))
                            if hc >= 2:
                                a.wait_ge(ctr.sems["dve"], g_h(hc - 2))
                            i = a.activation(out=silu_b[:, gb],
                                             in_=gu0l[gb][:, 0:P5_CH],
                                             func=AF.Silu, scale=DEQ)
                            ctr.inc(i, "act")

                    @block.vector
                    def _(v):
                        for hc in range(HC):
                            gb = hc % 2
                            v.wait_ge(ctr.sems["act"], g_silu(hc))
                            i = v.tensor_mul(hbuf[:, hc, :], silu_b[:, gb],
                                             gu1l[gb][:, 0:P5_CH])
                            ctr.inc(i, "dve")

                    gu_entry[0] = g_h(HC - 2)
                    gu_entry[1] = g_h(HC - 1)
                    assert ctr.c["pe"] == Pg0 + HC
                    assert ctr.c["act"] == Ag0 + HC
                    assert ctr.c["dve"] == Dg0 + HC
                    h_all = ctr.c["dve"]

                    if ch > 0:
                        emit_p5_tail(ch - 1)

                    Pd0 = ctr.c["pe"]
                    Ad0 = ctr.c["act"]
                    Dd0 = ctr.c["dve"]
                    for dc in range(DC):
                        pb = dc % 2
                        lds = f"ld{pb}"

                        @block.sync
                        def _(sp):
                            if ch == 0 and dc < 2:
                                ctr.wait(sp, "sd0", wprep_s0)
                                ctr.wait(sp, "sd1", wprep_s1)
                            ctr.wait(sp, "pe", wd_guard[pb])
                            i = sp.dma_start(out=wd_bf[:, pb], in_=wdTb[:, dc])
                            ctr.inc(i, lds, 16)
                        wd_ld = ctr.c[lds]

                        @block.tensor
                        def _(t):
                            ctr.wait(t, lds, wd_ld)
                            if dc == 0:
                                ctr.wait(t, "dve", h_all)
                                ctr.wait(t, "dve", bcc_done)
                            if dc >= 2:
                                t.wait_ge(ctr.sems["act"], Ad0 + dc - 1)
                            elif ev5_entry[pb]:
                                t.wait_ge(ctr.sems["act"], ev5_entry[pb])
                            for hc in range(HC):
                                i = t.matmul(dn[pb][:, 0:P5_CH],
                                             wd_bf[:, pb, hc, :],
                                             hbuf[:, hc, :],
                                             start=(hc == 0), stop=(hc == HC - 1))
                            ctr.inc(i, "pe")
                            t.wait_ge(ctr.sems["dve"], Dd0 + dc + 1)
                            t.matmul(st5[0:1, 0:P5_CH], onec_b[:], ybf5[:, pb],
                                     start=(dc == 0), stop=False)
                            i = t.matmul(st5[0:1, 256:256 + P5_CH], onec_b[:],
                                         sqb5[:, pb],
                                         start=False, stop=(dc == DC - 1))
                            ctr.inc(i, "pe")
                        wd_guard[pb] = Pd0 + 2 * dc + 1

                    @block.scalar
                    def _(a):
                        for dc in range(DC):
                            pb = dc % 2
                            a.wait_ge(ctr.sems["pe"], Pd0 + 2 * dc + 1)
                            if dc == 0 and tail_guard[xb]:
                                a.wait_ge(ctr.sems["dve"], tail_guard[xb])
                            i = a.copy(yf5[:, xb, dc, :], dn[pb][:, 0:P5_CH])
                            ctr.inc(i, "act")

                    @block.vector
                    def _(v):
                        for dc in range(DC):
                            pb = dc % 2
                            v.wait_ge(ctr.sems["act"], Ad0 + dc + 1)
                            if dc >= 2:
                                v.wait_ge(ctr.sems["pe"], Pd0 + 2 * (dc - 2) + 2)
                            elif dstat_entry5[pb]:
                                v.wait_ge(ctr.sems["pe"], dstat_entry5[pb])
                            v.tensor_copy(ybf5[:, pb], yf5[:, xb, dc, :])
                            i = v.tensor_mul(sqb5[:, pb], yf5[:, xb, dc, :],
                                             yf5[:, xb, dc, :])
                            ctr.inc(i, "dve")

                    ev5_entry[0] = Ad0 + 7
                    ev5_entry[1] = Ad0 + 8
                    dstat_entry5[0] = Pd0 + 2 * 6 + 2
                    dstat_entry5[1] = Pd0 + 2 * 7 + 2
                    stat_entry5[0] = Pd0 + 2 * 6 + 2
                    stat_entry5[1] = Pd0 + 2 * 7 + 2
                    assert ctr.c["pe"] == Pd0 + 16
                    assert ctr.c["act"] == Ad0 + 8
                    assert ctr.c["dve"] == Dd0 + 8
                emit_p5_tail(nch - 1)

                @block.sync
                def _(sp):
                    ctr.wait(sp, "out")

    ctr.close()
    return nc


def _host_prep(L0_attn_tokens, L0_attn_norm_w, L0_attn_Wq, L0_attn_Wkv,
               L0_attn_Wout, L0_attn_Wg, L0_ffn_w_gate, L0_ffn_w_up,
               L0_ffn_w_down):
    bf = ml_dtypes.bfloat16
    shared = {
        "wq": np.ascontiguousarray(L0_attn_Wq, dtype=np.float32),
        "wkv": np.ascontiguousarray(L0_attn_Wkv, dtype=np.float32),
        "wgh": np.ascontiguousarray(L0_attn_Wg, dtype=np.float32),
        "wout": np.ascontiguousarray(L0_attn_Wout, dtype=np.float32),
        "normw": np.ascontiguousarray(
            np.asarray(L0_attn_norm_w, dtype=np.float32).reshape(DC, 128).T),
        "wgT": np.ascontiguousarray(np.asarray(L0_ffn_w_gate, dtype=np.int32).T),
        "wuT": np.ascontiguousarray(np.asarray(L0_ffn_w_up, dtype=np.int32).T),
        "wdT": np.ascontiguousarray(np.asarray(L0_ffn_w_down, dtype=np.int32).T),
    }
    k_i = np.arange(128)[:, None]
    q_i = np.arange(QW)[None, :]
    t1 = (q_i >= k_i).astype(bf)
    t2 = (q_i >= 128 + k_i).astype(bf)
    ones = np.ones((128, QW), dtype=bf)
    zeros = np.zeros((128, QW), dtype=bf)
    shared["ident16"] = np.eye(H, dtype=bf)
    masks_by_par = {
        0: np.ascontiguousarray(np.stack([t1, t2, zeros, zeros])),
        1: np.ascontiguousarray(np.stack([ones, ones, t1, t2])),
    }
    toks = np.asarray(L0_attn_tokens, dtype=np.float32)
    own_cols_by_par = {
        par: np.concatenate([np.arange((2 * j + par) * QW, (2 * j + par + 1) * QW)
                             for j in range(NSLOT)])
        for par in range(2)
    }
    in_maps = []
    for c in range(8):
        seq, par = c // 2, c % 2
        xt = np.ascontiguousarray(toks[seq].T)
        m = dict(shared)
        m["xT_kv"] = xt
        m["xT_own"] = np.ascontiguousarray(xt[:, own_cols_by_par[par]])
        m["masks"] = masks_by_par[par]
        in_maps.append(m)
    return in_maps, own_cols_by_par


def kernel(**inputs):
    in_maps, own_cols_by_par = _host_prep(**inputs)
    nc = build_program()
    res = run_bass_kernel_spmd(nc, in_maps, core_ids=list(range(8)))
    out = np.empty((4, S, D), dtype=np.float32)
    for c in range(8):
        seq, par = c // 2, c % 2
        out[seq, own_cols_by_par[par], :] = res.results[c]["outT"].T
    return out
